# version 16
# baseline (speedup 1.0000x reference)
import numpy as np
from contextlib import ExitStack

import concourse.bass as bass
import concourse.mybir as mybir
from concourse.bass_utils import run_bass_kernel_spmd

F32 = mybir.dt.float32
F16 = mybir.dt.float16
I32 = mybir.dt.int32
ALU = mybir.AluOpType
ACT = mybir.ActivationFunctionType
AX = mybir.AxisListType

H = 2048
NKH = H // 128
II = 8192
NKI = II // 128
SUB = 512
WARM = 512
EPS = 1e-5
NW_ELEM = float(II * H)


class Sem:
    __slots__ = ("h", "n", "name")

    def __init__(self, h, name):
        self.h = h
        self.n = 0
        self.name = name


class Ring:
    def __init__(self, n):
        self.n = n
        self.tok = [None] * n
        self.i = 0

    def acquire(self):
        s = self.i % self.n
        self.i += 1
        return s, self.tok[s]

    def release(self, s, token):
        self.tok[s] = token


def build_nc(T_OWN=2048, debug_stop=None):
    NS_T = T_OWN // SUB
    TCOL = T_OWN + WARM

    nc = bass.Bass("TRN2", target_bir_lowering=False,
                   detect_race_conditions=False)

    xin = nc.dram_tensor("xin", [H, TCOL], F32, kind="ExternalInput").ap()
    ident_in = nc.dram_tensor("ident", [128, 128], F32, kind="ExternalInput").ap()
    wnames = ["wf", "wi", "wv", "wq_", "wro", "wup", "wgt", "wdn"]
    wap = {n: nc.dram_tensor(n, [4, NKH, 128, 512], F32, kind="ExternalInput").ap()
           for n in wnames}
    fg = nc.dram_tensor("fg", [16, NKH, 128, 512], F32, kind="ExternalInput").ap()
    fu = nc.dram_tensor("fu", [16, NKH, 128, 512], F32, kind="ExternalInput").ap()
    fd = nc.dram_tensor("fd", [II, H], F32, kind="ExternalInput").ap()
    scratch_kind = "ExternalOutput" if debug_stop else "Internal"
    xrdbg = (nc.dram_tensor("xrdbg", [H, T_OWN], F32, kind="ExternalOutput").ap()
             if debug_stop else None)
    rodbg = (nc.dram_tensor("rodbg", [H, T_OWN], F16, kind="ExternalOutput").ap()
             if debug_stop else None)
    stdbg = (nc.dram_tensor("stdbg", [H, T_OWN], F16, kind="ExternalOutput").ap()
             if debug_stop else None)
    scdbg = (nc.dram_tensor("scdbg", [128, 32], F32, kind="ExternalOutput").ap()
             if debug_stop else None)
    fdbg = (nc.dram_tensor("fdbg", [H, T_OWN], F16, kind="ExternalOutput").ap()
            if debug_stop else None)
    gdbg = (nc.dram_tensor("gdbg", [H, T_OWN], F16, kind="ExternalOutput").ap()
            if debug_stop else None)
    pdbg = (nc.dram_tensor("pdbg", [H, T_OWN], F16, kind="ExternalOutput").ap()
            if debug_stop else None)
    out_d = nc.dram_tensor("out", [H, T_OWN], F32, kind="ExternalOutput").ap()
    x1s = nc.dram_tensor("x1s", [H, T_OWN], F32, kind=scratch_kind).ap()
    xqs = nc.dram_tensor("xqs", [H, T_OWN], F16, kind=scratch_kind).ap()
    hs = nc.dram_tensor("hs", [II, T_OWN], F16, kind=scratch_kind).ap()
    hqs = nc.dram_tensor("hqs", [II, T_OWN], F16, kind=scratch_kind).ap()
    wdqs = nc.dram_tensor("wdqs", [II, H], F16, kind=scratch_kind).ap()
    wf16s = nc.dram_tensor("wf16s", [8, 4, 128, 8192], F16, kind="Internal").ap()
    y1s = nc.dram_tensor("y1s", [H // 2, T_OWN], F32, kind=scratch_kind).ap()

    ctx = ExitStack()
    with ctx:
        AR = 53192
        arena = ctx.enter_context(nc.sbuf_tensor("arena", [128, AR], F32)).ap()
        off = [0]

        def alloc(n):
            o = off[0]
            off[0] += n
            assert off[0] <= AR, f"arena overflow {off[0]}"
            return arena[:, o:o + n]

        A0 = alloc(8192)
        A1 = alloc(8192)
        A2 = alloc(8192)
        A3 = alloc(8192)
        A4 = alloc(8192)
        A5 = alloc(2048)
        A6 = alloc(max(T_OWN, 2048))
        SM_N = AR - off[0]
        smo = [0]
        SMbase = alloc(SM_N)

        def sm(n):
            o = smo[0]
            smo[0] += n
            assert smo[0] <= SM_N, f"smalls overflow {smo[0]} > {SM_N}"
            return SMbase[:, o:o + n]

        ident = sm(128)
        ones16 = sm(16).bitcast(F16)
        onesrow16 = sm(64).bitcast(F16)
        state_col = sm(32)
        rowA = sm(512)
        rowB = sm(512)
        rowC = sm(512)
        r16 = sm(512).bitcast(F16)
        Mx = sm(512)
        st_base = smo[0]
        st16a = [sm(256).bitcast(F16) for _ in range(2)]
        st16b = [sm(256).bitcast(F16) for _ in range(2)]
        stwide = SMbase[:, st_base:st_base + 1024]
        stwideB16 = SMbase[:, st_base + 512:st_base + 1024].bitcast(F16)
        xres = [sm(512) for _ in range(3)]
        tmp_base = smo[0]
        tmp = [sm(512) for _ in range(3)]
        tmpwide = SMbase[:, tmp_base:tmp_base + 1024]
        clb_base = smo[0]
        clb = [sm(512) for _ in range(2)]
        clbwide = SMbase[:, clb_base:clb_base + 1024]
        wsc = sm(24)
        R2sbA = clb[0]

        psb = [ctx.enter_context(nc.psum_tensor(f"pb{i}", [128, 512], F32))
               for i in range(8)]

        Q = {"sync": [], "tensor": [], "vector": [], "scalar": [], "gpsimd": []}
        sems = {}

        def sem(name):
            if name not in sems:
                sems[name] = Sem(ctx.enter_context(nc.semaphore(name)), name)
            return sems[name]

        def emit(eng, fn):
            Q[eng].append(fn)

        def wof(wl):
            return [(s, v) for (s, v) in wl if s is not None]

        def barrier(eng, wl):
            wl = wof(wl)
            if not wl:
                return

            def fn(E):
                for s, v in wl:
                    E.wait_ge(s.h, v)
            emit(eng, fn)

        def dma(eng, dst, src, s, waits=()):
            wl = wof(waits)

            def fn(E):
                for ss, v in wl:
                    E.wait_ge(ss.h, v)
                E.dma_start(out=dst, in_=src).then_inc(s.h, 16)
            emit(eng, fn)
            s.n += 16
            return (s, s.n)

        def op(eng, build, waits=(), inc=None):
            wl = wof(waits)
            if inc is not None:
                inc.n += 1

            def fn(E):
                for s, v in wl:
                    E.wait_ge(s.h, v)
                ins = build(E)
                if inc is not None:
                    ins.then_inc(inc.h, 1)
            emit(eng, fn)
            return (inc, inc.n) if inc is not None else None

        rW = [sem(f"rW{i}") for i in range(2)]
        rP = [sem(f"rP{i}") for i in range(2)]
        rB = [sem(f"rB{i}") for i in range(10)]
        rX = [sem(f"rX{i}") for i in range(2)]
        rZ = [sem(f"rZ{i}") for i in range(3)]
        rH = [sem(f"rH{i}") for i in range(8)]
        rD = [sem(f"rD{i}") for i in range(2)]
        rE = [sem(f"rE{i}") for i in range(8)]
        rF = [sem(f"rF{i}") for i in range(4)]
        sID = sem("sID")
        sXQ = sem("sXQ")
        stXQ4 = [sem(f"stXQ4{i}") for i in range(4)]
        stWF = sem("stWF")
        rWF = [sem(f"rWF{i}") for i in range(2)]
        stX1 = sem("stX1")
        stXQ = sem("stXQ")
        stH = sem("stH")
        stHQ = sem("stHQ")
        stWDQ = sem("stWDQ")
        stY1 = sem("stY1")
        stOUT = sem("stOUT")
        cGP = sem("cGP")
        cDVE = sem("cDVE")
        cSCA = sem("cSCA")
        cPE = sem("cPE")
        ESEM = {"vector": cDVE, "scalar": cSCA, "gpsimd": cGP}

        wring = Ring(2)
        bring = Ring(10)
        xring = Ring(2)
        zring = Ring(3)
        dring = Ring(2)
        fring = Ring(4)

        def cast_slice(eng, dst, src, waits=(), inc=None):
            if eng == "scalar":
                return op(eng, lambda E: E.activation(out=dst, in_=src,
                                                      func=ACT.Copy), waits, inc)
            return op(eng, lambda E: E.tensor_copy(out=dst, in_=src), waits, inc)

        A0_16 = A0.bitcast(F16)
        A1_16 = A1.bitcast(F16)
        A2_16 = A2.bitcast(F16)
        A3_16 = A3.bitcast(F16)
        A4_16 = A4.bitcast(F16)

        def x16v(k, c0, n):
            return A0_16[:, k * 1024 + c0:k * 1024 + c0 + n]

        def bAv(k, c0, n):
            return A1_16[:, k * 1024 + c0:k * 1024 + c0 + n]

        def bBv(k, c0, n):
            return A2_16[:, k * 1024 + c0:k * 1024 + c0 + n]

        def bCv(k, c0, n):
            return A3_16[:, k * 1024 + c0:k * 1024 + c0 + n]

        def wstage(slot):
            return A5[:, slot * 1024:(slot + 1) * 1024]

        prescan_units = []
        acc_init = [False, False, False]
        acc_done_tok = [None, None, None]

        def queue_prescan_qmajor(dram, nq, ai):
            for q in range(nq):
                for kg in range(NKH // 2):
                    src = dram[q, kg * 2:(kg + 1) * 2, :, :].rearrange(
                        "k p c -> p k c")
                    prescan_units.append((src, ai))

        def queue_prescan_down(ai):
            for k in range(NKI):
                for hh in range(2):
                    src = fd[k * 128:(k + 1) * 128, hh * 1024:(hh + 1) * 1024]
                    prescan_units.append((src, ai))

        def pump_prescan(n):
            for _ in range(n):
                if not prescan_units:
                    return
                src, ai = prescan_units.pop(0)
                slot, tok = wring.acquire()
                dst = wstage(slot)
                if len(src.shape) == 3:
                    dst = dst.rearrange("p (k c) -> p k c", k=2)
                t = dma("sync" if slot == 0 else "scalar", dst, src, rP[slot],
                        waits=[tok] if tok else [])
                acc = wsc[:, ai:ai + 1]
                part = wsc[:, 3 + ai:4 + ai]
                stg = wstage(slot)
                ptok = acc_done_tok[ai]
                rtok = op("scalar", lambda E, s_=stg, p=part: E.activation(
                    out=s_, in_=s_, func=ACT.Abs, accum_out=p),
                    waits=[t] + ([ptok] if ptok else []), inc=cSCA)
                if not acc_init[ai]:
                    acc_init[ai] = True
                    rel = op("gpsimd", lambda E, a=acc, p=part: E.tensor_copy(
                        out=a, in_=p), waits=[rtok], inc=cGP)
                else:
                    rel = op("gpsimd", lambda E, a=acc, p=part: E.tensor_tensor(
                        out=a, in0=a, in1=p, op=ALU.add), waits=[rtok],
                        inc=cGP)
                wring.release(slot, (cSCA, rtok[1]))
                acc_done_tok[ai] = rel

        idtok = dma("sync", ident, ident_in, sID)
        op("vector", lambda E: E.memset(ones16[:, 0:1], 1.0))
        op("vector", lambda E: E.memset(onesrow16[0:1, 0:128], 1.0))
        setup_tok = op("vector", lambda E: E.memset(state_col, 0.0), inc=cDVE)
        barrier("tensor", [idtok, setup_tok])
        barrier("vector", [setup_tok])

        queue_prescan_qmajor(fg, 16, 0)
        queue_prescan_qmajor(fu, 16, 1)
        queue_prescan_down(2)

        tct_state = {"prev_copy": None}

        def transpose_col_to_row(col_ap, bank, dst_row, waits=()):
            wl = list(waits)
            if tct_state["prev_copy"] is not None:
                wl.append(tct_state["prev_copy"])
            g = op("tensor", lambda E: E.transpose(
                psb[bank][0:1, 0:128], col_ap, ident), waits=wl, inc=cPE)
            v = op("vector", lambda E: E.tensor_copy(
                out=dst_row, in_=psb[bank][0:1, 0:128]), waits=[g], inc=cDVE)
            tct_state["prev_copy"] = v
            return v

        pair_tok = [None]

        def hilo_bcast(row_f32, n, bank, waits=(), bank_guard=None):
            hi = r16[0:1, 0:n]
            lo = r16[0:1, 512:512 + n]
            wl = list(waits)
            if pair_tok[0] is not None:
                wl.append(pair_tok[0])
            c1 = op("vector", lambda E: E.tensor_copy(out=hi, in_=row_f32),
                    waits=wl, inc=cDVE)
            c2 = op("vector", lambda E: E.tensor_tensor(
                out=lo, in0=row_f32, in1=hi, op=ALU.subtract), waits=[c1],
                inc=cDVE)
            mw = [c2]
            if bank_guard is not None:
                mw.append(bank_guard)

            def mm(E):
                E.matmul(psb[bank][:, 0:n], onesrow16[0:1, 0:128], hi,
                         start=True, stop=False)
                return E.matmul(psb[bank][:, 0:n], onesrow16[0:1, 0:128], lo,
                                start=False, stop=True)
            g = op("tensor", mm, waits=mw, inc=cPE)
            pair_tok[0] = g
            return psb[bank][:, 0:n], g

        tile_guard = {"x16": None, "bufA": None, "bufB": None, "bufC": None,
                      "st16": None}

        def emit_x_load(tile_t0, width):
            last = None
            for k in range(NKH):
                for s in range(width // SUB):
                    slot, tok = xring.acquire()
                    stage = tmp[slot]
                    wl = [tok] if tok else []
                    t = dma("sync", stage,
                            xin[k * 128:(k + 1) * 128,
                                tile_t0 + s * SUB:tile_t0 + (s + 1) * SUB],
                            rX[slot], waits=wl)
                    wts = [t]
                    if k == 0 and s == 0 and tile_guard["x16"] is not None:
                        wts += tile_guard["x16"]
                    r = cast_slice("gpsimd", x16v(k, s * SUB, SUB), stage,
                                   waits=wts, inc=cGP)
                    xring.release(slot, r)
                    last = r
            return last

        cast_rr = ["vector", "scalar", "vector", "gpsimd"]
        rrci = [0]

        def stream_quarter(dram_q, dest_slice_fn, extra_guard=None,
                           cache_dst=None, dest_full=None):
            eng = cast_rr[rrci[0] % len(cast_rr)]
            rrci[0] += 1
            done = None
            for kg in range(8):
                slot, tok = wring.acquire()
                wl = [tok] if tok else []
                t = dma("sync" if kg % 2 == 0 else "scalar",
                        wstage(slot).rearrange("p (k c) -> p k c", k=2),
                        dram_q[kg * 2:(kg + 1) * 2, :, :].rearrange(
                            "k p c -> p k c"),
                        rW[slot], waits=wl)
                last = None
                for j in range(2):
                    k = kg * 2 + j
                    wts = [t] if j == 0 else []
                    if kg == 0 and j == 0 and extra_guard:
                        wts = wts + list(extra_guard)
                    src = A5[:, slot * 1024 + j * 512:slot * 1024 + (j + 1) * 512]
                    last = cast_slice(eng, dest_slice_fn(k), src, waits=wts,
                                      inc=ESEM[eng])
                wring.release(slot, last)
                done = last
            if cache_dst is not None:
                ck = dma("gpsimd", cache_dst, dest_full, stWF, waits=[done])
                return done, ck
            return done, None

        def load_quarter_f16(cache_src, dest_full, extra_guard=None, qi=0):
            wl = [(stWF, stWF.n)]
            if extra_guard:
                wl = wl + list(extra_guard)
            return dma("sync" if qi % 2 == 0 else "scalar", dest_full,
                       cache_src, rWF[qi % 2], waits=wl)

        def load_quarter_f16_split(cache_src, dest_full, extra_guard=None,
                                   qi=0):
            wl = [(stWF, stWF.n)]
            if extra_guard:
                wl = wl + list(extra_guard)
            hw = 8 * 512
            qname = "sync" if qi % 2 == 0 else "scalar"
            lo = dma(qname, dest_full[:, 0:hw], cache_src[:, 0:hw],
                     rWF[qi % 2], waits=wl)
            hi = dma(qname, dest_full[:, hw:2 * hw], cache_src[:, hw:2 * hw],
                     rWF[qi % 2], waits=[])
            return lo, hi

        proj_pass = {}
        prev_ck = {0: None, 1: None}
        wq_guard_g = {0: None, 1: None}

        def emit_proj(wname, width, evac, moving_v, prewait=None):
            ns = width // SUB
            dram = wap[wname]
            pidx = wnames.index(wname)
            first = wname not in proj_pass
            proj_pass[wname] = True
            free_tok = {0: None, 4: None}
            wq_guard = [wq_guard_g[0], wq_guard_g[1]]
            gi = [0]
            last_group = [None]
            for q in range(4):
                buf = q % 2

                def wsl(k, b=buf):
                    return A4_16[:, (b * 16 + k) * 512:(b * 16 + k + 1) * 512]

                def wfull(b=buf):
                    return A4_16[:, (b * 16) * 512:(b * 16 + 16) * 512]
                eg = list(wq_guard[buf] or [])
                if prev_ck[buf] is not None:
                    eg.append(prev_ck[buf])
                    prev_ck[buf] = None
                if first:
                    cd, ck = stream_quarter(
                        dram[q], wsl, extra_guard=eg,
                        cache_dst=wf16s[pidx, q], dest_full=wfull())
                else:
                    cd = load_quarter_f16(wf16s[pidx, q], wfull(),
                                          extra_guard=eg, qi=q)
                    ck = None
                pump_prescan(6)
                for s in range(ns):
                    g = gi[0]
                    gi[0] += 1
                    bank0 = (g % 2) * 4
                    wts = []
                    if s == 0:
                        wts.append(cd)
                        if prewait is not None:
                            wts.append(prewait)
                            prewait = None
                    if free_tok[bank0] is not None:
                        wts.append(free_tok[bank0])

                    def mmg(E, s_=s, b0=bank0, wsl_=wsl):
                        mm = None
                        for k in range(NKH):
                            for m in range(4):
                                mm = E.matmul(psb[b0 + m][:],
                                              wsl_(k)[:, m * 128:(m + 1) * 128],
                                              moving_v(k, s_ * SUB, SUB),
                                              start=(k == 0), stop=(k == NKH - 1))
                        return mm
                    gt = op("tensor", mmg, waits=wts, inc=cPE)
                    free_tok[bank0] = evac(q, s, bank0, gt)
                    last_group[0] = gt
                wq_guard[buf] = [last_group[0]] + ([ck] if ck else [])
                wq_guard_g[buf] = wq_guard[buf]
                if ck is not None:
                    prev_ck[buf] = ck
            return last_group[0]

        def emit_ln_rows(sum_bank, sq_bank, st_done):
            mean = rowA[0:1, :]
            var = rowB[0:1, :]
            istd = rowC[0:1, :]
            t1 = op("vector", lambda E: E.tensor_scalar(
                out=mean, in0=psb[sum_bank][0:1, :], scalar1=1.0 / H,
                scalar2=None, op0=ALU.mult), waits=[st_done], inc=cDVE)
            bank_rel = op("vector", lambda E: E.tensor_scalar(
                out=var, in0=psb[sq_bank][0:1, :], scalar1=1.0 / H,
                scalar2=None, op0=ALU.mult), inc=cDVE)
            t2 = op("vector", lambda E: E.tensor_tensor(
                out=istd, in0=mean, in1=mean, op=ALU.mult),
                waits=[t1, bank_rel], inc=cDVE)
            t3 = op("vector", lambda E: E.tensor_tensor(
                out=var, in0=var, in1=istd, op=ALU.subtract), waits=[t2],
                inc=cDVE)
            t4 = op("vector", lambda E: E.tensor_scalar(
                out=var, in0=var, scalar1=EPS, scalar2=None, op0=ALU.add),
                waits=[t3], inc=cDVE)
            rec = op("vector", lambda E: E.reciprocal(out=var, in_=var),
                     waits=[t4], inc=cDVE)
            sq = op("scalar", lambda E: E.activation(
                out=istd, in_=var, func=ACT.Sqrt), waits=[rec], inc=cSCA)
            bb = op("vector", lambda E: E.tensor_tensor(
                out=mean, in0=mean, in1=istd, op=ALU.mult), waits=[sq],
                inc=cDVE)
            return sq, bb, bank_rel

        def emit_amax_rows(M_ap, first_wait):
            last = None
            for g4 in range(4):
                bk = 4 + (g4 % 2)
                wl = [first_wait] if (g4 == 0 and first_wait) else []
                if tct_state["prev_copy"] is not None:
                    wl.append(tct_state["prev_copy"])
                tg = op("tensor", lambda E, g_=g4, b=bk, M=M_ap: E.transpose(
                    psb[b][:, 0:128], M[:, g_ * 128:(g_ + 1) * 128], ident),
                    waits=wl, inc=cPE)
                rd = op("vector", lambda E, b=bk: E.reduce_max(
                    out=wsc[:, 7:8], in_=psb[b][:, 0:128], axis=AX.X),
                    waits=[tg] + ([last] if last else []), inc=cDVE)
                last = transpose_col_to_row(
                    wsc[:, 7:8], 4 + ((g4 + 1) % 2),
                    rowB[0:1, g4 * 128:(g4 + 1) * 128], waits=[rd])
            return last

        bank_tok = {"psA": None, "psB": None, "psS": None, "psR": None}

        def emit_attn_tile(tile_t0, width, warm_only):
            ns = width // SUB
            own0 = tile_t0 - WARM
            xl = emit_x_load(tile_t0, width)
            pump_prescan(4)

            def ev_f(q, s, b0, gt):
                wts = [gt]
                if q == 0 and s == 0 and tile_guard["bufA"] is not None:
                    wts += tile_guard["bufA"]
                    tile_guard["bufA"] = None

                def fn(E):
                    a = None
                    for m in range(4):
                        a = E.activation(out=bAv(q * 4 + m, s * SUB, SUB),
                                         in_=psb[b0 + m][:], func=ACT.Tanh)
                    return a
                return op("scalar", fn, waits=wts, inc=cSCA)
            emit_proj("wf", width, ev_f, x16v, prewait=xl)
            f_done = (cSCA, cSCA.n)
            pump_prescan(4)

            def ev_i(q, s, b0, gt):
                wts = [gt]
                if q == 0 and s == 0 and tile_guard["bufB"] is not None:
                    wts += tile_guard["bufB"]
                    tile_guard["bufB"] = None

                def fn(E):
                    a = None
                    for m in range(4):
                        a = E.activation(out=bBv(q * 4 + m, s * SUB, SUB),
                                         in_=psb[b0 + m][:], func=ACT.Sigmoid)
                    return a
                return op("scalar", fn, waits=wts, inc=cSCA)
            emit_proj("wi", width, ev_i, x16v)
            pump_prescan(4)

            vtok = {}

            def ev_v(q, s, b0, gt):
                for m in range(4):
                    mi = q * 4 + m
                    ts = tmp[2]
                    wts = [gt] if m == 0 else []
                    pt = vtok.get(2)
                    if pt is not None:
                        wts.append(pt)
                    sv = op("scalar", lambda E, b=b0 + m, t=ts: E.activation(
                        out=t, in_=psb[b][:], func=ACT.Silu), waits=wts,
                        inc=cSCA)
                    dv = op("vector", lambda E, mi_=mi, t=ts, s_=s:
                            E.tensor_tensor(out=bBv(mi_, s_ * SUB, SUB), in0=t,
                                            in1=bBv(mi_, s_ * SUB, SUB),
                                            op=ALU.mult), waits=[sv], inc=cDVE)
                    vtok[2] = dv
                return (cSCA, cSCA.n)
            emit_proj("wv", width, ev_v, x16v)
            pump_prescan(2)

            barrier("vector", [f_done] + (tile_guard["bufC"] or []))
            tile_guard["bufC"] = None
            for k in range(NKH):
                sc_tok = op("vector", lambda E, k_=k: E.tensor_tensor_scan(
                    out=bCv(k_, 0, width), data0=bAv(k_, 0, width),
                    data1=bBv(k_, 0, width),
                    initial=state_col[:, 2 * k_ + 1:2 * k_ + 2],
                    op0=ALU.mult, op1=ALU.add), inc=cDVE)
                op("vector", lambda E, k_=k: E.tensor_copy(
                    out=state_col[:, 2 * k_:2 * k_ + 2],
                    in_=bCv(k_, width - 2, 2)), waits=[sc_tok], inc=cDVE)
            scan_done = (cDVE, cDVE.n)
            if debug_stop == "A" and not warm_only:
                for k in range(NKH):
                    dma("gpsimd", stdbg[k * 128:(k + 1) * 128,
                                        own0:own0 + width],
                        bCv(k, 0, width), sem("stST"), waits=[scan_done])
                    dma("gpsimd", fdbg[k * 128:(k + 1) * 128,
                                       own0:own0 + width],
                        bAv(k, 0, width), sem("stST"))
                    dma("gpsimd", gdbg[k * 128:(k + 1) * 128,
                                       own0:own0 + width],
                        bBv(k, 0, width), sem("stST"))
                barrier("scalar", [(sems["stST"], sems["stST"].n)])
            if warm_only:
                tile_guard["x16"] = [(cPE, cPE.n)]
                tile_guard["bufA"] = [scan_done]
                tile_guard["bufB"] = [scan_done]
                return

            barrier("scalar", [scan_done])
            qtok = {}

            def ev_q(q, s, b0, gt):
                for m in range(4):
                    mi = q * 4 + m
                    ts = tmp[2]
                    wts = [gt] if m == 0 else []
                    pt = qtok.get(2)
                    if pt is not None:
                        wts.append(pt)
                    dv = op("vector", lambda E, b=b0 + m, t=ts, mi_=mi, s_=s:
                            E.tensor_tensor(out=t, in0=psb[b][:],
                                            in1=bCv(mi_, s_ * SUB, SUB),
                                            op=ALU.mult), waits=wts, inc=cDVE)
                    sc = op("scalar", lambda E, t=ts, mi_=mi, s_=s: E.activation(
                        out=bAv(mi_, s_ * SUB, SUB), in_=t, func=ACT.Silu),
                        waits=[dv], inc=cSCA)
                    qtok[2] = sc
                return (cDVE, cDVE.n)
            emit_proj("wq_", width, ev_q, x16v)
            pump_prescan(4)

            def ev_g(q, s, b0, gt):
                def fn(E):
                    a = None
                    for m in range(4):
                        a = E.activation(out=bBv(q * 4 + m, s * SUB, SUB),
                                         in_=psb[b0 + m][:], func=ACT.Silu)
                    return a
                return op("scalar", fn, waits=[gt], inc=cSCA)
            emit_proj("wgt", width, ev_g, x16v)
            silg_done = (cSCA, cSCA.n)
            pump_prescan(4)

            barrier("vector", [silg_done])

            def ev_u(q, s, b0, gt):
                last = None
                for m in range(4):
                    mi = q * 4 + m
                    last = op("vector", lambda E, b=b0 + m, mi_=mi, s_=s:
                              E.tensor_tensor(out=bCv(mi_, s_ * SUB, SUB),
                                              in0=psb[b][:],
                                              in1=bBv(mi_, s_ * SUB, SUB),
                                              op=ALU.mult),
                              waits=[gt] if m == 0 else [], inc=cDVE)
                return last
            emit_proj("wup", width, ev_u, x16v)
            pump_prescan(2)

            if debug_stop == "A":
                pk = (cDVE, cDVE.n)
                sk = (cSCA, cSCA.n)
                for k in range(NKH):
                    dma("gpsimd", rodbg[k * 128:(k + 1) * 128,
                                        own0:own0 + width],
                        bAv(k, 0, width), sem("stRO"), waits=[sk, pk])
                    dma("gpsimd", pdbg[k * 128:(k + 1) * 128,
                                       own0:own0 + width],
                        bCv(k, 0, width), sem("stRO"), waits=[])

            def xrv(m, s):
                base = A0 if m < 8 else A2
                mi = m % 8
                return base[:, mi * 1024 + s * SUB:mi * 1024 + s * SUB + SUB]

            def wsl_ro(k):
                return A4_16[:, k * 512:(k + 1) * 512]

            def wsl_dn(k):
                return A4_16[:, (16 + k) * 512:(16 + k + 1) * 512]

            free_tok = {0: None, 4: None}
            gi = 0
            xr_ev = {}
            bufB_guard_pending = True
            op_first = "wro" not in proj_pass
            proj_pass["wro"] = True
            oc_tok = [None]

            for q in range(4):
                extra_t = [t_ for t_ in oc_tok if t_] + \
                    [t_ for t_ in (prev_ck[0], prev_ck[1]) if t_]
                prev_ck[0] = prev_ck[1] = None
                if q == 0:
                    guard1 = list(wq_guard_g[0] or []) + extra_t
                    guard2 = list(wq_guard_g[1] or []) + extra_t
                else:
                    guard1 = guard2 = [(cPE, cPE.n)] + extra_t
                guard = guard1
                if op_first:
                    cd1, ck1 = stream_quarter(
                        wap["wro"][q], wsl_ro, extra_guard=guard,
                        cache_dst=wf16s[4, q],
                        dest_full=A4_16[:, 0:16 * 512])
                    cd2, ck2 = stream_quarter(
                        wap["wdn"][q], wsl_dn, extra_guard=None,
                        cache_dst=wf16s[7, q],
                        dest_full=A4_16[:, 16 * 512:32 * 512])
                    oc_tok[:] = [ck1, ck2]
                    prev_ck[0] = ck1
                    prev_ck[1] = ck2
                    cd1a = cd1b = cd1
                    cd2a = cd2b = cd2
                else:
                    cd1a, cd1b = load_quarter_f16_split(
                        wf16s[4, q], A4_16[:, 0:16 * 512],
                        extra_guard=guard1, qi=0)
                    cd2a, cd2b = load_quarter_f16_split(
                        wf16s[7, q], A4_16[:, 16 * 512:32 * 512],
                        extra_guard=guard2, qi=1)
                    oc_tok[:] = [None]
                pump_prescan(6)
                for s in range(ns):
                    bank0 = (gi % 2) * 4
                    gi += 1
                    wts = [cd1a, cd2a] if s == 0 else []
                    if free_tok[bank0] is not None:
                        wts.append(free_tok[bank0])

                    def mmg_lo(E, s_=s, b0=bank0):
                        mm = None
                        for k in range(NKH // 2):
                            for m in range(4):
                                mm = E.matmul(psb[b0 + m][:],
                                              wsl_ro(k)[:, m * 128:(m + 1) * 128],
                                              bAv(k, s_ * SUB, SUB),
                                              start=(k == 0), stop=False)
                        for k in range(NKH // 2):
                            for m in range(4):
                                mm = E.matmul(psb[b0 + m][:],
                                              wsl_dn(k)[:, m * 128:(m + 1) * 128],
                                              bCv(k, s_ * SUB, SUB),
                                              start=False, stop=False)
                        return mm

                    def mmg_hi(E, s_=s, b0=bank0):
                        mm = None
                        for k in range(NKH // 2, NKH):
                            for m in range(4):
                                mm = E.matmul(psb[b0 + m][:],
                                              wsl_ro(k)[:, m * 128:(m + 1) * 128],
                                              bAv(k, s_ * SUB, SUB),
                                              start=False, stop=False)
                        for k in range(NKH // 2, NKH):
                            for m in range(4):
                                mm = E.matmul(psb[b0 + m][:],
                                              wsl_dn(k)[:, m * 128:(m + 1) * 128],
                                              bCv(k, s_ * SUB, SUB),
                                              start=False, stop=(k == NKH - 1))
                        return mm
                    op("tensor", mmg_lo, waits=wts, inc=cPE)
                    gt = op("tensor", mmg_hi,
                            waits=([cd1b, cd2b] if s == 0 else []), inc=cPE)
                    last = None
                    for m in range(4):
                        mi = q * 4 + m
                        slot, tok = zring.acquire()
                        wl = [tok] if tok else []
                        t = dma("sync", xres[slot],
                                xin[mi * 128:(mi + 1) * 128,
                                    tile_t0 + s * SUB:tile_t0 + (s + 1) * SUB],
                                rZ[slot], waits=wl)
                        wts2 = [t, gt] if m == 0 else [t]
                        if mi >= 8 and bufB_guard_pending:
                            bufB_guard_pending = False
                        last = op("vector", lambda E, b=bank0 + m, mi_=mi, s_=s,
                                  xs_=xres[slot]: E.tensor_tensor(
                                      out=xrv(mi_, s_), in0=psb[b][:], in1=xs_,
                                      op=ALU.add), waits=wts2, inc=cDVE)
                        zring.release(slot, last)
                        xr_ev[(mi, s)] = last
                    free_tok[bank0] = last
            wq_guard_g[0] = [gt]
            wq_guard_g[1] = [gt]

            xr_evac_done = (cDVE, cDVE.n)
            st_guard = tile_guard["st16"]
            tile_guard["st16"] = None
            if debug_stop in ("A", "X"):
                xr_all = (cDVE, cDVE.n)
                for m in range(NKH):
                    for s in range(ns):
                        dma("gpsimd",
                            xrdbg[m * 128:(m + 1) * 128,
                                  own0 + s * SUB:own0 + (s + 1) * SUB],
                            xrv(m, s), sem("stXR"), waits=[xr_all])
            for s in range(ns):
                pe0 = cPE.n
                for m in range(NKH):
                    t16 = st16a[m % 2]
                    q16 = st16b[m % 2]
                    wts = [(cPE, pe0 + m - 1)] if m >= 2 else []
                    if m < 2 and st_guard is not None:
                        wts = wts + st_guard
                    ev = xr_ev.get((m, s))
                    if ev is not None:
                        wts = wts + [ev]
                    c1 = cast_slice("gpsimd", t16[:, 0:SUB], xrv(m, s),
                                    waits=wts, inc=cGP)
                    c2 = op("scalar", lambda E, qq=q16, src=xrv(m, s):
                            E.activation(out=qq[:, 0:SUB], in_=src,
                                         func=ACT.Square), waits=wts, inc=cSCA)
                    mw = [c1, c2]
                    if m == 0:
                        mw.append(xr_evac_done)
                        if bank_tok.get("stats") is not None:
                            mw.append(bank_tok["stats"])

                    def smm(E, m_=m, t=t16, qq=q16):
                        E.matmul(psb[0][0:1, :], ones16[:, 0:1], t[:, 0:SUB],
                                 start=(m_ == 0), stop=(m_ == NKH - 1))
                        return E.matmul(psb[1][0:1, :], ones16[:, 0:1],
                                        qq[:, 0:SUB],
                                        start=(m_ == 0), stop=(m_ == NKH - 1))
                    op("tensor", smm, waits=mw, inc=cPE)
                st_done = (cPE, cPE.n)
                sq, bb, bank_rel = emit_ln_rows(0, 1, st_done)
                bank_tok["stats"] = bank_rel
                psA, gA = hilo_bcast(rowC[0:1, :], SUB, 2, waits=[sq],
                                     bank_guard=bank_tok["psA"])
                psB, gB = hilo_bcast(rowA[0:1, :], SUB, 3, waits=[bb],
                                     bank_guard=bank_tok["psB"])
                ap_done = None
                mx_done = None
                for m in range(NKH):
                    wts = [gA, gB] if m == 0 else []
                    if debug_stop == "A" and m == 0:
                        wts = wts + [(sems["stXR"], sems["stXR"].n)]
                    if debug_stop == "X" and m == 0:
                        wts = wts + [(sems["stXR"], sems["stXR"].n)]
                    op("vector", lambda E, m_=m, s_=s: E.tensor_tensor(
                        out=xrv(m_, s_), in0=xrv(m_, s_), in1=psA, op=ALU.mult),
                        waits=wts)
                    ap_done = op("vector", lambda E, m_=m, s_=s: E.tensor_tensor(
                        out=xrv(m_, s_), in0=xrv(m_, s_), in1=psB,
                        op=ALU.subtract), inc=cDVE)
                    dma("gpsimd", x1s[m * 128:(m + 1) * 128,
                                      own0 + s * SUB:own0 + (s + 1) * SUB],
                        xrv(m, s), stX1, waits=[ap_done])
                    mwts = [ap_done]
                    if m == 0:
                        mwts.append((cPE, cPE.n))
                        ab = op("scalar", lambda E, s_=s: E.activation(
                            out=Mx, in_=xrv(0, s_), func=ACT.Abs),
                            waits=mwts, inc=cSCA)
                        mx_done = ab
                    else:
                        ab = op("scalar", lambda E, m_=m, s_=s: E.activation(
                            out=clb[1], in_=xrv(m_, s_), func=ACT.Abs),
                            waits=mwts + [mx_done], inc=cSCA)
                        mx_done = op("vector", lambda E: E.tensor_max(
                            out=Mx, in0=Mx, in1=clb[1]), waits=[ab], inc=cDVE)
                bank_tok["psA"] = ap_done
                bank_tok["psB"] = ap_done
                am = emit_amax_rows(Mx, mx_done)
                srow = rowB[0:1, :]
                rrow = rowC[0:1, :]
                ts_ = op("vector", lambda E: E.tensor_scalar(
                    out=srow, in0=srow, scalar1=1.0 / 127.0, scalar2=EPS,
                    op0=ALU.mult, op1=ALU.add), waits=[am], inc=cDVE)
                rr = op("vector", lambda E: E.reciprocal(out=rrow, in_=srow),
                        waits=[ts_], inc=cDVE)
                psS, gS = hilo_bcast(srow, SUB, 6, waits=[rr],
                                     bank_guard=bank_tok["psS"])
                evS = op("vector", lambda E, s_=s: E.tensor_copy(
                    out=A6[:, own0 + s_ * SUB:own0 + (s_ + 1) * SUB], in_=psS),
                    waits=[gS], inc=cDVE)
                bank_tok["psS"] = evS
                psR, gR = hilo_bcast(rrow, SUB, 7, waits=[rr],
                                     bank_guard=bank_tok["psR"])
                evR = op("vector", lambda E: E.tensor_copy(out=R2sbA, in_=psR),
                         waits=[gR, (cGP, cGP.n)], inc=cDVE)
                bank_tok["psR"] = evR
                for m in range(NKH):
                    slot, tok = xring.acquire()
                    ts = tmp[slot]
                    wl = [evR] if m == 0 else []
                    if tok is not None:
                        wl.append(tok)
                    mq = op("gpsimd", lambda E, m_=m, s_=s, t=ts: E.tensor_tensor(
                        out=t, in0=xrv(m_, s_), in1=R2sbA, op=ALU.mult),
                        waits=wl, inc=cGP)
                    cv = op("vector", lambda E, t=ts: E.tensor_copy(
                        out=t.bitcast(I32), in_=t), waits=[mq], inc=cDVE)
                    xq16 = (st16a + st16b)[m % 4]
                    wl2 = [cv]
                    if m >= 4 or s > 0 or tile_t0 > WARM:
                        wl2.append((stXQ4[m % 4], stXQ4[m % 4].n))
                    x2c = op("vector", lambda E, t=ts, d=xq16: E.tensor_copy(
                        out=d[:, 0:SUB], in_=t.bitcast(I32)), waits=wl2,
                        inc=cDVE)
                    dma("gpsimd",
                        xqs[m * 128:(m + 1) * 128,
                            own0 + s * SUB:own0 + (s + 1) * SUB],
                        xq16[:, 0:SUB], stXQ4[m % 4], waits=[x2c])
                    xring.release(slot, x2c)
                st_guard = [(s_, s_.n) for s_ in stXQ4]
            tile_guard["x16"] = [(cDVE, cDVE.n), (stX1, stX1.n),
                                 (cGP, cGP.n)] + [(s_, s_.n) for s_ in stXQ4]
            tile_guard["bufA"] = None
            tile_guard["bufB"] = [(cDVE, cDVE.n), (stX1, stX1.n),
                                  (cGP, cGP.n)] + [(s_, s_.n) for s_ in stXQ4]
            tile_guard["bufC"] = [(cPE, cPE.n)]
            tile_guard["st16"] = [(s_, s_.n) for s_ in stXQ4]

        emit_attn_tile(0, WARM, warm_only=True)
        t0 = WARM
        while t0 < WARM + T_OWN:
            w = min(1024, WARM + T_OWN - t0)
            emit_attn_tile(t0, w, warm_only=False)
            t0 += w

        if debug_stop == "A":
            barrier("gpsimd", [(stX1, stX1.n)] + [(s_, s_.n) for s_ in stXQ4])
            barrier("sync", [(stX1, stX1.n)] + [(s_, s_.n) for s_ in stXQ4])
            with nc.Block() as block:
                @block.sync
                def _(E):
                    for fn in Q["sync"]:
                        fn(E)

                @block.tensor
                def _(E):
                    for fn in Q["tensor"]:
                        fn(E)

                @block.vector
                def _(E):
                    for fn in Q["vector"]:
                        fn(E)

                @block.scalar
                def _(E):
                    for fn in Q["scalar"]:
                        fn(E)

                @block.gpsimd
                def _(E):
                    for fn in Q["gpsimd"]:
                        fn(E)
            return nc

        def xqv(k, c0, n):
            base = A0_16 if k < 8 else A1_16
            kk = k % 8
            return base[:, kk * T_OWN + c0:kk * T_OWN + c0 + n]

        xq_guard = tile_guard["x16"] + [(cSCA, cSCA.n), (cPE, cPE.n)]
        xq_sp = [(s_, s_.n) for s_ in stXQ4]
        for k in range(NKH):
            dma("scalar", xqv(k, 0, T_OWN), xqs[k * 128:(k + 1) * 128, :],
                sXQ, waits=(xq_guard if k == 0 else []) + xq_sp)
        xq_ready = (sXQ, sXQ.n)

        pump_prescan(len(prescan_units))

        def finalize_ws(ai, ci):
            acc = wsc[:, ai:ai + 1]
            row = rowA[0:1, 0:128]
            tv = transpose_col_to_row(acc, 4, row, waits=[acc_done_tok[ai]])
            sm_ = op("vector", lambda E: E.reduce_sum(
                out=rowB[0:1, 0:1], in_=row, axis=AX.X), waits=[tv], inc=cDVE)
            sc_ = op("vector", lambda E: E.tensor_scalar(
                out=rowB[0:1, 0:1], in0=rowB[0:1, 0:1], scalar1=1.0 / NW_ELEM,
                scalar2=EPS, op0=ALU.mult, op1=ALU.add), waits=[sm_], inc=cDVE)
            ps, g = hilo_bcast(rowB[0:1, 0:1], 1, 5, waits=[sc_])
            cp_ = op("vector", lambda E: E.tensor_copy(
                out=wsc[:, ci:ci + 1], in_=ps[:, 0:1]), waits=[g], inc=cDVE)
            hf_ = op("vector", lambda E: E.tensor_scalar(
                out=wsc[:, ci + 1:ci + 2], in0=wsc[:, ci:ci + 1], scalar1=0.5,
                scalar2=None, op0=ALU.mult), waits=[cp_], inc=cDVE)
            return op("vector", lambda E: E.tensor_scalar(
                out=wsc[:, ci + 2:ci + 3], in0=wsc[:, ci:ci + 1], scalar1=-0.5,
                scalar2=None, op0=ALU.mult), waits=[hf_], inc=cDVE)

        wsg_done = finalize_ws(0, 8)
        wsu_done = finalize_ws(1, 11)
        wsd_done = finalize_ws(2, 14)
        S32 = A6
        b1_guard = [(cPE, cPE.n), (cDVE, cDVE.n), (cSCA, cSCA.n),
                    (cGP, cGP.n), (stX1, stX1.n), (stWF, stWF.n)] + \
            [(s_, s_.n) for s_ in stXQ4]

        def Mh_sub(s):
            return xres[s] if s < 3 else Mx

        clb_tok = [(cGP, cGP.n), None]
        hq_guard = [list(b1_guard), list(b1_guard)]
        wq_dest_guard = [None]
        wqg_guard = None
        bring.tok[0] = wring.tok[0]
        bring.tok[1] = wring.tok[1]
        for i_ in range(2, 10):
            bring.tok[i_] = (cPE, cPE.n)

        def bslot(i):
            if i < 2:
                return A5[:, i * 1024:(i + 1) * 1024]
            if i < 6:
                return A3[:, 4096 + (i - 2) * 1024:4096 + (i - 1) * 1024]
            return A4[:, 4096 + (i - 6) * 1024:4096 + (i - 5) * 1024]

        def emit_quant_unit(dram_slice, dest_ap, ci, extra, sgn_extra=None,
                            cache=None):
            sA, tokA = bring.acquire()
            sB, tokB = bring.acquire()
            wl = [tokA] if tokA else []
            dst = bslot(sA)
            if len(dram_slice.shape) == 3:
                dst = dst.rearrange("p (k c) -> p k c", k=dram_slice.shape[1])
            t = dma("sync" if sA % 2 == 0 else "scalar", dst, dram_slice,
                    rB[sA], waits=wl)
            wts = [t] + (extra or [])
            if tokB:
                wts.append(tokB)
            cl = op("vector", lambda E, a=sA, b=sB, ci_=ci: E.tensor_scalar(
                out=bslot(b), in0=bslot(a),
                scalar1=wsc[:, ci_ + 1:ci_ + 2],
                scalar2=wsc[:, ci_ + 2:ci_ + 3],
                op0=ALU.min, op1=ALU.max), waits=wts, inc=cDVE)
            sb = op("gpsimd", lambda E, a=sA, b=sB: E.tensor_tensor(
                out=bslot(a), in0=bslot(a), in1=bslot(b), op=ALU.subtract),
                waits=[cl], inc=cGP)
            bring.release(sB, sb)
            swl = [sb] + (sgn_extra or [])
            sg_ = op("scalar", lambda E, a=sA, d=dest_ap: E.activation(
                out=d, in_=bslot(a), func=ACT.Sign), waits=swl, inc=cSCA)
            if cache is not None:
                ct = dma("gpsimd", cache, dest_ap, stWDQ, waits=[sg_])
                bring.release(sA, sg_)
                return sg_, ct
            bring.release(sA, sg_)
            return sg_, None

        wdq_units = [(k, hh) for k in range(NKI) for hh in range(2)]
        wdq_state = {"sgtok": None, "first": True}

        def pump_wdquant(n):
            for _ in range(n):
                if not wdq_units:
                    return
                k, hh = wdq_units.pop(0)
                extra = []
                if wdq_state["first"]:
                    wdq_state["first"] = False
                    extra.append((wsd_done[0], wsd_done[1]))
                sgn_extra = []
                if wdq_state["sgtok"] is not None:
                    sgn_extra.append(wdq_state["sgtok"])
                sg_, ct = emit_quant_unit(
                    fd[k * 128:(k + 1) * 128, hh * 1024:(hh + 1) * 1024],
                    stwideB16, 14, extra, sgn_extra=sgn_extra,
                    cache=wdqs[k * 128:(k + 1) * 128,
                               hh * 1024:(hh + 1) * 1024])
                wdq_state["sgtok"] = ct
        mh_started = [False] * NS_T
        gate_free = None
        up_free = None
        h_tmp_tok = {}
        habs_tok = {}
        for qq in range(16):
            buf = qq % 2
            hqt = A2_16[:, buf * (4 * T_OWN):(buf + 1) * (4 * T_OWN)]

            def hv(jm, c0, n, b=hqt):
                return b[:, jm * T_OWN + c0:jm * T_OWN + c0 + n]

            def quant_quarter(dram_q, dest16, ci, first_extra, guard):
                done = None
                for kg in range(8):
                    extra = list(first_extra) if (kg == 0 and first_extra) \
                        else []
                    sgn_extra = [guard] if (kg == 0 and guard) else []
                    done, _ = emit_quant_unit(
                        dram_q[kg * 2:(kg + 1) * 2, :, :].rearrange(
                            "k p c -> p k c"),
                        dest16(kg * 2, wide=True), ci, extra,
                        sgn_extra=sgn_extra)
                return done

            def wg16(k, wide=False):
                n = 1024 if wide else 512
                return A3_16[:, k * 512:k * 512 + n]

            def wu16(k, wide=False):
                n = 1024 if wide else 512
                return A4_16[:, k * 512:k * 512 + n]

            fe = []
            if qq == 0:
                fe = [wsg_done] + b1_guard
            cdg = quant_quarter(fg[qq], wg16, 8, fe, wq_dest_guard[0])
            cdu = quant_quarter(fu[qq], wu16, 11,
                                ([wsu_done] + b1_guard) if qq == 0 else None,
                                wq_dest_guard[0])

            for s in range(NS_T):
                wts = [cdg, cdu] if s == 0 else []
                if qq == 0 and s == 0:
                    wts.append(xq_ready)
                if gate_free is not None:
                    wts.append(gate_free)

                def mmg_gate(E, s_=s, w_=wg16):
                    mm = None
                    for k in range(NKH):
                        for m in range(4):
                            mm = E.matmul(psb[m][:],
                                          w_(k)[:, m * 128:(m + 1) * 128],
                                          xqv(k, s_ * SUB, SUB),
                                          start=(k == 0), stop=(k == NKH - 1))
                    return mm
                gg = op("tensor", mmg_gate, waits=wts, inc=cPE)
                wts = [up_free] if up_free is not None else []

                def mmg_up(E, s_=s, w_=wu16):
                    mm = None
                    for k in range(NKH):
                        for m in range(4):
                            mm = E.matmul(psb[4 + m][:],
                                          w_(k)[:, m * 128:(m + 1) * 128],
                                          xqv(k, s_ * SUB, SUB),
                                          start=(k == 0), stop=(k == NKH - 1))
                    return mm
                gu = op("tensor", mmg_up, waits=wts, inc=cPE)

                hg = hq_guard[buf] if s == 0 else None
                for jm in range(4):
                    tg = tmp[jm % 2]
                    tu = tmp[2]
                    wts = [gg] if jm == 0 else []
                    pt = h_tmp_tok.get(jm % 2)
                    if pt is not None:
                        wts.append(pt)
                    d1 = op("vector", lambda E, b=jm, t=tg, s_=s:
                            E.scalar_tensor_tensor(
                                out=t, in0=psb[b][:], scalar=wsc[:, 8:9],
                                in1=S32[:, s_ * SUB:(s_ + 1) * SUB],
                                op0=ALU.mult, op1=ALU.mult),
                            waits=wts, inc=cDVE)
                    sl = op("scalar", lambda E, t=tg: E.activation(
                        out=t, in_=t, func=ACT.Silu), waits=[d1], inc=cSCA)
                    wts = [gu] if jm == 0 else []
                    pt = h_tmp_tok.get(2)
                    if pt is not None:
                        wts.append(pt)
                    d2 = op("vector", lambda E, b=4 + jm, t=tu, s_=s:
                            E.scalar_tensor_tensor(
                                out=t, in0=psb[b][:], scalar=wsc[:, 11:12],
                                in1=S32[:, s_ * SUB:(s_ + 1) * SUB],
                                op0=ALU.mult, op1=ALU.mult),
                            waits=wts, inc=cDVE)
                    wts = [sl, d2]
                    if hg is not None and jm == 0:
                        wts = wts + hg
                        hg = None
                    hw = op("gpsimd", lambda E, a=tg, b=tu, jm_=jm, s_=s,
                            hv_=hv: E.tensor_tensor(
                                out=hv_(jm_, s_ * SUB, SUB), in0=a,
                                in1=b, op=ALU.mult),
                            waits=wts, inc=cGP)
                    h_tmp_tok[jm % 2] = hw
                    h_tmp_tok[2] = hw
                    if not mh_started[s]:
                        mh_started[s] = True
                        op("scalar", lambda E, jm_=jm, s_=s, hv_=hv:
                            E.activation(out=Mh_sub(s_),
                                         in_=hv_(jm_, s_ * SUB, SUB),
                                         func=ACT.Abs),
                            waits=[hw], inc=cSCA)
                    else:
                        habs = st16a[jm % 2]
                        hbt = habs_tok.get(jm % 2)
                        ab = op("scalar", lambda E, jm_=jm, s_=s, hv_=hv,
                                hb=habs: E.activation(
                                    out=hb[:, 0:SUB],
                                    in_=hv_(jm_, s_ * SUB, SUB),
                                    func=ACT.Abs),
                            waits=[hw] + ([hbt] if hbt else []), inc=cSCA)
                        habs_tok[jm % 2] = op(
                            "vector", lambda E, s_=s, hb=habs: E.tensor_max(
                                out=Mh_sub(s_), in0=Mh_sub(s_),
                                in1=hb[:, 0:SUB]),
                            waits=[ab], inc=cDVE)
                gate_free = (cDVE, cDVE.n)
                up_free = (cDVE, cDVE.n)
            pump_wdquant(8)
            wqg_guard = (cPE, cPE.n)
            wq_dest_guard[0] = (cPE, cPE.n)
            hq_guard[buf] = [dma(
                "gpsimd",
                hs[qq * 512:(qq + 1) * 512, :].rearrange(
                    "(j p) t -> p j t", p=128),
                hqt.rearrange("p (j t) -> p j t", j=4),
                stH, waits=[(cGP, cGP.n)])]

        pump_wdquant(len(wdq_units))

        barrier("tensor", [(cDVE, cDVE.n), (cSCA, cSCA.n)])
        R2s = A6
        last_s32_use = (cDVE, cDVE.n)
        for s in range(NS_T):
            am = emit_amax_rows(Mh_sub(s), None)
            srow = rowB[0:1, :]
            rrow = rowC[0:1, :]
            ts_ = op("vector", lambda E: E.tensor_scalar(
                out=srow, in0=srow, scalar1=1.0 / 127.0, scalar2=EPS,
                op0=ALU.mult, op1=ALU.add), waits=[am], inc=cDVE)
            rr = op("vector", lambda E: E.reciprocal(out=rrow, in_=srow),
                    waits=[ts_], inc=cDVE)
            psR, gR = hilo_bcast(rrow, SUB, 7, waits=[rr],
                                 bank_guard=bank_tok["psR"])
            evR = op("vector", lambda E, s_=s: E.tensor_copy(
                out=R2s[:, s_ * SUB:(s_ + 1) * SUB], in_=psR),
                waits=[gR] + ([last_s32_use] if s == 0 else []), inc=cDVE)
            bank_tok["psR"] = evR
        r2_done = (cDVE, cDVE.n)

        if debug_stop == "B1":
            barrier("gpsimd", [(stH, stH.n)])
            barrier("vector", [r2_done, (wsd_done[0], wsd_done[1])])
            dbg = nc.dram_tensor("dbg", [128, T_OWN + 32], F32,
                                 kind="ExternalOutput").ap()
            dmy = dma("gpsimd", dbg[:, 0:T_OWN], R2s[:, 0:T_OWN], sem("stDBG"),
                      waits=[r2_done])
            dmy2 = dma("gpsimd", dbg[:, T_OWN:T_OWN + 17], wsc[:, 0:17],
                       sems["stDBG"], waits=[(wsd_done[0], wsd_done[1])])
            barrier("gpsimd", [(sems["stDBG"], sems["stDBG"].n)])
            barrier("sync", [(stH, stH.n)])
            with nc.Block() as block:
                @block.sync
                def _(E):
                    for fn in Q["sync"]:
                        fn(E)

                @block.tensor
                def _(E):
                    for fn in Q["tensor"]:
                        fn(E)

                @block.vector
                def _(E):
                    for fn in Q["vector"]:
                        fn(E)

                @block.scalar
                def _(E):
                    for fn in Q["scalar"]:
                        fn(E)

                @block.gpsimd
                def _(E):
                    for fn in Q["gpsimd"]:
                        fn(E)
            return nc

        def yv(m, c0, n):
            mm_ = m % 8
            base = A0 if mm_ < 4 else A1
            return base[:, (mm_ % 4) * T_OWN + c0:(mm_ % 4) * T_OWN + c0 + n]

        def wdq16(j):
            return A2_16[:, j * 2048:(j + 1) * 2048]

        def hqslot(i):
            base = A3_16 if i < 8 else A4_16
            ii = i % 8
            return base[:, ii * T_OWN:(ii + 1) * T_OWN]

        def wdhalf(i):
            return A5[:, i * 1024:(i + 1) * 1024]

        b2_guard = [wqg_guard, r2_done, (cSCA, cSCA.n), (cGP, cGP.n),
                    (stH, stH.n), (wsd_done[0], wsd_done[1])]
        hq_free = {}
        wdq_free = {}
        for half in range(2):
            mlist = list(range(half * 8, half * 8 + 8))
            bank_free = {0: None, 1: None, 2: None, 3: None}
            for kg in range(8):
                hq_toks = []
                wd_toks = []
                for j in range(8):
                    k = kg * 8 + j
                    ring_i = k % 16
                    hguard = hq_free.get(ring_i)
                    wguard = wdq_free.get(j)
                    wl2 = [wguard] if wguard else []
                    if kg == 0 and j == 0:
                        wl2.append((stWDQ, stWDQ.n))
                    t2 = dma("sync" if j % 2 == 0 else "scalar", wdq16(j),
                             wdqs[k * 128:(k + 1) * 128, :],
                             rE[j], waits=wl2)
                    wd_toks.append(t2)
                    if half == 0:
                        wl = [hguard] if hguard else []
                        if kg == 0 and j == 0:
                            wl = wl + b2_guard
                        t = dma("sync" if j % 2 == 1 else "scalar",
                                hqslot(ring_i),
                                hs[k * 128:(k + 1) * 128, :], rH[j],
                                waits=wl + [(stH, stH.n)])
                        last = None
                        for hh in range(max(1, T_OWN // 1024)):
                            c0 = hh * 1024
                            cn = min(1024, T_OWN - c0)
                            wslot = [tmpwide, stwide][(j * 2 + hh) % 2]
                            wlq = [t]
                            pt = h_tmp_tok.get((j * 2 + hh) % 2)
                            if pt is not None:
                                wlq.append(pt)
                            mq = op("vector", lambda E, ri=ring_i, c0_=c0,
                                    cn_=cn, t_=wslot: E.tensor_tensor(
                                        out=t_[:, 0:cn_],
                                        in0=hqslot(ri)[:, c0_:c0_ + cn_],
                                        in1=R2s[:, c0_:c0_ + cn_],
                                        op=ALU.mult), waits=wlq, inc=cDVE)
                            cv = op("vector", lambda E, t_=wslot, cn_=cn:
                                    E.tensor_copy(
                                        out=t_.bitcast(I32)[:, 0:cn_],
                                        in_=t_[:, 0:cn_]),
                                    waits=[mq], inc=cDVE)
                            last = op("gpsimd", lambda E, t_=wslot, ri=ring_i,
                                      c0_=c0, cn_=cn: E.tensor_copy(
                                          out=hqslot(ri)[:, c0_:c0_ + cn_],
                                          in_=t_.bitcast(I32)[:, 0:cn_]),
                                      waits=[cv], inc=cGP)
                            h_tmp_tok[(j * 2 + hh) % 2] = last
                        dma("gpsimd", hqs[k * 128:(k + 1) * 128, :],
                            hqslot(ring_i), stHQ, waits=[last])
                        hq_toks.append(last)
                    else:
                        wl = [hguard] if hguard else []
                        if kg == 0 and j == 0:
                            wl.append((stHQ, stHQ.n))
                            wl.append((cGP, cGP.n))
                            wl.append((stY1, stY1.n))
                        t = dma("sync" if j % 2 == 1 else "scalar",
                                hqslot(ring_i),
                                hqs[k * 128:(k + 1) * 128, :], rH[j],
                                waits=wl)
                        hq_toks.append(t)

                gidx = 0
                first_wts = hq_toks + wd_toks
                for s in range(NS_T):
                    for m in mlist:
                        bank = gidx % 4
                        gidx += 1
                        wts = []
                        if first_wts is not None:
                            wts = list(first_wts)
                            first_wts = None
                        if bank_free[bank] is not None:
                            wts.append(bank_free[bank])

                        def mmg(E, s_=s, m_=m, b_=bank, kg_=kg):
                            mm = None
                            for j in range(8):
                                k = kg_ * 8 + j
                                mm = E.matmul(
                                    psb[b_][:],
                                    wdq16(j)[:, m_ * 128:(m_ + 1) * 128],
                                    hqslot(k % 16)[:, s_ * SUB:(s_ + 1) * SUB],
                                    start=(j == 0), stop=(j == 7))
                            return mm
                        gt = op("tensor", mmg, waits=wts, inc=cPE)
                        if kg == 0:
                            ev = op("vector", lambda E, m_=m, s_=s, b_=bank:
                                    E.tensor_copy(out=yv(m_, s_ * SUB, SUB),
                                                  in_=psb[b_][:]),
                                    waits=[gt], inc=cDVE)
                        else:
                            ev = op("vector", lambda E, m_=m, s_=s, b_=bank:
                                    E.tensor_tensor(out=yv(m_, s_ * SUB, SUB),
                                                    in0=yv(m_, s_ * SUB, SUB),
                                                    in1=psb[b_][:], op=ALU.add),
                                    waits=[gt], inc=cDVE)
                        bank_free[bank] = ev
                for j in range(8):
                    k = kg * 8 + j
                    hq_free[k % 16] = (cPE, cPE.n)
                for j in range(8):
                    wdq_free[j] = (cPE, cPE.n)
            if half == 0:
                ydone = (cDVE, cDVE.n)
                for m in range(8):
                    dma("scalar", y1s[m * 128:(m + 1) * 128, :],
                        yv(m, 0, T_OWN), stY1, waits=[ydone])

        pe_all = (cPE, cPE.n)
        y_all = (cDVE, cDVE.n)

        op("vector", lambda E: E.reciprocal(out=R2s[:, 0:T_OWN],
                                            in_=R2s[:, 0:T_OWN]),
           waits=[(cGP, cGP.n), (wsd_done[0], wsd_done[1])])
        s2w_done = op("vector", lambda E: E.tensor_scalar(
            out=R2s[:, 0:T_OWN], in0=R2s[:, 0:T_OWN],
            scalar1=wsc[:, 14:15], scalar2=None, op0=ALU.mult), inc=cDVE)
        S2W = R2s

        def xr2v(m, s):
            if m >= 8:
                return yv(m, s * SUB, SUB)
            base = A3 if m < 4 else A4
            return base[:, (m % 4) * T_OWN + s * SUB:
                        (m % 4) * T_OWN + (s + 1) * SUB]

        for m in range(NKH):
            slot, tok = fring.acquire()
            stage = A2[:, (slot % 4) * 2048:((slot % 4) + 1) * 2048]
            wl = [tok] if tok else []
            if m == 0:
                wl += [pe_all, y_all, (cGP, cGP.n), (cSCA, cSCA.n)]
            t = dma("sync", stage[:, 0:T_OWN],
                    x1s[m * 128:(m + 1) * 128, :], rF[slot],
                    waits=wl + [(stX1, stX1.n)])
            if m < 8:
                slot2, tok2 = fring.acquire()
                stage2 = A2[:, (slot2 % 4) * 2048:((slot2 % 4) + 1) * 2048]
                wl2 = [tok2] if tok2 else []
                t2 = dma("sync", stage2[:, 0:T_OWN],
                         y1s[m * 128:(m + 1) * 128, :], rF[slot2],
                         waits=wl2 + [(stY1, stY1.n)])
                last = None
                for s in range(NS_T):
                    wl3 = [t, t2, s2w_done] if s == 0 else []
                    op("vector", lambda E, m_=m, s_=s, b=stage2:
                       E.tensor_tensor(
                           out=xr2v(m_, s_),
                           in0=b[:, s_ * SUB:(s_ + 1) * SUB],
                           in1=S2W[:, s_ * SUB:(s_ + 1) * SUB],
                           op=ALU.mult), waits=wl3)
                    last = op("vector", lambda E, m_=m, s_=s, a=stage:
                              E.tensor_tensor(
                                  out=xr2v(m_, s_), in0=xr2v(m_, s_),
                                  in1=a[:, s_ * SUB:(s_ + 1) * SUB],
                                  op=ALU.add), inc=cDVE)
                fring.release(slot, last)
                fring.release(slot2, last)
            else:
                last = None
                for s in range(NS_T):
                    wl3 = [t, s2w_done] if s == 0 else []
                    op("vector", lambda E, m_=m, s_=s:
                       E.tensor_tensor(out=xr2v(m_, s_), in0=xr2v(m_, s_),
                                       in1=S2W[:, s_ * SUB:(s_ + 1) * SUB],
                                       op=ALU.mult), waits=wl3)
                    last = op("vector", lambda E, m_=m, s_=s, a=stage:
                              E.tensor_tensor(out=xr2v(m_, s_),
                                              in0=xr2v(m_, s_),
                                              in1=a[:, s_ * SUB:(s_ + 1) * SUB],
                                              op=ALU.add),
                              waits=[] if s > 0 else [], inc=cDVE)
                fring.release(slot, last)
        xr2_done = (cDVE, cDVE.n)

        out_tok = [None] * NKH
        for s in range(NS_T):
            pe0 = cPE.n
            for m in range(NKH):
                t16 = st16a[m % 2]
                q16 = st16b[m % 2]
                wts = [(cPE, pe0 + m - 1)] if m >= 2 else [xr2_done]
                c1 = cast_slice("gpsimd", t16[:, 0:SUB], xr2v(m, s), waits=wts,
                                inc=cGP)
                c2 = op("scalar", lambda E, qq=q16, src=xr2v(m, s): E.activation(
                    out=qq[:, 0:SUB], in_=src, func=ACT.Square), waits=wts,
                    inc=cSCA)
                mw = [c1, c2]
                if m == 0 and bank_tok.get("stats") is not None:
                    mw.append(bank_tok["stats"])

                def smm(E, m_=m, t=t16, qq=q16):
                    E.matmul(psb[0][0:1, :], ones16[:, 0:1], t[:, 0:SUB],
                             start=(m_ == 0), stop=(m_ == NKH - 1))
                    return E.matmul(psb[1][0:1, :], ones16[:, 0:1], qq[:, 0:SUB],
                                    start=(m_ == 0), stop=(m_ == NKH - 1))
                op("tensor", smm, waits=mw, inc=cPE)
            st_done = (cPE, cPE.n)
            sq, bb, bank_rel = emit_ln_rows(0, 1, st_done)
            bank_tok["stats"] = bank_rel
            psA, gA = hilo_bcast(rowC[0:1, :], SUB, 2, waits=[sq],
                                 bank_guard=bank_tok["psA"])
            psB, gB = hilo_bcast(rowA[0:1, :], SUB, 3, waits=[bb],
                                 bank_guard=bank_tok["psB"])
            ap_done = None
            for m in range(NKH):
                wts = [gA, gB] if m == 0 else []
                op("vector", lambda E, m_=m, s_=s: E.tensor_tensor(
                    out=xr2v(m_, s_), in0=xr2v(m_, s_), in1=psA, op=ALU.mult),
                    waits=wts)
                ap_done = op("vector", lambda E, m_=m, s_=s: E.tensor_tensor(
                    out=xr2v(m_, s_), in0=xr2v(m_, s_), in1=psB,
                    op=ALU.subtract), inc=cDVE)
                if s == NS_T - 1:
                    out_tok[m] = ap_done
            bank_tok["psA"] = ap_done
            bank_tok["psB"] = ap_done
        for m in range(NKH):
            if m >= 8:
                mm_ = m % 8
                base = A0 if mm_ < 4 else A1
                srcv = base[:, (mm_ % 4) * T_OWN:(mm_ % 4 + 1) * T_OWN]
            else:
                base = A3 if m < 4 else A4
                srcv = base[:, (m % 4) * T_OWN:(m % 4 + 1) * T_OWN]
            dma("scalar", out_d[m * 128:(m + 1) * 128, :], srcv,
                stOUT, waits=[out_tok[m]])
        barrier("gpsimd", [(stOUT, stOUT.n)])
        barrier("scalar", [(stOUT, stOUT.n)])
        barrier("sync", [(stOUT, stOUT.n)])

        with nc.Block() as block:
            @block.sync
            def _(E):
                for fn in Q["sync"]:
                    fn(E)

            @block.tensor
            def _(E):
                for fn in Q["tensor"]:
                    fn(E)

            @block.vector
            def _(E):
                for fn in Q["vector"]:
                    fn(E)

            @block.scalar
            def _(E):
                for fn in Q["scalar"]:
                    fn(E)

            @block.gpsimd
            def _(E):
                for fn in Q["gpsimd"]:
                    fn(E)

    return nc


def _qmajor(w, nq):
    K, M = w.shape
    assert M == nq * 512
    a = w.reshape(K // 128, 128, nq, 512)
    return np.ascontiguousarray(a.transpose(2, 0, 1, 3))


def kernel(**inp):
    B_, S_, H_ = 2, 8192, 2048
    T = 2048
    x = np.asarray(inp["L0_attn_x"], np.float32)

    att = {k: np.ascontiguousarray(np.asarray(inp["L0_attn_" + k], np.float32))
           for k in ["w_f", "w_i", "w_v", "w_q", "w_ro", "w_up", "w_gate",
                     "w_down"]}
    ffn_g = np.ascontiguousarray(np.asarray(inp["L0_ffn_w_gate"], np.float32).T)
    ffn_u = np.ascontiguousarray(np.asarray(inp["L0_ffn_w_up"], np.float32).T)
    ffn_d = np.ascontiguousarray(np.asarray(inp["L0_ffn_w_down"], np.float32).T)

    wm = {
        "wf": _qmajor(att["w_f"], 4), "wi": _qmajor(att["w_i"], 4),
        "wv": _qmajor(att["w_v"], 4), "wq_": _qmajor(att["w_q"], 4),
        "wro": _qmajor(att["w_ro"], 4), "wup": _qmajor(att["w_up"], 4),
        "wgt": _qmajor(att["w_gate"], 4), "wdn": _qmajor(att["w_down"], 4),
        "fg": _qmajor(ffn_g, 16), "fu": _qmajor(ffn_u, 16),
        "fd": ffn_d,
        "ident": np.eye(128, dtype=np.float32),
    }

    in_maps = []
    for c in range(8):
        b, ch = divmod(c, 4)
        lo = ch * T - WARM
        if lo < 0:
            seg = np.concatenate(
                [np.zeros((WARM, H_), np.float32), x[b, 0:(ch + 1) * T]], 0)
        else:
            seg = x[b, lo:(ch + 1) * T]
        xc = np.ascontiguousarray(seg.T)
        m = {"xin": xc}
        m.update(wm)
        in_maps.append(m)

    nc = build_nc(T_OWN=T)
    res = run_bass_kernel_spmd(nc, in_maps, core_ids=list(range(8)))
    out = np.empty((B_, S_, H_), np.float32)
    for c in range(8):
        b, ch = divmod(c, 4)
        out[b, ch * T:(ch + 1) * T, :] = res.results[c]["out"].T
    return out


# version 17
# speedup vs baseline: 1.2026x; 1.2026x over previous
import numpy as np
from contextlib import ExitStack

import concourse.bass as bass
import concourse.mybir as mybir
from concourse.bass_utils import run_bass_kernel_spmd

F32 = mybir.dt.float32
F16 = mybir.dt.float16
I32 = mybir.dt.int32
ALU = mybir.AluOpType
ACT = mybir.ActivationFunctionType
AX = mybir.AxisListType

H = 2048
NKH = H // 128
II = 8192
NKI = II // 128
SUB = 512
WARM = 512
EPS = 1e-5
NW_ELEM = float(II * H)


class Sem:
    __slots__ = ("h", "n", "name")

    def __init__(self, h, name):
        self.h = h
        self.n = 0
        self.name = name


class Ring:
    def __init__(self, n):
        self.n = n
        self.tok = [None] * n
        self.i = 0

    def acquire(self):
        s = self.i % self.n
        self.i += 1
        return s, self.tok[s]

    def release(self, s, token):
        self.tok[s] = token


def build_nc(T_OWN=2048, debug_stop=None):
    NS_T = T_OWN // SUB
    TCOL = T_OWN + WARM

    nc = bass.Bass("TRN2", target_bir_lowering=False,
                   detect_race_conditions=False)

    xin = nc.dram_tensor("xin", [H, TCOL], F32, kind="ExternalInput").ap()
    ident_in = nc.dram_tensor("ident", [128, 128], F32, kind="ExternalInput").ap()
    wnames = ["wf", "wi", "wv", "wq_", "wro", "wup", "wgt", "wdn"]
    wap = {n: nc.dram_tensor(n, [4, NKH, 128, 512], F32, kind="ExternalInput").ap()
           for n in wnames}
    fg = nc.dram_tensor("fg", [16, NKH, 128, 512], F32, kind="ExternalInput").ap()
    fu = nc.dram_tensor("fu", [16, NKH, 128, 512], F32, kind="ExternalInput").ap()
    fd = nc.dram_tensor("fd", [II, H], F32, kind="ExternalInput").ap()
    scratch_kind = "ExternalOutput" if debug_stop else "Internal"
    xrdbg = (nc.dram_tensor("xrdbg", [H, T_OWN], F32, kind="ExternalOutput").ap()
             if debug_stop else None)
    rodbg = (nc.dram_tensor("rodbg", [H, T_OWN], F16, kind="ExternalOutput").ap()
             if debug_stop else None)
    stdbg = (nc.dram_tensor("stdbg", [H, T_OWN], F16, kind="ExternalOutput").ap()
             if debug_stop else None)
    scdbg = (nc.dram_tensor("scdbg", [128, 32], F32, kind="ExternalOutput").ap()
             if debug_stop else None)
    fdbg = (nc.dram_tensor("fdbg", [H, T_OWN], F16, kind="ExternalOutput").ap()
            if debug_stop else None)
    gdbg = (nc.dram_tensor("gdbg", [H, T_OWN], F16, kind="ExternalOutput").ap()
            if debug_stop else None)
    pdbg = (nc.dram_tensor("pdbg", [H, T_OWN], F16, kind="ExternalOutput").ap()
            if debug_stop else None)
    out_d = nc.dram_tensor("out", [H, T_OWN], F32, kind="ExternalOutput").ap()
    x1s = nc.dram_tensor("x1s", [H, T_OWN], F32, kind=scratch_kind).ap()
    xqs = nc.dram_tensor("xqs", [H, T_OWN], F16, kind=scratch_kind).ap()
    hs = nc.dram_tensor("hs", [II, T_OWN], F16, kind=scratch_kind).ap()
    hqs = nc.dram_tensor("hqs", [II, T_OWN], F16, kind=scratch_kind).ap()
    wdqs = nc.dram_tensor("wdqs", [II, H], F16, kind=scratch_kind).ap()
    wf16s = nc.dram_tensor("wf16s", [8, 4, 128, 8192], F16, kind="Internal").ap()
    y1s = nc.dram_tensor("y1s", [H // 2, T_OWN], F32, kind=scratch_kind).ap()

    ctx = ExitStack()
    with ctx:
        AR = 53192
        arena = ctx.enter_context(nc.sbuf_tensor("arena", [128, AR], F32)).ap()
        off = [0]

        def alloc(n):
            o = off[0]
            off[0] += n
            assert off[0] <= AR, f"arena overflow {off[0]}"
            return arena[:, o:o + n]

        A0 = alloc(8192)
        A1 = alloc(8192)
        A2 = alloc(8192)
        A3 = alloc(8192)
        A4 = alloc(8192)
        A5 = alloc(2048)
        A6 = alloc(max(T_OWN, 2048))
        SM_N = AR - off[0]
        smo = [0]
        SMbase = alloc(SM_N)

        def sm(n):
            o = smo[0]
            smo[0] += n
            assert smo[0] <= SM_N, f"smalls overflow {smo[0]} > {SM_N}"
            return SMbase[:, o:o + n]

        ident = sm(128)
        ones16 = sm(16).bitcast(F16)
        onesrow16 = sm(64).bitcast(F16)
        state_col = sm(32)
        rowA = sm(512)
        rowB = sm(512)
        rowC = sm(512)
        r16 = sm(512).bitcast(F16)
        Mx = sm(512)
        st_base = smo[0]
        st16a = [sm(256).bitcast(F16) for _ in range(2)]
        st16b = [sm(256).bitcast(F16) for _ in range(2)]
        stwide = SMbase[:, st_base:st_base + 1024]
        stwideB16 = SMbase[:, st_base + 512:st_base + 1024].bitcast(F16)
        xres = [sm(512) for _ in range(3)]
        tmp_base = smo[0]
        tmp = [sm(512) for _ in range(3)]
        tmpwide = SMbase[:, tmp_base:tmp_base + 1024]
        clb_base = smo[0]
        clb = [sm(512) for _ in range(2)]
        clbwide = SMbase[:, clb_base:clb_base + 1024]
        wsc = sm(24)
        R2sbA = clb[0]

        psb = [ctx.enter_context(nc.psum_tensor(f"pb{i}", [128, 512], F32))
               for i in range(8)]

        Q = {"sync": [], "tensor": [], "vector": [], "scalar": [], "gpsimd": []}
        sems = {}

        def sem(name):
            if name not in sems:
                sems[name] = Sem(ctx.enter_context(nc.semaphore(name)), name)
            return sems[name]

        def emit(eng, fn):
            Q[eng].append(fn)

        def wof(wl):
            return [(s, v) for (s, v) in wl if s is not None]

        def barrier(eng, wl):
            wl = wof(wl)
            if not wl:
                return

            def fn(E):
                for s, v in wl:
                    E.wait_ge(s.h, v)
            emit(eng, fn)

        def dma(eng, dst, src, s, waits=()):
            wl = wof(waits)

            def fn(E):
                for ss, v in wl:
                    E.wait_ge(ss.h, v)
                E.dma_start(out=dst, in_=src).then_inc(s.h, 16)
            emit(eng, fn)
            s.n += 16
            return (s, s.n)

        def op(eng, build, waits=(), inc=None):
            wl = wof(waits)
            if inc is not None:
                inc.n += 1

            def fn(E):
                for s, v in wl:
                    E.wait_ge(s.h, v)
                ins = build(E)
                if inc is not None:
                    ins.then_inc(inc.h, 1)
            emit(eng, fn)
            return (inc, inc.n) if inc is not None else None

        rW = [sem(f"rW{i}") for i in range(2)]
        rP = [sem(f"rP{i}") for i in range(2)]
        rB = [sem(f"rB{i}") for i in range(10)]
        rX = [sem(f"rX{i}") for i in range(2)]
        rZ = [sem(f"rZ{i}") for i in range(3)]
        rH = [sem(f"rH{i}") for i in range(8)]
        rD = [sem(f"rD{i}") for i in range(2)]
        rE = [sem(f"rE{i}") for i in range(8)]
        rF = [sem(f"rF{i}") for i in range(4)]
        sID = sem("sID")
        sXQ = sem("sXQ")
        stXQ4 = [sem(f"stXQ4{i}") for i in range(4)]
        stWF = sem("stWF")
        rWF = [sem(f"rWF{i}") for i in range(2)]
        stX1 = sem("stX1")
        stXQ = sem("stXQ")
        stH = sem("stH")
        stHQ = sem("stHQ")
        stWDQ = sem("stWDQ")
        stY1 = sem("stY1")
        stOUT = sem("stOUT")
        cGP = sem("cGP")
        cDVE = sem("cDVE")
        cSCA = sem("cSCA")
        cPE = sem("cPE")
        ESEM = {"vector": cDVE, "scalar": cSCA, "gpsimd": cGP}

        wring = Ring(2)
        bring = Ring(10)
        xring = Ring(2)
        zring = Ring(3)
        dring = Ring(2)
        fring = Ring(4)

        def cast_slice(eng, dst, src, waits=(), inc=None):
            if eng == "scalar":
                return op(eng, lambda E: E.activation(out=dst, in_=src,
                                                      func=ACT.Copy), waits, inc)
            return op(eng, lambda E: E.tensor_copy(out=dst, in_=src), waits, inc)

        A0_16 = A0.bitcast(F16)
        A1_16 = A1.bitcast(F16)
        A2_16 = A2.bitcast(F16)
        A3_16 = A3.bitcast(F16)
        A4_16 = A4.bitcast(F16)

        def x16v(k, c0, n):
            return A0_16[:, k * 1024 + c0:k * 1024 + c0 + n]

        def bAv(k, c0, n):
            return A1_16[:, k * 1024 + c0:k * 1024 + c0 + n]

        def bBv(k, c0, n):
            return A2_16[:, k * 1024 + c0:k * 1024 + c0 + n]

        def bCv(k, c0, n):
            return A3_16[:, k * 1024 + c0:k * 1024 + c0 + n]

        def wstage(slot):
            return A5[:, slot * 1024:(slot + 1) * 1024]

        prescan_units = []
        acc_init = [False, False, False]
        acc_done_tok = [None, None, None]

        def queue_prescan_qmajor(dram, nq, ai):
            for q in range(nq):
                for kg in range(NKH // 2):
                    src = dram[q, kg * 2:(kg + 1) * 2, :, :].rearrange(
                        "k p c -> p k c")
                    prescan_units.append((src, ai))

        def queue_prescan_down(ai):
            for k in range(NKI):
                for hh in range(2):
                    src = fd[k * 128:(k + 1) * 128, hh * 1024:(hh + 1) * 1024]
                    prescan_units.append((src, ai))

        def pump_prescan(n):
            for _ in range(n):
                if not prescan_units:
                    return
                src, ai = prescan_units.pop(0)
                slot, tok = wring.acquire()
                dst = wstage(slot)
                if len(src.shape) == 3:
                    dst = dst.rearrange("p (k c) -> p k c", k=2)
                t = dma("sync" if slot == 0 else "scalar", dst, src, rP[slot],
                        waits=[tok] if tok else [])
                acc = wsc[:, ai:ai + 1]
                part = wsc[:, 3 + ai:4 + ai]
                stg = wstage(slot)
                ptok = acc_done_tok[ai]
                rtok = op("scalar", lambda E, s_=stg, p=part: E.activation(
                    out=s_, in_=s_, func=ACT.Abs, accum_out=p),
                    waits=[t] + ([ptok] if ptok else []), inc=cSCA)
                if not acc_init[ai]:
                    acc_init[ai] = True
                    rel = op("gpsimd", lambda E, a=acc, p=part: E.tensor_copy(
                        out=a, in_=p), waits=[rtok], inc=cGP)
                else:
                    rel = op("gpsimd", lambda E, a=acc, p=part: E.tensor_tensor(
                        out=a, in0=a, in1=p, op=ALU.add), waits=[rtok],
                        inc=cGP)
                wring.release(slot, (cSCA, rtok[1]))
                acc_done_tok[ai] = rel

        idtok = dma("sync", ident, ident_in, sID)
        op("vector", lambda E: E.memset(ones16[:, 0:1], 1.0))
        op("vector", lambda E: E.memset(onesrow16[0:1, 0:128], 1.0))
        setup_tok = op("vector", lambda E: E.memset(state_col, 0.0), inc=cDVE)
        barrier("tensor", [idtok, setup_tok])
        barrier("vector", [setup_tok])

        queue_prescan_qmajor(fg, 16, 0)
        queue_prescan_qmajor(fu, 16, 1)
        queue_prescan_down(2)

        tct_state = {"prev_copy": None}

        def transpose_col_to_row(col_ap, bank, dst_row, waits=()):
            wl = list(waits)
            if tct_state["prev_copy"] is not None:
                wl.append(tct_state["prev_copy"])
            g = op("tensor", lambda E: E.transpose(
                psb[bank][0:1, 0:128], col_ap, ident), waits=wl, inc=cPE)
            v = op("vector", lambda E: E.tensor_copy(
                out=dst_row, in_=psb[bank][0:1, 0:128]), waits=[g], inc=cDVE)
            tct_state["prev_copy"] = v
            return v

        pair_tok = [None]

        def hilo_bcast(row_f32, n, bank, waits=(), bank_guard=None):
            hi = r16[0:1, 0:n]
            lo = r16[0:1, 512:512 + n]
            wl = list(waits)
            if pair_tok[0] is not None:
                wl.append(pair_tok[0])
            c1 = op("vector", lambda E: E.tensor_copy(out=hi, in_=row_f32),
                    waits=wl, inc=cDVE)
            c2 = op("vector", lambda E: E.tensor_tensor(
                out=lo, in0=row_f32, in1=hi, op=ALU.subtract), waits=[c1],
                inc=cDVE)
            mw = [c2]
            if bank_guard is not None:
                mw.append(bank_guard)

            def mm(E):
                E.matmul(psb[bank][:, 0:n], onesrow16[0:1, 0:128], hi,
                         start=True, stop=False)
                return E.matmul(psb[bank][:, 0:n], onesrow16[0:1, 0:128], lo,
                                start=False, stop=True)
            g = op("tensor", mm, waits=mw, inc=cPE)
            pair_tok[0] = g
            return psb[bank][:, 0:n], g

        tile_guard = {"x16": None, "bufA": None, "bufB": None, "bufC": None,
                      "st16": None}

        def emit_x_load(tile_t0, width):
            last = None
            for k in range(NKH):
                for s in range(width // SUB):
                    slot, tok = xring.acquire()
                    stage = tmp[slot]
                    wl = [tok] if tok else []
                    t = dma("sync", stage,
                            xin[k * 128:(k + 1) * 128,
                                tile_t0 + s * SUB:tile_t0 + (s + 1) * SUB],
                            rX[slot], waits=wl)
                    wts = [t]
                    if k == 0 and s == 0 and tile_guard["x16"] is not None:
                        wts += tile_guard["x16"]
                    r = cast_slice("gpsimd", x16v(k, s * SUB, SUB), stage,
                                   waits=wts, inc=cGP)
                    xring.release(slot, r)
                    last = r
            return last

        cast_rr = ["vector", "scalar", "vector", "gpsimd"]
        rrci = [0]

        def stream_quarter(dram_q, dest_slice_fn, extra_guard=None,
                           cache_dst=None, dest_full=None):
            eng = cast_rr[rrci[0] % len(cast_rr)]
            rrci[0] += 1
            done = None
            for kg in range(8):
                slot, tok = wring.acquire()
                wl = [tok] if tok else []
                t = dma("sync" if kg % 2 == 0 else "scalar",
                        wstage(slot).rearrange("p (k c) -> p k c", k=2),
                        dram_q[kg * 2:(kg + 1) * 2, :, :].rearrange(
                            "k p c -> p k c"),
                        rW[slot], waits=wl)
                last = None
                for j in range(2):
                    k = kg * 2 + j
                    wts = [t] if j == 0 else []
                    if kg == 0 and j == 0 and extra_guard:
                        wts = wts + list(extra_guard)
                    src = A5[:, slot * 1024 + j * 512:slot * 1024 + (j + 1) * 512]
                    last = cast_slice(eng, dest_slice_fn(k), src, waits=wts,
                                      inc=ESEM[eng])
                wring.release(slot, last)
                done = last
            if cache_dst is not None:
                ck = dma("gpsimd", cache_dst, dest_full, stWF, waits=[done])
                return done, ck
            return done, None

        def load_quarter_f16(cache_src, dest_full, extra_guard=None, qi=0):
            wl = [(stWF, stWF.n)]
            if extra_guard:
                wl = wl + list(extra_guard)
            return dma("sync" if qi % 2 == 0 else "scalar", dest_full,
                       cache_src, rWF[qi % 2], waits=wl)

        def load_quarter_f16_split(cache_src, dest_full, extra_guard=None,
                                   qi=0):
            wl = [(stWF, stWF.n)]
            if extra_guard:
                wl = wl + list(extra_guard)
            hw = 8 * 512
            qname = "sync" if qi % 2 == 0 else "scalar"
            lo = dma(qname, dest_full[:, 0:hw], cache_src[:, 0:hw],
                     rWF[qi % 2], waits=wl)
            hi = dma(qname, dest_full[:, hw:2 * hw], cache_src[:, hw:2 * hw],
                     rWF[qi % 2], waits=[])
            return lo, hi

        proj_pass = {}
        prev_ck = {0: None, 1: None}
        wq_guard_g = {0: None, 1: None}

        def emit_proj(wname, width, evac, moving_v, prewait=None):
            ns = width // SUB
            dram = wap[wname]
            pidx = wnames.index(wname)
            first = wname not in proj_pass
            proj_pass[wname] = True
            free_tok = {0: None, 4: None}
            wq_guard = [wq_guard_g[0], wq_guard_g[1]]
            gi = [0]
            last_group = [None]
            for q in range(4):
                buf = q % 2

                def wsl(k, b=buf):
                    return A4_16[:, (b * 16 + k) * 512:(b * 16 + k + 1) * 512]

                def wfull(b=buf):
                    return A4_16[:, (b * 16) * 512:(b * 16 + 16) * 512]
                eg = list(wq_guard[buf] or [])
                if prev_ck[buf] is not None:
                    eg.append(prev_ck[buf])
                    prev_ck[buf] = None
                if first:
                    cd, ck = stream_quarter(
                        dram[q], wsl, extra_guard=eg,
                        cache_dst=wf16s[pidx, q], dest_full=wfull())
                else:
                    cd = load_quarter_f16(wf16s[pidx, q], wfull(),
                                          extra_guard=eg, qi=q)
                    ck = None
                pump_prescan(6)
                for s in range(ns):
                    g = gi[0]
                    gi[0] += 1
                    bank0 = (g % 2) * 4
                    wts = []
                    if s == 0:
                        wts.append(cd)
                        if prewait is not None:
                            wts.append(prewait)
                            prewait = None
                    if free_tok[bank0] is not None:
                        wts.append(free_tok[bank0])

                    def mmg(E, s_=s, b0=bank0, wsl_=wsl):
                        mm = None
                        for k in range(NKH):
                            for m in range(4):
                                mm = E.matmul(psb[b0 + m][:],
                                              wsl_(k)[:, m * 128:(m + 1) * 128],
                                              moving_v(k, s_ * SUB, SUB),
                                              start=(k == 0), stop=(k == NKH - 1))
                        return mm
                    gt = op("tensor", mmg, waits=wts, inc=cPE)
                    free_tok[bank0] = evac(q, s, bank0, gt)
                    last_group[0] = gt
                wq_guard[buf] = [last_group[0]] + ([ck] if ck else [])
                wq_guard_g[buf] = wq_guard[buf]
                if ck is not None:
                    prev_ck[buf] = ck
            return last_group[0]

        def emit_ln_rows(sum_bank, sq_bank, st_done):
            mean = rowA[0:1, :]
            var = rowB[0:1, :]
            istd = rowC[0:1, :]
            t1 = op("vector", lambda E: E.tensor_scalar(
                out=mean, in0=psb[sum_bank][0:1, :], scalar1=1.0 / H,
                scalar2=None, op0=ALU.mult), waits=[st_done], inc=cDVE)
            bank_rel = op("vector", lambda E: E.tensor_scalar(
                out=var, in0=psb[sq_bank][0:1, :], scalar1=1.0 / H,
                scalar2=None, op0=ALU.mult), inc=cDVE)
            t2 = op("vector", lambda E: E.tensor_tensor(
                out=istd, in0=mean, in1=mean, op=ALU.mult),
                waits=[t1, bank_rel], inc=cDVE)
            t3 = op("vector", lambda E: E.tensor_tensor(
                out=var, in0=var, in1=istd, op=ALU.subtract), waits=[t2],
                inc=cDVE)
            t4 = op("vector", lambda E: E.tensor_scalar(
                out=var, in0=var, scalar1=EPS, scalar2=None, op0=ALU.add),
                waits=[t3], inc=cDVE)
            rec = op("vector", lambda E: E.reciprocal(out=var, in_=var),
                     waits=[t4], inc=cDVE)
            sq = op("scalar", lambda E: E.activation(
                out=istd, in_=var, func=ACT.Sqrt), waits=[rec], inc=cSCA)
            bb = op("vector", lambda E: E.tensor_tensor(
                out=mean, in0=mean, in1=istd, op=ALU.mult), waits=[sq],
                inc=cDVE)
            return sq, bb, bank_rel

        def emit_amax_rows(M_ap, first_wait):
            last = None
            for g4 in range(4):
                bk = 4 + (g4 % 2)
                wl = [first_wait] if (g4 == 0 and first_wait) else []
                if tct_state["prev_copy"] is not None:
                    wl.append(tct_state["prev_copy"])
                tg = op("tensor", lambda E, g_=g4, b=bk, M=M_ap: E.transpose(
                    psb[b][:, 0:128], M[:, g_ * 128:(g_ + 1) * 128], ident),
                    waits=wl, inc=cPE)
                rd = op("vector", lambda E, b=bk: E.reduce_max(
                    out=wsc[:, 7:8], in_=psb[b][:, 0:128], axis=AX.X),
                    waits=[tg] + ([last] if last else []), inc=cDVE)
                last = transpose_col_to_row(
                    wsc[:, 7:8], 4 + ((g4 + 1) % 2),
                    rowB[0:1, g4 * 128:(g4 + 1) * 128], waits=[rd])
            return last

        bank_tok = {"psA": None, "psB": None, "psS": None, "psR": None}

        def emit_attn_tile(tile_t0, width, warm_only):
            ns = width // SUB
            own0 = tile_t0 - WARM
            xl = emit_x_load(tile_t0, width)
            pump_prescan(4)

            def ev_f(q, s, b0, gt):
                wts = [gt]
                if q == 0 and s == 0 and tile_guard["bufA"] is not None:
                    wts += tile_guard["bufA"]
                    tile_guard["bufA"] = None

                def fn(E):
                    a = None
                    for m in range(4):
                        a = E.activation(out=bAv(q * 4 + m, s * SUB, SUB),
                                         in_=psb[b0 + m][:], func=ACT.Tanh)
                    return a
                return op("scalar", fn, waits=wts, inc=cSCA)
            emit_proj("wf", width, ev_f, x16v, prewait=xl)
            f_done = (cSCA, cSCA.n)
            pump_prescan(4)

            def ev_i(q, s, b0, gt):
                wts = [gt]
                if q == 0 and s == 0 and tile_guard["bufB"] is not None:
                    wts += tile_guard["bufB"]
                    tile_guard["bufB"] = None

                def fn(E):
                    a = None
                    for m in range(4):
                        a = E.activation(out=bBv(q * 4 + m, s * SUB, SUB),
                                         in_=psb[b0 + m][:], func=ACT.Sigmoid)
                    return a
                return op("scalar", fn, waits=wts, inc=cSCA)
            emit_proj("wi", width, ev_i, x16v)
            pump_prescan(4)

            vtok = {}

            def ev_v(q, s, b0, gt):
                for m in range(4):
                    mi = q * 4 + m
                    ts = tmp[2]
                    wts = [gt] if m == 0 else []
                    pt = vtok.get(2)
                    if pt is not None:
                        wts.append(pt)
                    sv = op("scalar", lambda E, b=b0 + m, t=ts: E.activation(
                        out=t, in_=psb[b][:], func=ACT.Silu), waits=wts,
                        inc=cSCA)
                    dv = op("vector", lambda E, mi_=mi, t=ts, s_=s:
                            E.tensor_tensor(out=bBv(mi_, s_ * SUB, SUB), in0=t,
                                            in1=bBv(mi_, s_ * SUB, SUB),
                                            op=ALU.mult), waits=[sv], inc=cDVE)
                    vtok[2] = dv
                return (cSCA, cSCA.n)
            emit_proj("wv", width, ev_v, x16v)
            pump_prescan(2)

            barrier("vector", [f_done] + (tile_guard["bufC"] or []))
            tile_guard["bufC"] = None
            for k in range(NKH):
                sc_tok = op("vector", lambda E, k_=k: E.tensor_tensor_scan(
                    out=bCv(k_, 0, width), data0=bAv(k_, 0, width),
                    data1=bBv(k_, 0, width),
                    initial=state_col[:, 2 * k_ + 1:2 * k_ + 2],
                    op0=ALU.mult, op1=ALU.add), inc=cDVE)
                op("vector", lambda E, k_=k: E.tensor_copy(
                    out=state_col[:, 2 * k_:2 * k_ + 2],
                    in_=bCv(k_, width - 2, 2)), waits=[sc_tok], inc=cDVE)
            scan_done = (cDVE, cDVE.n)
            if debug_stop == "A" and not warm_only:
                for k in range(NKH):
                    dma("gpsimd", stdbg[k * 128:(k + 1) * 128,
                                        own0:own0 + width],
                        bCv(k, 0, width), sem("stST"), waits=[scan_done])
                    dma("gpsimd", fdbg[k * 128:(k + 1) * 128,
                                       own0:own0 + width],
                        bAv(k, 0, width), sem("stST"))
                    dma("gpsimd", gdbg[k * 128:(k + 1) * 128,
                                       own0:own0 + width],
                        bBv(k, 0, width), sem("stST"))
                barrier("scalar", [(sems["stST"], sems["stST"].n)])
            if warm_only:
                tile_guard["x16"] = [(cPE, cPE.n)]
                tile_guard["bufA"] = [scan_done]
                tile_guard["bufB"] = [scan_done]
                return

            barrier("scalar", [scan_done])
            qtok = {}

            def ev_q(q, s, b0, gt):
                for m in range(4):
                    mi = q * 4 + m
                    ts = tmp[2]
                    wts = [gt] if m == 0 else []
                    pt = qtok.get(2)
                    if pt is not None:
                        wts.append(pt)
                    dv = op("vector", lambda E, b=b0 + m, t=ts, mi_=mi, s_=s:
                            E.tensor_tensor(out=t, in0=psb[b][:],
                                            in1=bCv(mi_, s_ * SUB, SUB),
                                            op=ALU.mult), waits=wts, inc=cDVE)
                    sc = op("scalar", lambda E, t=ts, mi_=mi, s_=s: E.activation(
                        out=bAv(mi_, s_ * SUB, SUB), in_=t, func=ACT.Silu),
                        waits=[dv], inc=cSCA)
                    qtok[2] = sc
                return (cDVE, cDVE.n)
            emit_proj("wq_", width, ev_q, x16v)
            pump_prescan(4)

            def ev_g(q, s, b0, gt):
                def fn(E):
                    a = None
                    for m in range(4):
                        a = E.activation(out=bBv(q * 4 + m, s * SUB, SUB),
                                         in_=psb[b0 + m][:], func=ACT.Silu)
                    return a
                return op("scalar", fn, waits=[gt], inc=cSCA)
            emit_proj("wgt", width, ev_g, x16v)
            silg_done = (cSCA, cSCA.n)
            pump_prescan(4)

            barrier("vector", [silg_done])

            def ev_u(q, s, b0, gt):
                last = None
                for m in range(4):
                    mi = q * 4 + m
                    last = op("vector", lambda E, b=b0 + m, mi_=mi, s_=s:
                              E.tensor_tensor(out=bCv(mi_, s_ * SUB, SUB),
                                              in0=psb[b][:],
                                              in1=bBv(mi_, s_ * SUB, SUB),
                                              op=ALU.mult),
                              waits=[gt] if m == 0 else [], inc=cDVE)
                return last
            emit_proj("wup", width, ev_u, x16v)
            pump_prescan(2)

            if debug_stop == "A":
                pk = (cDVE, cDVE.n)
                sk = (cSCA, cSCA.n)
                for k in range(NKH):
                    dma("gpsimd", rodbg[k * 128:(k + 1) * 128,
                                        own0:own0 + width],
                        bAv(k, 0, width), sem("stRO"), waits=[sk, pk])
                    dma("gpsimd", pdbg[k * 128:(k + 1) * 128,
                                       own0:own0 + width],
                        bCv(k, 0, width), sem("stRO"), waits=[])

            def xrv(m, s):
                base = A0 if m < 8 else A2
                mi = m % 8
                return base[:, mi * 1024 + s * SUB:mi * 1024 + s * SUB + SUB]

            def wsl_ro(k):
                return A4_16[:, k * 512:(k + 1) * 512]

            def wsl_dn(k):
                return A4_16[:, (16 + k) * 512:(16 + k + 1) * 512]

            free_tok = {0: None, 4: None}
            gi = 0
            xr_ev = {}
            bufB_guard_pending = True
            op_first = "wro" not in proj_pass
            proj_pass["wro"] = True
            oc_tok = [None]

            for q in range(4):
                extra_t = [t_ for t_ in oc_tok if t_] + \
                    [t_ for t_ in (prev_ck[0], prev_ck[1]) if t_]
                prev_ck[0] = prev_ck[1] = None
                if q == 0:
                    guard1 = list(wq_guard_g[0] or []) + extra_t
                    guard2 = list(wq_guard_g[1] or []) + extra_t
                else:
                    guard1 = guard2 = [(cPE, cPE.n)] + extra_t
                guard = guard1
                if op_first:
                    cd1, ck1 = stream_quarter(
                        wap["wro"][q], wsl_ro, extra_guard=guard,
                        cache_dst=wf16s[4, q],
                        dest_full=A4_16[:, 0:16 * 512])
                    cd2, ck2 = stream_quarter(
                        wap["wdn"][q], wsl_dn, extra_guard=None,
                        cache_dst=wf16s[7, q],
                        dest_full=A4_16[:, 16 * 512:32 * 512])
                    oc_tok[:] = [ck1, ck2]
                    prev_ck[0] = ck1
                    prev_ck[1] = ck2
                    cd1a = cd1b = cd1
                    cd2a = cd2b = cd2
                else:
                    cd1a, cd1b = load_quarter_f16_split(
                        wf16s[4, q], A4_16[:, 0:16 * 512],
                        extra_guard=guard1, qi=0)
                    cd2a, cd2b = load_quarter_f16_split(
                        wf16s[7, q], A4_16[:, 16 * 512:32 * 512],
                        extra_guard=guard2, qi=1)
                    oc_tok[:] = [None]
                pump_prescan(6)
                for s in range(ns):
                    bank0 = (gi % 2) * 4
                    gi += 1
                    wts = [cd1a, cd2a] if s == 0 else []
                    if free_tok[bank0] is not None:
                        wts.append(free_tok[bank0])

                    def mmg_lo(E, s_=s, b0=bank0):
                        mm = None
                        for k in range(NKH // 2):
                            for m in range(4):
                                mm = E.matmul(psb[b0 + m][:],
                                              wsl_ro(k)[:, m * 128:(m + 1) * 128],
                                              bAv(k, s_ * SUB, SUB),
                                              start=(k == 0), stop=False)
                        for k in range(NKH // 2):
                            for m in range(4):
                                mm = E.matmul(psb[b0 + m][:],
                                              wsl_dn(k)[:, m * 128:(m + 1) * 128],
                                              bCv(k, s_ * SUB, SUB),
                                              start=False, stop=False)
                        return mm

                    def mmg_hi(E, s_=s, b0=bank0):
                        mm = None
                        for k in range(NKH // 2, NKH):
                            for m in range(4):
                                mm = E.matmul(psb[b0 + m][:],
                                              wsl_ro(k)[:, m * 128:(m + 1) * 128],
                                              bAv(k, s_ * SUB, SUB),
                                              start=False, stop=False)
                        for k in range(NKH // 2, NKH):
                            for m in range(4):
                                mm = E.matmul(psb[b0 + m][:],
                                              wsl_dn(k)[:, m * 128:(m + 1) * 128],
                                              bCv(k, s_ * SUB, SUB),
                                              start=False, stop=(k == NKH - 1))
                        return mm
                    op("tensor", mmg_lo, waits=wts, inc=cPE)
                    gt = op("tensor", mmg_hi,
                            waits=([cd1b, cd2b] if s == 0 else []), inc=cPE)
                    last = None
                    for m in range(4):
                        mi = q * 4 + m
                        slot, tok = zring.acquire()
                        wl = [tok] if tok else []
                        t = dma("sync", xres[slot],
                                xin[mi * 128:(mi + 1) * 128,
                                    tile_t0 + s * SUB:tile_t0 + (s + 1) * SUB],
                                rZ[slot], waits=wl)
                        wts2 = [t, gt] if m == 0 else [t]
                        if mi >= 8 and bufB_guard_pending:
                            bufB_guard_pending = False
                        last = op("vector", lambda E, b=bank0 + m, mi_=mi, s_=s,
                                  xs_=xres[slot]: E.tensor_tensor(
                                      out=xrv(mi_, s_), in0=psb[b][:], in1=xs_,
                                      op=ALU.add), waits=wts2, inc=cDVE)
                        zring.release(slot, last)
                        xr_ev[(mi, s)] = last
                    free_tok[bank0] = last
            wq_guard_g[0] = [gt]
            wq_guard_g[1] = [gt]

            xr_evac_done = (cDVE, cDVE.n)
            st_guard = tile_guard["st16"]
            tile_guard["st16"] = None
            if debug_stop in ("A", "X"):
                xr_all = (cDVE, cDVE.n)
                for m in range(NKH):
                    for s in range(ns):
                        dma("gpsimd",
                            xrdbg[m * 128:(m + 1) * 128,
                                  own0 + s * SUB:own0 + (s + 1) * SUB],
                            xrv(m, s), sem("stXR"), waits=[xr_all])
            for s in range(ns):
                pe0 = cPE.n
                for m in range(NKH):
                    t16 = st16a[m % 2]
                    q16 = st16b[m % 2]
                    wts = [(cPE, pe0 + m - 1)] if m >= 2 else []
                    if m < 2 and st_guard is not None:
                        wts = wts + st_guard
                    ev = xr_ev.get((m, s))
                    if ev is not None:
                        wts = wts + [ev]
                    c1 = cast_slice("gpsimd", t16[:, 0:SUB], xrv(m, s),
                                    waits=wts, inc=cGP)
                    c2 = op("scalar", lambda E, qq=q16, src=xrv(m, s):
                            E.activation(out=qq[:, 0:SUB], in_=src,
                                         func=ACT.Square), waits=wts, inc=cSCA)
                    mw = [c1, c2]
                    if m == 0:
                        mw.append(xr_evac_done)
                        if bank_tok.get("stats") is not None:
                            mw.append(bank_tok["stats"])

                    def smm(E, m_=m, t=t16, qq=q16):
                        E.matmul(psb[0][0:1, :], ones16[:, 0:1], t[:, 0:SUB],
                                 start=(m_ == 0), stop=(m_ == NKH - 1))
                        return E.matmul(psb[1][0:1, :], ones16[:, 0:1],
                                        qq[:, 0:SUB],
                                        start=(m_ == 0), stop=(m_ == NKH - 1))
                    op("tensor", smm, waits=mw, inc=cPE)
                st_done = (cPE, cPE.n)
                sq, bb, bank_rel = emit_ln_rows(0, 1, st_done)
                bank_tok["stats"] = bank_rel
                psA, gA = hilo_bcast(rowC[0:1, :], SUB, 2, waits=[sq],
                                     bank_guard=bank_tok["psA"])
                psB, gB = hilo_bcast(rowA[0:1, :], SUB, 3, waits=[bb],
                                     bank_guard=bank_tok["psB"])
                ap_done = None
                mx_done = None
                for m in range(NKH):
                    wts = [gA, gB] if m == 0 else []
                    if debug_stop == "A" and m == 0:
                        wts = wts + [(sems["stXR"], sems["stXR"].n)]
                    if debug_stop == "X" and m == 0:
                        wts = wts + [(sems["stXR"], sems["stXR"].n)]
                    op("vector", lambda E, m_=m, s_=s: E.tensor_tensor(
                        out=xrv(m_, s_), in0=xrv(m_, s_), in1=psA, op=ALU.mult),
                        waits=wts)
                    ap_done = op("vector", lambda E, m_=m, s_=s: E.tensor_tensor(
                        out=xrv(m_, s_), in0=xrv(m_, s_), in1=psB,
                        op=ALU.subtract), inc=cDVE)
                    dma("gpsimd", x1s[m * 128:(m + 1) * 128,
                                      own0 + s * SUB:own0 + (s + 1) * SUB],
                        xrv(m, s), stX1, waits=[ap_done])
                    mwts = [ap_done]
                    if m == 0:
                        mwts.append((cPE, cPE.n))
                        ab = op("scalar", lambda E, s_=s: E.activation(
                            out=Mx, in_=xrv(0, s_), func=ACT.Abs),
                            waits=mwts, inc=cSCA)
                        mx_done = ab
                    else:
                        ab = op("scalar", lambda E, m_=m, s_=s: E.activation(
                            out=clb[1], in_=xrv(m_, s_), func=ACT.Abs),
                            waits=mwts + [mx_done], inc=cSCA)
                        mx_done = op("vector", lambda E: E.tensor_max(
                            out=Mx, in0=Mx, in1=clb[1]), waits=[ab], inc=cDVE)
                bank_tok["psA"] = ap_done
                bank_tok["psB"] = ap_done
                am = emit_amax_rows(Mx, mx_done)
                srow = rowB[0:1, :]
                rrow = rowC[0:1, :]
                ts_ = op("vector", lambda E: E.tensor_scalar(
                    out=srow, in0=srow, scalar1=1.0 / 127.0, scalar2=EPS,
                    op0=ALU.mult, op1=ALU.add), waits=[am], inc=cDVE)
                rr = op("vector", lambda E: E.reciprocal(out=rrow, in_=srow),
                        waits=[ts_], inc=cDVE)
                psS, gS = hilo_bcast(srow, SUB, 6, waits=[rr],
                                     bank_guard=bank_tok["psS"])
                evS = op("vector", lambda E, s_=s: E.tensor_copy(
                    out=A6[:, own0 + s_ * SUB:own0 + (s_ + 1) * SUB], in_=psS),
                    waits=[gS], inc=cDVE)
                bank_tok["psS"] = evS
                psR, gR = hilo_bcast(rrow, SUB, 7, waits=[rr],
                                     bank_guard=bank_tok["psR"])
                evR = op("vector", lambda E: E.tensor_copy(out=R2sbA, in_=psR),
                         waits=[gR, (cGP, cGP.n)], inc=cDVE)
                bank_tok["psR"] = evR
                for m in range(NKH):
                    slot, tok = xring.acquire()
                    ts = tmp[slot]
                    wl = [evR] if m == 0 else []
                    if tok is not None:
                        wl.append(tok)
                    mq = op("gpsimd", lambda E, m_=m, s_=s, t=ts: E.tensor_tensor(
                        out=t, in0=xrv(m_, s_), in1=R2sbA, op=ALU.mult),
                        waits=wl, inc=cGP)
                    cv = op("vector", lambda E, t=ts: E.tensor_copy(
                        out=t.bitcast(I32), in_=t), waits=[mq], inc=cDVE)
                    xq16 = (st16a + st16b)[m % 4]
                    wl2 = [cv]
                    if m >= 4 or s > 0 or tile_t0 > WARM:
                        wl2.append((stXQ4[m % 4], stXQ4[m % 4].n))
                    x2c = op("vector", lambda E, t=ts, d=xq16: E.tensor_copy(
                        out=d[:, 0:SUB], in_=t.bitcast(I32)), waits=wl2,
                        inc=cDVE)
                    dma("gpsimd",
                        xqs[m * 128:(m + 1) * 128,
                            own0 + s * SUB:own0 + (s + 1) * SUB],
                        xq16[:, 0:SUB], stXQ4[m % 4], waits=[x2c])
                    xring.release(slot, x2c)
                st_guard = [(s_, s_.n) for s_ in stXQ4]
            tile_guard["x16"] = [(cDVE, cDVE.n), (stX1, stX1.n),
                                 (cGP, cGP.n)] + [(s_, s_.n) for s_ in stXQ4]
            tile_guard["bufA"] = None
            tile_guard["bufB"] = [(cDVE, cDVE.n), (stX1, stX1.n),
                                  (cGP, cGP.n)] + [(s_, s_.n) for s_ in stXQ4]
            tile_guard["bufC"] = [(cPE, cPE.n)]
            tile_guard["st16"] = [(s_, s_.n) for s_ in stXQ4]

        emit_attn_tile(0, WARM, warm_only=True)
        t0 = WARM
        while t0 < WARM + T_OWN:
            w = min(1024, WARM + T_OWN - t0)
            emit_attn_tile(t0, w, warm_only=False)
            t0 += w

        if debug_stop == "A":
            barrier("gpsimd", [(stX1, stX1.n)] + [(s_, s_.n) for s_ in stXQ4])
            barrier("sync", [(stX1, stX1.n)] + [(s_, s_.n) for s_ in stXQ4])
            with nc.Block() as block:
                @block.sync
                def _(E):
                    for fn in Q["sync"]:
                        fn(E)

                @block.tensor
                def _(E):
                    for fn in Q["tensor"]:
                        fn(E)

                @block.vector
                def _(E):
                    for fn in Q["vector"]:
                        fn(E)

                @block.scalar
                def _(E):
                    for fn in Q["scalar"]:
                        fn(E)

                @block.gpsimd
                def _(E):
                    for fn in Q["gpsimd"]:
                        fn(E)
            return nc

        def xqv(k, c0, n):
            base = A0_16 if k < 8 else A1_16
            kk = k % 8
            return base[:, kk * T_OWN + c0:kk * T_OWN + c0 + n]

        xq_guard = tile_guard["x16"] + [(cSCA, cSCA.n), (cPE, cPE.n)]
        xq_sp = [(s_, s_.n) for s_ in stXQ4]
        for k in range(NKH):
            dma("scalar", xqv(k, 0, T_OWN), xqs[k * 128:(k + 1) * 128, :],
                sXQ, waits=(xq_guard if k == 0 else []) + xq_sp)
        xq_ready = (sXQ, sXQ.n)

        pump_prescan(len(prescan_units))

        def finalize_ws(ai, ci):
            acc = wsc[:, ai:ai + 1]
            row = rowA[0:1, 0:128]
            tv = transpose_col_to_row(acc, 4, row, waits=[acc_done_tok[ai]])
            sm_ = op("vector", lambda E: E.reduce_sum(
                out=rowB[0:1, 0:1], in_=row, axis=AX.X), waits=[tv], inc=cDVE)
            sc_ = op("vector", lambda E: E.tensor_scalar(
                out=rowB[0:1, 0:1], in0=rowB[0:1, 0:1], scalar1=1.0 / NW_ELEM,
                scalar2=EPS, op0=ALU.mult, op1=ALU.add), waits=[sm_], inc=cDVE)
            ps, g = hilo_bcast(rowB[0:1, 0:1], 1, 5, waits=[sc_])
            cp_ = op("vector", lambda E: E.tensor_copy(
                out=wsc[:, ci:ci + 1], in_=ps[:, 0:1]), waits=[g], inc=cDVE)
            hf_ = op("vector", lambda E: E.tensor_scalar(
                out=wsc[:, ci + 1:ci + 2], in0=wsc[:, ci:ci + 1], scalar1=0.5,
                scalar2=None, op0=ALU.mult), waits=[cp_], inc=cDVE)
            return op("vector", lambda E: E.tensor_scalar(
                out=wsc[:, ci + 2:ci + 3], in0=wsc[:, ci:ci + 1], scalar1=-0.5,
                scalar2=None, op0=ALU.mult), waits=[hf_], inc=cDVE)

        wsg_done = finalize_ws(0, 8)
        wsu_done = finalize_ws(1, 11)
        wsd_done = finalize_ws(2, 14)
        S32 = A6
        b1_guard = [(cPE, cPE.n), (cDVE, cDVE.n), (cSCA, cSCA.n),
                    (cGP, cGP.n), (stX1, stX1.n), (stWF, stWF.n)] + \
            [(s_, s_.n) for s_ in stXQ4]

        def Mh_sub(s):
            return xres[s] if s < 3 else Mx

        clb_tok = [(cGP, cGP.n), None]
        hq_guard = [list(b1_guard), list(b1_guard)]
        wq_dest_guard = [None]
        wqg_guard = None
        bring.tok[0] = wring.tok[0]
        bring.tok[1] = wring.tok[1]
        for i_ in range(2, 10):
            bring.tok[i_] = (cPE, cPE.n)

        def bslot(i):
            if i < 2:
                return A5[:, i * 1024:(i + 1) * 1024]
            if i < 6:
                return A3[:, 4096 + (i - 2) * 1024:4096 + (i - 1) * 1024]
            return A4[:, 4096 + (i - 6) * 1024:4096 + (i - 5) * 1024]

        def emit_quant_unit(dram_slice, dest_ap, ci, extra, sgn_extra=None,
                            cache=None):
            sA, tokA = bring.acquire()
            sB, tokB = bring.acquire()
            wl = [tokA] if tokA else []
            dst = bslot(sA)
            if len(dram_slice.shape) == 3:
                dst = dst.rearrange("p (k c) -> p k c", k=dram_slice.shape[1])
            t = dma("sync" if sA % 2 == 0 else "scalar", dst, dram_slice,
                    rB[sA], waits=wl)
            wts = [t] + (extra or [])
            if tokB:
                wts.append(tokB)
            cl = op("vector", lambda E, a=sA, b=sB, ci_=ci: E.tensor_scalar(
                out=bslot(b), in0=bslot(a),
                scalar1=wsc[:, ci_ + 1:ci_ + 2],
                scalar2=wsc[:, ci_ + 2:ci_ + 3],
                op0=ALU.min, op1=ALU.max), waits=wts, inc=cDVE)
            sb = op("gpsimd", lambda E, a=sA, b=sB: E.tensor_tensor(
                out=bslot(a), in0=bslot(a), in1=bslot(b), op=ALU.subtract),
                waits=[cl], inc=cGP)
            bring.release(sB, sb)
            swl = [sb] + (sgn_extra or [])
            sg_ = op("scalar", lambda E, a=sA, d=dest_ap: E.activation(
                out=d, in_=bslot(a), func=ACT.Sign), waits=swl, inc=cSCA)
            if cache is not None:
                ct = dma("gpsimd", cache, dest_ap, stWDQ, waits=[sg_])
                bring.release(sA, sg_)
                return sg_, ct
            bring.release(sA, sg_)
            return sg_, None

        wdq_units = [(k, hh) for k in range(NKI) for hh in range(2)]
        wdq_state = {"sgtok": None, "first": True}

        def pump_wdquant(n):
            for _ in range(n):
                if not wdq_units:
                    return
                k, hh = wdq_units.pop(0)
                extra = []
                if wdq_state["first"]:
                    wdq_state["first"] = False
                    extra.append((wsd_done[0], wsd_done[1]))
                sgn_extra = []
                if wdq_state["sgtok"] is not None:
                    sgn_extra.append(wdq_state["sgtok"])
                sg_, ct = emit_quant_unit(
                    fd[k * 128:(k + 1) * 128, hh * 1024:(hh + 1) * 1024],
                    stwideB16, 14, extra, sgn_extra=sgn_extra,
                    cache=wdqs[k * 128:(k + 1) * 128,
                               hh * 1024:(hh + 1) * 1024])
                wdq_state["sgtok"] = ct
        mh_started = [False] * NS_T
        gate_free = None
        up_free = None
        h_tmp_tok = {}
        habs_tok = {}
        pend_maxes = []
        for qq in range(16):
            buf = qq % 2
            hqt = A2_16[:, buf * (4 * T_OWN):(buf + 1) * (4 * T_OWN)]

            def hv(jm, c0, n, b=hqt):
                return b[:, jm * T_OWN + c0:jm * T_OWN + c0 + n]

            def quant_quarter(dram_q, dest16, ci, first_extra, guard):
                done = None
                for kg in range(8):
                    extra = list(first_extra) if (kg == 0 and first_extra) \
                        else []
                    sgn_extra = [guard] if (kg == 0 and guard) else []
                    done, _ = emit_quant_unit(
                        dram_q[kg * 2:(kg + 1) * 2, :, :].rearrange(
                            "k p c -> p k c"),
                        dest16(kg * 2, wide=True), ci, extra,
                        sgn_extra=sgn_extra)
                return done

            def wg16(k, wide=False):
                n = 1024 if wide else 512
                return A3_16[:, k * 512:k * 512 + n]

            def wu16(k, wide=False):
                n = 1024 if wide else 512
                return A4_16[:, k * 512:k * 512 + n]

            fe = []
            if qq == 0:
                fe = [wsg_done] + b1_guard
            cdg = quant_quarter(fg[qq], wg16, 8, fe, wq_dest_guard[0])
            cdu = quant_quarter(fu[qq], wu16, 11,
                                ([wsu_done] + b1_guard) if qq == 0 else None,
                                wq_dest_guard[0])

            for s in range(NS_T):
                wts = [cdg, cdu] if s == 0 else []
                if qq == 0 and s == 0:
                    wts.append(xq_ready)
                if gate_free is not None:
                    wts.append(gate_free)

                def mmg_gate(E, s_=s, w_=wg16):
                    mm = None
                    for k in range(NKH):
                        for m in range(4):
                            mm = E.matmul(psb[m][:],
                                          w_(k)[:, m * 128:(m + 1) * 128],
                                          xqv(k, s_ * SUB, SUB),
                                          start=(k == 0), stop=(k == NKH - 1))
                    return mm
                gg = op("tensor", mmg_gate, waits=wts, inc=cPE)
                wts = [up_free] if up_free is not None else []

                def mmg_up(E, s_=s, w_=wu16):
                    mm = None
                    for k in range(NKH):
                        for m in range(4):
                            mm = E.matmul(psb[4 + m][:],
                                          w_(k)[:, m * 128:(m + 1) * 128],
                                          xqv(k, s_ * SUB, SUB),
                                          start=(k == 0), stop=(k == NKH - 1))
                    return mm
                gu = op("tensor", mmg_up, waits=wts, inc=cPE)

                hg = hq_guard[buf] if s == 0 else None
                for jm in range(4):
                    tg = tmp[jm % 2]
                    tu = tmp[2]
                    wts = [gg] if jm == 0 else []
                    pt = h_tmp_tok.get(jm % 2)
                    if pt is not None:
                        wts.append(pt)
                    d1 = op("vector", lambda E, b=jm, t=tg, s_=s:
                            E.scalar_tensor_tensor(
                                out=t, in0=psb[b][:], scalar=wsc[:, 8:9],
                                in1=S32[:, s_ * SUB:(s_ + 1) * SUB],
                                op0=ALU.mult, op1=ALU.mult),
                            waits=wts, inc=cDVE)
                    sl = op("scalar", lambda E, t=tg: E.activation(
                        out=t, in_=t, func=ACT.Silu), waits=[d1], inc=cSCA)
                    wts = [gu] if jm == 0 else []
                    pt = h_tmp_tok.get(2)
                    if pt is not None:
                        wts.append(pt)
                    d2 = op("vector", lambda E, b=4 + jm, t=tu, s_=s:
                            E.scalar_tensor_tensor(
                                out=t, in0=psb[b][:], scalar=wsc[:, 11:12],
                                in1=S32[:, s_ * SUB:(s_ + 1) * SUB],
                                op0=ALU.mult, op1=ALU.mult),
                            waits=wts, inc=cDVE)
                    wts = [sl, d2]
                    if hg is not None and jm == 0:
                        wts = wts + hg
                        hg = None
                    hw = op("gpsimd", lambda E, a=tg, b=tu, jm_=jm, s_=s,
                            hv_=hv: E.tensor_tensor(
                                out=hv_(jm_, s_ * SUB, SUB), in0=a,
                                in1=b, op=ALU.mult),
                            waits=wts, inc=cGP)
                    h_tmp_tok[jm % 2] = hw
                    h_tmp_tok[2] = hw
                    if not mh_started[s]:
                        mh_started[s] = True
                        op("scalar", lambda E, jm_=jm, s_=s, hv_=hv:
                            E.activation(out=Mh_sub(s_),
                                         in_=hv_(jm_, s_ * SUB, SUB),
                                         func=ACT.Abs),
                            waits=[hw], inc=cSCA)
                    else:
                        habs = (st16a + st16b)[jm]
                        hbt = habs_tok.get(jm)
                        ab = op("scalar", lambda E, jm_=jm, s_=s, hv_=hv,
                                hb=habs: E.activation(
                                    out=hb[:, 0:SUB],
                                    in_=hv_(jm_, s_ * SUB, SUB),
                                    func=ACT.Abs),
                            waits=[hw] + ([hbt] if hbt else []), inc=cSCA)
                        pend_maxes.append((jm, s, ab, habs))
                gate_free = d1
                up_free = d2
                for jm_, s_, ab_, hb_ in pend_maxes:
                    habs_tok[jm_] = op(
                        "vector", lambda E, s2=s_, hb=hb_: E.tensor_max(
                            out=Mh_sub(s2), in0=Mh_sub(s2),
                            in1=hb[:, 0:SUB]),
                        waits=[ab_], inc=cDVE)
                pend_maxes = []
            pump_wdquant(8)
            wqg_guard = (cPE, cPE.n)
            wq_dest_guard[0] = (cPE, cPE.n)
            hq_guard[buf] = [dma(
                "gpsimd",
                hs[qq * 512:(qq + 1) * 512, :].rearrange(
                    "(j p) t -> p j t", p=128),
                hqt.rearrange("p (j t) -> p j t", j=4),
                stH, waits=[(cGP, cGP.n)])]

        pump_wdquant(len(wdq_units))

        barrier("tensor", [(cDVE, cDVE.n), (cSCA, cSCA.n)])
        R2s = A6
        last_s32_use = (cDVE, cDVE.n)
        for s in range(NS_T):
            am = emit_amax_rows(Mh_sub(s), None)
            srow = rowB[0:1, :]
            rrow = rowC[0:1, :]
            ts_ = op("vector", lambda E: E.tensor_scalar(
                out=srow, in0=srow, scalar1=1.0 / 127.0, scalar2=EPS,
                op0=ALU.mult, op1=ALU.add), waits=[am], inc=cDVE)
            rr = op("vector", lambda E: E.reciprocal(out=rrow, in_=srow),
                    waits=[ts_], inc=cDVE)
            psR, gR = hilo_bcast(rrow, SUB, 7, waits=[rr],
                                 bank_guard=bank_tok["psR"])
            evR = op("vector", lambda E, s_=s: E.tensor_copy(
                out=R2s[:, s_ * SUB:(s_ + 1) * SUB], in_=psR),
                waits=[gR] + ([last_s32_use] if s == 0 else []), inc=cDVE)
            bank_tok["psR"] = evR
        r2_done = (cDVE, cDVE.n)

        if debug_stop == "B1":
            barrier("gpsimd", [(stH, stH.n)])
            barrier("vector", [r2_done, (wsd_done[0], wsd_done[1])])
            dbg = nc.dram_tensor("dbg", [128, T_OWN + 32], F32,
                                 kind="ExternalOutput").ap()
            dmy = dma("gpsimd", dbg[:, 0:T_OWN], R2s[:, 0:T_OWN], sem("stDBG"),
                      waits=[r2_done])
            dmy2 = dma("gpsimd", dbg[:, T_OWN:T_OWN + 17], wsc[:, 0:17],
                       sems["stDBG"], waits=[(wsd_done[0], wsd_done[1])])
            barrier("gpsimd", [(sems["stDBG"], sems["stDBG"].n)])
            barrier("sync", [(stH, stH.n)])
            with nc.Block() as block:
                @block.sync
                def _(E):
                    for fn in Q["sync"]:
                        fn(E)

                @block.tensor
                def _(E):
                    for fn in Q["tensor"]:
                        fn(E)

                @block.vector
                def _(E):
                    for fn in Q["vector"]:
                        fn(E)

                @block.scalar
                def _(E):
                    for fn in Q["scalar"]:
                        fn(E)

                @block.gpsimd
                def _(E):
                    for fn in Q["gpsimd"]:
                        fn(E)
            return nc

        def yv(m, c0, n):
            mm_ = m % 8
            base = A0 if mm_ < 4 else A1
            return base[:, (mm_ % 4) * T_OWN + c0:(mm_ % 4) * T_OWN + c0 + n]

        def wdq16(j):
            return A2_16[:, j * 2048:(j + 1) * 2048]

        def hqslot(i):
            base = A3_16 if i < 8 else A4_16
            ii = i % 8
            return base[:, ii * T_OWN:(ii + 1) * T_OWN]

        def wdhalf(i):
            return A5[:, i * 1024:(i + 1) * 1024]

        b2_guard = [wqg_guard, r2_done, (cSCA, cSCA.n), (cGP, cGP.n),
                    (stH, stH.n), (wsd_done[0], wsd_done[1])]
        hq_free = {}
        wdq_free = {}
        for half in range(2):
            mlist = list(range(half * 8, half * 8 + 8))
            bank_free = {0: None, 1: None, 2: None, 3: None}
            for kg in range(8):
                hq_toks = []
                wd_toks = []
                for j in range(8):
                    k = kg * 8 + j
                    ring_i = k % 16
                    hguard = hq_free.get(ring_i)
                    wguard = wdq_free.get(j)
                    wl2 = [wguard] if wguard else []
                    if kg == 0 and j == 0:
                        wl2.append((stWDQ, stWDQ.n))
                    t2 = dma("sync" if j % 2 == 0 else "scalar", wdq16(j),
                             wdqs[k * 128:(k + 1) * 128, :],
                             rE[j], waits=wl2)
                    wd_toks.append(t2)
                    if half == 0:
                        wl = [hguard] if hguard else []
                        if kg == 0 and j == 0:
                            wl = wl + b2_guard
                        t = dma("sync" if j % 2 == 1 else "scalar",
                                hqslot(ring_i),
                                hs[k * 128:(k + 1) * 128, :], rH[j],
                                waits=wl + [(stH, stH.n)])
                        last = None
                        for hh in range(max(1, T_OWN // 1024)):
                            c0 = hh * 1024
                            cn = min(1024, T_OWN - c0)
                            wslot = [tmpwide, stwide][(j * 2 + hh) % 2]
                            wlq = [t]
                            pt = h_tmp_tok.get((j * 2 + hh) % 2)
                            if pt is not None:
                                wlq.append(pt)
                            mq = op("vector", lambda E, ri=ring_i, c0_=c0,
                                    cn_=cn, t_=wslot: E.tensor_tensor(
                                        out=t_[:, 0:cn_],
                                        in0=hqslot(ri)[:, c0_:c0_ + cn_],
                                        in1=R2s[:, c0_:c0_ + cn_],
                                        op=ALU.mult), waits=wlq, inc=cDVE)
                            cv = op("vector", lambda E, t_=wslot, cn_=cn:
                                    E.tensor_copy(
                                        out=t_.bitcast(I32)[:, 0:cn_],
                                        in_=t_[:, 0:cn_]),
                                    waits=[mq], inc=cDVE)
                            last = op("gpsimd", lambda E, t_=wslot, ri=ring_i,
                                      c0_=c0, cn_=cn: E.tensor_copy(
                                          out=hqslot(ri)[:, c0_:c0_ + cn_],
                                          in_=t_.bitcast(I32)[:, 0:cn_]),
                                      waits=[cv], inc=cGP)
                            h_tmp_tok[(j * 2 + hh) % 2] = last
                        dma("gpsimd", hqs[k * 128:(k + 1) * 128, :],
                            hqslot(ring_i), stHQ, waits=[last])
                        hq_toks.append(last)
                    else:
                        wl = [hguard] if hguard else []
                        if kg == 0 and j == 0:
                            wl.append((stHQ, stHQ.n))
                            wl.append((cGP, cGP.n))
                            wl.append((stY1, stY1.n))
                        t = dma("sync" if j % 2 == 1 else "scalar",
                                hqslot(ring_i),
                                hqs[k * 128:(k + 1) * 128, :], rH[j],
                                waits=wl)
                        hq_toks.append(t)

                gidx = 0
                first_wts = hq_toks + wd_toks
                for s in range(NS_T):
                    for m in mlist:
                        bank = gidx % 4
                        gidx += 1
                        wts = []
                        if first_wts is not None:
                            wts = list(first_wts)
                            first_wts = None
                        if bank_free[bank] is not None:
                            wts.append(bank_free[bank])

                        def mmg(E, s_=s, m_=m, b_=bank, kg_=kg):
                            mm = None
                            for j in range(8):
                                k = kg_ * 8 + j
                                mm = E.matmul(
                                    psb[b_][:],
                                    wdq16(j)[:, m_ * 128:(m_ + 1) * 128],
                                    hqslot(k % 16)[:, s_ * SUB:(s_ + 1) * SUB],
                                    start=(j == 0), stop=(j == 7))
                            return mm
                        gt = op("tensor", mmg, waits=wts, inc=cPE)
                        if kg == 0:
                            ev = op("vector", lambda E, m_=m, s_=s, b_=bank:
                                    E.tensor_copy(out=yv(m_, s_ * SUB, SUB),
                                                  in_=psb[b_][:]),
                                    waits=[gt], inc=cDVE)
                        else:
                            ev = op("vector", lambda E, m_=m, s_=s, b_=bank:
                                    E.tensor_tensor(out=yv(m_, s_ * SUB, SUB),
                                                    in0=yv(m_, s_ * SUB, SUB),
                                                    in1=psb[b_][:], op=ALU.add),
                                    waits=[gt], inc=cDVE)
                        bank_free[bank] = ev
                for j in range(8):
                    k = kg * 8 + j
                    hq_free[k % 16] = (cPE, cPE.n)
                for j in range(8):
                    wdq_free[j] = (cPE, cPE.n)
            if half == 0:
                ydone = (cDVE, cDVE.n)
                for m in range(8):
                    dma("scalar", y1s[m * 128:(m + 1) * 128, :],
                        yv(m, 0, T_OWN), stY1, waits=[ydone])

        pe_all = (cPE, cPE.n)
        y_all = (cDVE, cDVE.n)

        op("vector", lambda E: E.reciprocal(out=R2s[:, 0:T_OWN],
                                            in_=R2s[:, 0:T_OWN]),
           waits=[(cGP, cGP.n), (wsd_done[0], wsd_done[1])])
        s2w_done = op("vector", lambda E: E.tensor_scalar(
            out=R2s[:, 0:T_OWN], in0=R2s[:, 0:T_OWN],
            scalar1=wsc[:, 14:15], scalar2=None, op0=ALU.mult), inc=cDVE)
        S2W = R2s

        def xr2v(m, s):
            if m >= 8:
                return yv(m, s * SUB, SUB)
            base = A3 if m < 4 else A4
            return base[:, (m % 4) * T_OWN + s * SUB:
                        (m % 4) * T_OWN + (s + 1) * SUB]

        for m in range(NKH):
            slot, tok = fring.acquire()
            stage = A2[:, (slot % 4) * 2048:((slot % 4) + 1) * 2048]
            wl = [tok] if tok else []
            if m == 0:
                wl += [pe_all, y_all, (cGP, cGP.n), (cSCA, cSCA.n)]
            t = dma("sync", stage[:, 0:T_OWN],
                    x1s[m * 128:(m + 1) * 128, :], rF[slot],
                    waits=wl + [(stX1, stX1.n)])
            if m < 8:
                slot2, tok2 = fring.acquire()
                stage2 = A2[:, (slot2 % 4) * 2048:((slot2 % 4) + 1) * 2048]
                wl2 = [tok2] if tok2 else []
                t2 = dma("sync", stage2[:, 0:T_OWN],
                         y1s[m * 128:(m + 1) * 128, :], rF[slot2],
                         waits=wl2 + [(stY1, stY1.n)])
                last = None
                for s in range(NS_T):
                    wl3 = [t, t2, s2w_done] if s == 0 else []
                    op("vector", lambda E, m_=m, s_=s, b=stage2:
                       E.tensor_tensor(
                           out=xr2v(m_, s_),
                           in0=b[:, s_ * SUB:(s_ + 1) * SUB],
                           in1=S2W[:, s_ * SUB:(s_ + 1) * SUB],
                           op=ALU.mult), waits=wl3)
                    last = op("vector", lambda E, m_=m, s_=s, a=stage:
                              E.tensor_tensor(
                                  out=xr2v(m_, s_), in0=xr2v(m_, s_),
                                  in1=a[:, s_ * SUB:(s_ + 1) * SUB],
                                  op=ALU.add), inc=cDVE)
                fring.release(slot, last)
                fring.release(slot2, last)
            else:
                last = None
                for s in range(NS_T):
                    wl3 = [t, s2w_done] if s == 0 else []
                    op("vector", lambda E, m_=m, s_=s:
                       E.tensor_tensor(out=xr2v(m_, s_), in0=xr2v(m_, s_),
                                       in1=S2W[:, s_ * SUB:(s_ + 1) * SUB],
                                       op=ALU.mult), waits=wl3)
                    last = op("vector", lambda E, m_=m, s_=s, a=stage:
                              E.tensor_tensor(out=xr2v(m_, s_),
                                              in0=xr2v(m_, s_),
                                              in1=a[:, s_ * SUB:(s_ + 1) * SUB],
                                              op=ALU.add),
                              waits=[] if s > 0 else [], inc=cDVE)
                fring.release(slot, last)
        xr2_done = (cDVE, cDVE.n)

        out_tok = [None] * NKH
        for s in range(NS_T):
            pe0 = cPE.n
            for m in range(NKH):
                t16 = st16a[m % 2]
                q16 = st16b[m % 2]
                wts = [(cPE, pe0 + m - 1)] if m >= 2 else [xr2_done]
                c1 = cast_slice("gpsimd", t16[:, 0:SUB], xr2v(m, s), waits=wts,
                                inc=cGP)
                c2 = op("scalar", lambda E, qq=q16, src=xr2v(m, s): E.activation(
                    out=qq[:, 0:SUB], in_=src, func=ACT.Square), waits=wts,
                    inc=cSCA)
                mw = [c1, c2]
                if m == 0 and bank_tok.get("stats") is not None:
                    mw.append(bank_tok["stats"])

                def smm(E, m_=m, t=t16, qq=q16):
                    E.matmul(psb[0][0:1, :], ones16[:, 0:1], t[:, 0:SUB],
                             start=(m_ == 0), stop=(m_ == NKH - 1))
                    return E.matmul(psb[1][0:1, :], ones16[:, 0:1], qq[:, 0:SUB],
                                    start=(m_ == 0), stop=(m_ == NKH - 1))
                op("tensor", smm, waits=mw, inc=cPE)
            st_done = (cPE, cPE.n)
            sq, bb, bank_rel = emit_ln_rows(0, 1, st_done)
            bank_tok["stats"] = bank_rel
            psA, gA = hilo_bcast(rowC[0:1, :], SUB, 2, waits=[sq],
                                 bank_guard=bank_tok["psA"])
            psB, gB = hilo_bcast(rowA[0:1, :], SUB, 3, waits=[bb],
                                 bank_guard=bank_tok["psB"])
            ap_done = None
            for m in range(NKH):
                wts = [gA, gB] if m == 0 else []
                op("vector", lambda E, m_=m, s_=s: E.tensor_tensor(
                    out=xr2v(m_, s_), in0=xr2v(m_, s_), in1=psA, op=ALU.mult),
                    waits=wts)
                ap_done = op("vector", lambda E, m_=m, s_=s: E.tensor_tensor(
                    out=xr2v(m_, s_), in0=xr2v(m_, s_), in1=psB,
                    op=ALU.subtract), inc=cDVE)
                if s == NS_T - 1:
                    out_tok[m] = ap_done
            bank_tok["psA"] = ap_done
            bank_tok["psB"] = ap_done
        for m in range(NKH):
            if m >= 8:
                mm_ = m % 8
                base = A0 if mm_ < 4 else A1
                srcv = base[:, (mm_ % 4) * T_OWN:(mm_ % 4 + 1) * T_OWN]
            else:
                base = A3 if m < 4 else A4
                srcv = base[:, (m % 4) * T_OWN:(m % 4 + 1) * T_OWN]
            dma("scalar", out_d[m * 128:(m + 1) * 128, :], srcv,
                stOUT, waits=[out_tok[m]])
        barrier("gpsimd", [(stOUT, stOUT.n)])
        barrier("scalar", [(stOUT, stOUT.n)])
        barrier("sync", [(stOUT, stOUT.n)])

        with nc.Block() as block:
            @block.sync
            def _(E):
                for fn in Q["sync"]:
                    fn(E)

            @block.tensor
            def _(E):
                for fn in Q["tensor"]:
                    fn(E)

            @block.vector
            def _(E):
                for fn in Q["vector"]:
                    fn(E)

            @block.scalar
            def _(E):
                for fn in Q["scalar"]:
                    fn(E)

            @block.gpsimd
            def _(E):
                for fn in Q["gpsimd"]:
                    fn(E)

    return nc


def _qmajor(w, nq):
    K, M = w.shape
    assert M == nq * 512
    a = w.reshape(K // 128, 128, nq, 512)
    return np.ascontiguousarray(a.transpose(2, 0, 1, 3))


def kernel(**inp):
    B_, S_, H_ = 2, 8192, 2048
    T = 2048
    x = np.asarray(inp["L0_attn_x"], np.float32)

    att = {k: np.ascontiguousarray(np.asarray(inp["L0_attn_" + k], np.float32))
           for k in ["w_f", "w_i", "w_v", "w_q", "w_ro", "w_up", "w_gate",
                     "w_down"]}
    ffn_g = np.ascontiguousarray(np.asarray(inp["L0_ffn_w_gate"], np.float32).T)
    ffn_u = np.ascontiguousarray(np.asarray(inp["L0_ffn_w_up"], np.float32).T)
    ffn_d = np.ascontiguousarray(np.asarray(inp["L0_ffn_w_down"], np.float32).T)

    wm = {
        "wf": _qmajor(att["w_f"], 4), "wi": _qmajor(att["w_i"], 4),
        "wv": _qmajor(att["w_v"], 4), "wq_": _qmajor(att["w_q"], 4),
        "wro": _qmajor(att["w_ro"], 4), "wup": _qmajor(att["w_up"], 4),
        "wgt": _qmajor(att["w_gate"], 4), "wdn": _qmajor(att["w_down"], 4),
        "fg": _qmajor(ffn_g, 16), "fu": _qmajor(ffn_u, 16),
        "fd": ffn_d,
        "ident": np.eye(128, dtype=np.float32),
    }

    in_maps = []
    for c in range(8):
        b, ch = divmod(c, 4)
        lo = ch * T - WARM
        if lo < 0:
            seg = np.concatenate(
                [np.zeros((WARM, H_), np.float32), x[b, 0:(ch + 1) * T]], 0)
        else:
            seg = x[b, lo:(ch + 1) * T]
        xc = np.ascontiguousarray(seg.T)
        m = {"xin": xc}
        m.update(wm)
        in_maps.append(m)

    nc = build_nc(T_OWN=T)
    res = run_bass_kernel_spmd(nc, in_maps, core_ids=list(range(8)))
    out = np.empty((B_, S_, H_), np.float32)
    for c in range(8):
        b, ch = divmod(c, 4)
        out[b, ch * T:(ch + 1) * T, :] = res.results[c]["out"].T
    return out


# version 18 (final)
# speedup vs baseline: 1.2159x; 1.0111x over previous
import numpy as np
from contextlib import ExitStack

import concourse.bass as bass
import concourse.mybir as mybir
from concourse.bass_utils import run_bass_kernel_spmd

F32 = mybir.dt.float32
F16 = mybir.dt.float16
I32 = mybir.dt.int32
ALU = mybir.AluOpType
ACT = mybir.ActivationFunctionType
AX = mybir.AxisListType

H = 2048
NKH = H // 128
II = 8192
NKI = II // 128
SUB = 512
WARM = 512
EPS = 1e-5
NW_ELEM = float(II * H)


class Sem:
    __slots__ = ("h", "n", "name")

    def __init__(self, h, name):
        self.h = h
        self.n = 0
        self.name = name


class Ring:
    def __init__(self, n):
        self.n = n
        self.tok = [None] * n
        self.i = 0

    def acquire(self):
        s = self.i % self.n
        self.i += 1
        return s, self.tok[s]

    def release(self, s, token):
        self.tok[s] = token


def build_nc(T_OWN=2048, debug_stop=None):
    NS_T = T_OWN // SUB
    TCOL = T_OWN + WARM

    nc = bass.Bass("TRN2", target_bir_lowering=False,
                   detect_race_conditions=False)

    xin = nc.dram_tensor("xin", [H, TCOL], F32, kind="ExternalInput").ap()
    ident_in = nc.dram_tensor("ident", [128, 128], F32, kind="ExternalInput").ap()
    wnames = ["wf", "wi", "wv", "wq_", "wro", "wup", "wgt", "wdn"]
    wap = {n: nc.dram_tensor(n, [4, NKH, 128, 512], F32, kind="ExternalInput").ap()
           for n in wnames}
    fg = nc.dram_tensor("fg", [16, NKH, 128, 512], F32, kind="ExternalInput").ap()
    fu = nc.dram_tensor("fu", [16, NKH, 128, 512], F32, kind="ExternalInput").ap()
    fd = nc.dram_tensor("fd", [II, H], F32, kind="ExternalInput").ap()
    scratch_kind = "ExternalOutput" if debug_stop else "Internal"
    xrdbg = (nc.dram_tensor("xrdbg", [H, T_OWN], F32, kind="ExternalOutput").ap()
             if debug_stop else None)
    rodbg = (nc.dram_tensor("rodbg", [H, T_OWN], F16, kind="ExternalOutput").ap()
             if debug_stop else None)
    stdbg = (nc.dram_tensor("stdbg", [H, T_OWN], F16, kind="ExternalOutput").ap()
             if debug_stop else None)
    scdbg = (nc.dram_tensor("scdbg", [128, 32], F32, kind="ExternalOutput").ap()
             if debug_stop else None)
    fdbg = (nc.dram_tensor("fdbg", [H, T_OWN], F16, kind="ExternalOutput").ap()
            if debug_stop else None)
    gdbg = (nc.dram_tensor("gdbg", [H, T_OWN], F16, kind="ExternalOutput").ap()
            if debug_stop else None)
    pdbg = (nc.dram_tensor("pdbg", [H, T_OWN], F16, kind="ExternalOutput").ap()
            if debug_stop else None)
    out_d = nc.dram_tensor("out", [H, T_OWN], F32, kind="ExternalOutput").ap()
    x1s = nc.dram_tensor("x1s", [H, T_OWN], F32, kind=scratch_kind).ap()
    xqs = nc.dram_tensor("xqs", [H, T_OWN], F16, kind=scratch_kind).ap()
    hs = nc.dram_tensor("hs", [II, T_OWN], F16, kind=scratch_kind).ap()
    hqs = nc.dram_tensor("hqs", [II, T_OWN], F16, kind=scratch_kind).ap()
    wdqs = nc.dram_tensor("wdqs", [II, H], F16, kind=scratch_kind).ap()
    wf16s = nc.dram_tensor("wf16s", [8, 4, 128, 8192], F16, kind="Internal").ap()
    y1s = nc.dram_tensor("y1s", [H // 2, T_OWN], F32, kind=scratch_kind).ap()

    ctx = ExitStack()
    with ctx:
        AR = 53192
        arena = ctx.enter_context(nc.sbuf_tensor("arena", [128, AR], F32)).ap()
        off = [0]

        def alloc(n):
            o = off[0]
            off[0] += n
            assert off[0] <= AR, f"arena overflow {off[0]}"
            return arena[:, o:o + n]

        A0 = alloc(8192)
        A1 = alloc(8192)
        A2 = alloc(8192)
        A3 = alloc(8192)
        A4 = alloc(8192)
        A5 = alloc(2048)
        A6 = alloc(max(T_OWN, 2048))
        SM_N = AR - off[0]
        smo = [0]
        SMbase = alloc(SM_N)

        def sm(n):
            o = smo[0]
            smo[0] += n
            assert smo[0] <= SM_N, f"smalls overflow {smo[0]} > {SM_N}"
            return SMbase[:, o:o + n]

        ident = sm(128)
        ones16 = sm(16).bitcast(F16)
        onesrow16 = sm(64).bitcast(F16)
        state_col = sm(32)
        rowA = sm(512)
        rowB = sm(512)
        rowC = sm(512)
        r16 = sm(512).bitcast(F16)
        Mx = sm(512)
        st_base = smo[0]
        st16a = [sm(256).bitcast(F16) for _ in range(2)]
        st16b = [sm(256).bitcast(F16) for _ in range(2)]
        stwide = SMbase[:, st_base:st_base + 1024]
        stwideB16 = SMbase[:, st_base + 512:st_base + 1024].bitcast(F16)
        xres = [sm(512) for _ in range(3)]
        tmp_base = smo[0]
        tmp = [sm(512) for _ in range(3)]
        tmpwide = SMbase[:, tmp_base:tmp_base + 1024]
        clb_base = smo[0]
        clb = [sm(512) for _ in range(2)]
        clbwide = SMbase[:, clb_base:clb_base + 1024]
        wsc = sm(24)
        R2sbA = clb[0]

        psb = [ctx.enter_context(nc.psum_tensor(f"pb{i}", [128, 512], F32))
               for i in range(8)]

        Q = {"sync": [], "tensor": [], "vector": [], "scalar": [], "gpsimd": []}
        sems = {}

        def sem(name):
            if name not in sems:
                sems[name] = Sem(ctx.enter_context(nc.semaphore(name)), name)
            return sems[name]

        def emit(eng, fn):
            Q[eng].append(fn)

        def wof(wl):
            return [(s, v) for (s, v) in wl if s is not None]

        def barrier(eng, wl):
            wl = wof(wl)
            if not wl:
                return

            def fn(E):
                for s, v in wl:
                    E.wait_ge(s.h, v)
            emit(eng, fn)

        def dma(eng, dst, src, s, waits=()):
            wl = wof(waits)

            def fn(E):
                for ss, v in wl:
                    E.wait_ge(ss.h, v)
                E.dma_start(out=dst, in_=src).then_inc(s.h, 16)
            emit(eng, fn)
            s.n += 16
            return (s, s.n)

        def op(eng, build, waits=(), inc=None):
            wl = wof(waits)
            if inc is not None:
                inc.n += 1

            def fn(E):
                for s, v in wl:
                    E.wait_ge(s.h, v)
                ins = build(E)
                if inc is not None:
                    ins.then_inc(inc.h, 1)
            emit(eng, fn)
            return (inc, inc.n) if inc is not None else None

        rW = [sem(f"rW{i}") for i in range(2)]
        rP = [sem(f"rP{i}") for i in range(2)]
        rB = [sem(f"rB{i}") for i in range(10)]
        rX = [sem(f"rX{i}") for i in range(2)]
        rZ = [sem(f"rZ{i}") for i in range(3)]
        rH = [sem(f"rH{i}") for i in range(8)]
        rD = [sem(f"rD{i}") for i in range(2)]
        rE = [sem(f"rE{i}") for i in range(8)]
        rF = [sem(f"rF{i}") for i in range(4)]
        sID = sem("sID")
        sXQ = sem("sXQ")
        stXQ4 = [sem(f"stXQ4{i}") for i in range(4)]
        stWF = sem("stWF")
        rWF = [sem(f"rWF{i}") for i in range(2)]
        stX1 = sem("stX1")
        stXQ = sem("stXQ")
        stH = sem("stH")
        stHQ = sem("stHQ")
        stWDQ = sem("stWDQ")
        stY1 = sem("stY1")
        stOUT = sem("stOUT")
        cGP = sem("cGP")
        cDVE = sem("cDVE")
        cSCA = sem("cSCA")
        cPE = sem("cPE")
        ESEM = {"vector": cDVE, "scalar": cSCA, "gpsimd": cGP}

        wring = Ring(2)
        bring = Ring(10)
        xring = Ring(2)
        zring = Ring(3)
        dring = Ring(2)
        fring = Ring(4)

        def cast_slice(eng, dst, src, waits=(), inc=None):
            if eng == "scalar":
                return op(eng, lambda E: E.activation(out=dst, in_=src,
                                                      func=ACT.Copy), waits, inc)
            return op(eng, lambda E: E.tensor_copy(out=dst, in_=src), waits, inc)

        A0_16 = A0.bitcast(F16)
        A1_16 = A1.bitcast(F16)
        A2_16 = A2.bitcast(F16)
        A3_16 = A3.bitcast(F16)
        A4_16 = A4.bitcast(F16)

        def x16v(k, c0, n):
            return A0_16[:, k * 1024 + c0:k * 1024 + c0 + n]

        def bAv(k, c0, n):
            return A1_16[:, k * 1024 + c0:k * 1024 + c0 + n]

        def bBv(k, c0, n):
            return A2_16[:, k * 1024 + c0:k * 1024 + c0 + n]

        def bCv(k, c0, n):
            return A3_16[:, k * 1024 + c0:k * 1024 + c0 + n]

        def wstage(slot):
            return A5[:, slot * 1024:(slot + 1) * 1024]

        prescan_units = []
        acc_init = [False, False, False]
        acc_done_tok = [None, None, None]

        def queue_prescan_qmajor(dram, nq, ai):
            for q in range(nq):
                for kg in range(NKH // 2):
                    src = dram[q, kg * 2:(kg + 1) * 2, :, :].rearrange(
                        "k p c -> p k c")
                    prescan_units.append((src, ai))

        def queue_prescan_down(ai):
            for k in range(NKI):
                for hh in range(2):
                    src = fd[k * 128:(k + 1) * 128, hh * 1024:(hh + 1) * 1024]
                    prescan_units.append((src, ai))

        def pump_prescan(n):
            for _ in range(n):
                if not prescan_units:
                    return
                src, ai = prescan_units.pop(0)
                slot, tok = wring.acquire()
                dst = wstage(slot)
                if len(src.shape) == 3:
                    dst = dst.rearrange("p (k c) -> p k c", k=2)
                t = dma("sync" if slot == 0 else "scalar", dst, src, rP[slot],
                        waits=[tok] if tok else [])
                acc = wsc[:, ai:ai + 1]
                part = wsc[:, 3 + ai:4 + ai]
                stg = wstage(slot)
                ptok = acc_done_tok[ai]
                rtok = op("scalar", lambda E, s_=stg, p=part: E.activation(
                    out=s_, in_=s_, func=ACT.Abs, accum_out=p),
                    waits=[t] + ([ptok] if ptok else []), inc=cSCA)
                if not acc_init[ai]:
                    acc_init[ai] = True
                    rel = op("gpsimd", lambda E, a=acc, p=part: E.tensor_copy(
                        out=a, in_=p), waits=[rtok], inc=cGP)
                else:
                    rel = op("gpsimd", lambda E, a=acc, p=part: E.tensor_tensor(
                        out=a, in0=a, in1=p, op=ALU.add), waits=[rtok],
                        inc=cGP)
                wring.release(slot, (cSCA, rtok[1]))
                acc_done_tok[ai] = rel

        idtok = dma("sync", ident, ident_in, sID)
        op("vector", lambda E: E.memset(ones16[:, 0:1], 1.0))
        op("vector", lambda E: E.memset(onesrow16[0:1, 0:128], 1.0))
        setup_tok = op("vector", lambda E: E.memset(state_col, 0.0), inc=cDVE)
        barrier("tensor", [idtok, setup_tok])
        barrier("vector", [setup_tok])

        queue_prescan_qmajor(fg, 16, 0)
        queue_prescan_qmajor(fu, 16, 1)
        queue_prescan_down(2)

        tct_state = {"prev_copy": None}

        def transpose_col_to_row(col_ap, bank, dst_row, waits=()):
            wl = list(waits)
            if tct_state["prev_copy"] is not None:
                wl.append(tct_state["prev_copy"])
            g = op("tensor", lambda E: E.transpose(
                psb[bank][0:1, 0:128], col_ap, ident), waits=wl, inc=cPE)
            v = op("vector", lambda E: E.tensor_copy(
                out=dst_row, in_=psb[bank][0:1, 0:128]), waits=[g], inc=cDVE)
            tct_state["prev_copy"] = v
            return v

        pair_tok = [None]

        def hilo_bcast(row_f32, n, bank, waits=(), bank_guard=None):
            hi = r16[0:1, 0:n]
            lo = r16[0:1, 512:512 + n]
            wl = list(waits)
            if pair_tok[0] is not None:
                wl.append(pair_tok[0])
            c1 = op("vector", lambda E: E.tensor_copy(out=hi, in_=row_f32),
                    waits=wl, inc=cDVE)
            c2 = op("vector", lambda E: E.tensor_tensor(
                out=lo, in0=row_f32, in1=hi, op=ALU.subtract), waits=[c1],
                inc=cDVE)
            mw = [c2]
            if bank_guard is not None:
                mw.append(bank_guard)

            def mm(E):
                E.matmul(psb[bank][:, 0:n], onesrow16[0:1, 0:128], hi,
                         start=True, stop=False)
                return E.matmul(psb[bank][:, 0:n], onesrow16[0:1, 0:128], lo,
                                start=False, stop=True)
            g = op("tensor", mm, waits=mw, inc=cPE)
            pair_tok[0] = g
            return psb[bank][:, 0:n], g

        tile_guard = {"x16": None, "bufA": None, "bufB": None, "bufC": None,
                      "st16": None}

        def emit_x_load(tile_t0, width):
            last = None
            for k in range(NKH):
                for s in range(width // SUB):
                    slot, tok = xring.acquire()
                    stage = tmp[slot]
                    wl = [tok] if tok else []
                    t = dma("sync", stage,
                            xin[k * 128:(k + 1) * 128,
                                tile_t0 + s * SUB:tile_t0 + (s + 1) * SUB],
                            rX[slot], waits=wl)
                    wts = [t]
                    if k == 0 and s == 0 and tile_guard["x16"] is not None:
                        wts += tile_guard["x16"]
                    r = cast_slice("gpsimd", x16v(k, s * SUB, SUB), stage,
                                   waits=wts, inc=cGP)
                    xring.release(slot, r)
                    last = r
            return last

        cast_rr = ["vector", "scalar", "vector", "gpsimd"]
        rrci = [0]

        def stream_quarter(dram_q, dest_slice_fn, extra_guard=None,
                           cache_dst=None, dest_full=None):
            eng = cast_rr[rrci[0] % len(cast_rr)]
            rrci[0] += 1
            done = None
            for kg in range(8):
                slot, tok = wring.acquire()
                wl = [tok] if tok else []
                t = dma("sync" if kg % 2 == 0 else "scalar",
                        wstage(slot).rearrange("p (k c) -> p k c", k=2),
                        dram_q[kg * 2:(kg + 1) * 2, :, :].rearrange(
                            "k p c -> p k c"),
                        rW[slot], waits=wl)
                last = None
                for j in range(2):
                    k = kg * 2 + j
                    wts = [t] if j == 0 else []
                    if kg == 0 and j == 0 and extra_guard:
                        wts = wts + list(extra_guard)
                    src = A5[:, slot * 1024 + j * 512:slot * 1024 + (j + 1) * 512]
                    last = cast_slice(eng, dest_slice_fn(k), src, waits=wts,
                                      inc=ESEM[eng])
                wring.release(slot, last)
                done = last
            if cache_dst is not None:
                ck = dma("gpsimd", cache_dst, dest_full, stWF, waits=[done])
                return done, ck
            return done, None

        def load_quarter_f16(cache_src, dest_full, extra_guard=None, qi=0):
            wl = [(stWF, stWF.n)]
            if extra_guard:
                wl = wl + list(extra_guard)
            return dma("sync" if qi % 2 == 0 else "scalar", dest_full,
                       cache_src, rWF[qi % 2], waits=wl)

        def load_quarter_f16_split(cache_src, dest_full, extra_guard=None,
                                   qi=0):
            wl = [(stWF, stWF.n)]
            if extra_guard:
                wl = wl + list(extra_guard)
            hw = 8 * 512
            qname = "sync" if qi % 2 == 0 else "scalar"
            lo = dma(qname, dest_full[:, 0:hw], cache_src[:, 0:hw],
                     rWF[qi % 2], waits=wl)
            hi = dma(qname, dest_full[:, hw:2 * hw], cache_src[:, hw:2 * hw],
                     rWF[qi % 2], waits=[])
            return lo, hi

        proj_pass = {}
        prev_ck = {0: None, 1: None}
        wq_guard_g = {0: None, 1: None}

        def emit_proj(wname, width, evac, moving_v, prewait=None):
            ns = width // SUB
            dram = wap[wname]
            pidx = wnames.index(wname)
            first = wname not in proj_pass
            proj_pass[wname] = True
            free_tok = {0: None, 4: None}
            wq_guard = [wq_guard_g[0], wq_guard_g[1]]
            gi = [0]
            last_group = [None]
            for q in range(4):
                buf = q % 2

                def wsl(k, b=buf):
                    return A4_16[:, (b * 16 + k) * 512:(b * 16 + k + 1) * 512]

                def wfull(b=buf):
                    return A4_16[:, (b * 16) * 512:(b * 16 + 16) * 512]
                eg = list(wq_guard[buf] or [])
                if prev_ck[buf] is not None:
                    eg.append(prev_ck[buf])
                    prev_ck[buf] = None
                if first:
                    cd, ck = stream_quarter(
                        dram[q], wsl, extra_guard=eg,
                        cache_dst=wf16s[pidx, q], dest_full=wfull())
                else:
                    cd = load_quarter_f16(wf16s[pidx, q], wfull(),
                                          extra_guard=eg, qi=q)
                    ck = None
                pump_prescan(6)
                for s in range(ns):
                    g = gi[0]
                    gi[0] += 1
                    bank0 = (g % 2) * 4
                    wts = []
                    if s == 0:
                        wts.append(cd)
                        if prewait is not None:
                            wts.append(prewait)
                            prewait = None
                    if free_tok[bank0] is not None:
                        wts.append(free_tok[bank0])

                    def mmg(E, s_=s, b0=bank0, wsl_=wsl):
                        mm = None
                        for k in range(NKH):
                            for m in range(4):
                                mm = E.matmul(psb[b0 + m][:],
                                              wsl_(k)[:, m * 128:(m + 1) * 128],
                                              moving_v(k, s_ * SUB, SUB),
                                              start=(k == 0), stop=(k == NKH - 1))
                        return mm
                    gt = op("tensor", mmg, waits=wts, inc=cPE)
                    free_tok[bank0] = evac(q, s, bank0, gt)
                    last_group[0] = gt
                wq_guard[buf] = [last_group[0]] + ([ck] if ck else [])
                wq_guard_g[buf] = wq_guard[buf]
                if ck is not None:
                    prev_ck[buf] = ck
            return last_group[0]

        def emit_ln_rows(sum_bank, sq_bank, st_done):
            mean = rowA[0:1, :]
            var = rowB[0:1, :]
            istd = rowC[0:1, :]
            t1 = op("vector", lambda E: E.tensor_scalar(
                out=mean, in0=psb[sum_bank][0:1, :], scalar1=1.0 / H,
                scalar2=None, op0=ALU.mult), waits=[st_done], inc=cDVE)
            bank_rel = op("vector", lambda E: E.tensor_scalar(
                out=var, in0=psb[sq_bank][0:1, :], scalar1=1.0 / H,
                scalar2=None, op0=ALU.mult), inc=cDVE)
            t2 = op("vector", lambda E: E.tensor_tensor(
                out=istd, in0=mean, in1=mean, op=ALU.mult),
                waits=[t1, bank_rel], inc=cDVE)
            t3 = op("vector", lambda E: E.tensor_tensor(
                out=var, in0=var, in1=istd, op=ALU.subtract), waits=[t2],
                inc=cDVE)
            t4 = op("vector", lambda E: E.tensor_scalar(
                out=var, in0=var, scalar1=EPS, scalar2=None, op0=ALU.add),
                waits=[t3], inc=cDVE)
            rec = op("vector", lambda E: E.reciprocal(out=var, in_=var),
                     waits=[t4], inc=cDVE)
            sq = op("scalar", lambda E: E.activation(
                out=istd, in_=var, func=ACT.Sqrt), waits=[rec], inc=cSCA)
            bb = op("vector", lambda E: E.tensor_tensor(
                out=mean, in0=mean, in1=istd, op=ALU.mult), waits=[sq],
                inc=cDVE)
            return sq, bb, bank_rel

        def emit_amax_rows(M_ap, first_wait):
            last = None
            for g4 in range(4):
                bk = 4 + (g4 % 2)
                wl = [first_wait] if (g4 == 0 and first_wait) else []
                if tct_state["prev_copy"] is not None:
                    wl.append(tct_state["prev_copy"])
                tg = op("tensor", lambda E, g_=g4, b=bk, M=M_ap: E.transpose(
                    psb[b][:, 0:128], M[:, g_ * 128:(g_ + 1) * 128], ident),
                    waits=wl, inc=cPE)
                rd = op("vector", lambda E, b=bk: E.reduce_max(
                    out=wsc[:, 7:8], in_=psb[b][:, 0:128], axis=AX.X),
                    waits=[tg] + ([last] if last else []), inc=cDVE)
                last = transpose_col_to_row(
                    wsc[:, 7:8], 4 + ((g4 + 1) % 2),
                    rowB[0:1, g4 * 128:(g4 + 1) * 128], waits=[rd])
            return last

        bank_tok = {"psA": None, "psB": None, "psS": None, "psR": None}

        def emit_attn_tile(tile_t0, width, warm_only):
            ns = width // SUB
            own0 = tile_t0 - WARM
            xl = emit_x_load(tile_t0, width)
            pump_prescan(4)

            def ev_f(q, s, b0, gt):
                wts = [gt]
                if q == 0 and s == 0 and tile_guard["bufA"] is not None:
                    wts += tile_guard["bufA"]
                    tile_guard["bufA"] = None

                def fn(E):
                    a = None
                    for m in range(4):
                        a = E.activation(out=bAv(q * 4 + m, s * SUB, SUB),
                                         in_=psb[b0 + m][:], func=ACT.Tanh)
                    return a
                return op("scalar", fn, waits=wts, inc=cSCA)
            emit_proj("wf", width, ev_f, x16v, prewait=xl)
            f_done = (cSCA, cSCA.n)
            pump_prescan(4)

            def ev_i(q, s, b0, gt):
                wts = [gt]
                if q == 0 and s == 0 and tile_guard["bufB"] is not None:
                    wts += tile_guard["bufB"]
                    tile_guard["bufB"] = None

                def fn(E):
                    a = None
                    for m in range(4):
                        a = E.activation(out=bBv(q * 4 + m, s * SUB, SUB),
                                         in_=psb[b0 + m][:], func=ACT.Sigmoid)
                    return a
                return op("scalar", fn, waits=wts, inc=cSCA)
            emit_proj("wi", width, ev_i, x16v)
            pump_prescan(4)

            vtok = {}

            def ev_v(q, s, b0, gt):
                for m in range(4):
                    mi = q * 4 + m
                    ts = tmp[2]
                    wts = [gt] if m == 0 else []
                    pt = vtok.get(2)
                    if pt is not None:
                        wts.append(pt)
                    sv = op("scalar", lambda E, b=b0 + m, t=ts: E.activation(
                        out=t, in_=psb[b][:], func=ACT.Silu), waits=wts,
                        inc=cSCA)
                    dv = op("vector", lambda E, mi_=mi, t=ts, s_=s:
                            E.tensor_tensor(out=bBv(mi_, s_ * SUB, SUB), in0=t,
                                            in1=bBv(mi_, s_ * SUB, SUB),
                                            op=ALU.mult), waits=[sv], inc=cDVE)
                    vtok[2] = dv
                return (cSCA, cSCA.n)
            emit_proj("wv", width, ev_v, x16v)
            pump_prescan(2)

            barrier("vector", [f_done] + (tile_guard["bufC"] or []))
            tile_guard["bufC"] = None
            for k in range(NKH):
                sc_tok = op("vector", lambda E, k_=k: E.tensor_tensor_scan(
                    out=bCv(k_, 0, width), data0=bAv(k_, 0, width),
                    data1=bBv(k_, 0, width),
                    initial=state_col[:, 2 * k_ + 1:2 * k_ + 2],
                    op0=ALU.mult, op1=ALU.add), inc=cDVE)
                op("vector", lambda E, k_=k: E.tensor_copy(
                    out=state_col[:, 2 * k_:2 * k_ + 2],
                    in_=bCv(k_, width - 2, 2)), waits=[sc_tok], inc=cDVE)
            scan_done = (cDVE, cDVE.n)
            if debug_stop == "A" and not warm_only:
                for k in range(NKH):
                    dma("gpsimd", stdbg[k * 128:(k + 1) * 128,
                                        own0:own0 + width],
                        bCv(k, 0, width), sem("stST"), waits=[scan_done])
                    dma("gpsimd", fdbg[k * 128:(k + 1) * 128,
                                       own0:own0 + width],
                        bAv(k, 0, width), sem("stST"))
                    dma("gpsimd", gdbg[k * 128:(k + 1) * 128,
                                       own0:own0 + width],
                        bBv(k, 0, width), sem("stST"))
                barrier("scalar", [(sems["stST"], sems["stST"].n)])
            if warm_only:
                tile_guard["x16"] = [(cPE, cPE.n)]
                tile_guard["bufA"] = [scan_done]
                tile_guard["bufB"] = [scan_done]
                return

            barrier("scalar", [scan_done])
            qtok = {}

            def ev_q(q, s, b0, gt):
                for m in range(4):
                    mi = q * 4 + m
                    ts = tmp[2]
                    wts = [gt] if m == 0 else []
                    pt = qtok.get(2)
                    if pt is not None:
                        wts.append(pt)
                    dv = op("vector", lambda E, b=b0 + m, t=ts, mi_=mi, s_=s:
                            E.tensor_tensor(out=t, in0=psb[b][:],
                                            in1=bCv(mi_, s_ * SUB, SUB),
                                            op=ALU.mult), waits=wts, inc=cDVE)
                    sc = op("scalar", lambda E, t=ts, mi_=mi, s_=s: E.activation(
                        out=bAv(mi_, s_ * SUB, SUB), in_=t, func=ACT.Silu),
                        waits=[dv], inc=cSCA)
                    qtok[2] = sc
                return (cDVE, cDVE.n)
            emit_proj("wq_", width, ev_q, x16v)
            pump_prescan(4)

            def ev_g(q, s, b0, gt):
                def fn(E):
                    a = None
                    for m in range(4):
                        a = E.activation(out=bBv(q * 4 + m, s * SUB, SUB),
                                         in_=psb[b0 + m][:], func=ACT.Silu)
                    return a
                return op("scalar", fn, waits=[gt], inc=cSCA)
            emit_proj("wgt", width, ev_g, x16v)
            silg_done = (cSCA, cSCA.n)
            pump_prescan(4)

            barrier("vector", [silg_done])

            def ev_u(q, s, b0, gt):
                last = None
                for m in range(4):
                    mi = q * 4 + m
                    last = op("vector", lambda E, b=b0 + m, mi_=mi, s_=s:
                              E.tensor_tensor(out=bCv(mi_, s_ * SUB, SUB),
                                              in0=psb[b][:],
                                              in1=bBv(mi_, s_ * SUB, SUB),
                                              op=ALU.mult),
                              waits=[gt] if m == 0 else [], inc=cDVE)
                return last
            emit_proj("wup", width, ev_u, x16v)
            pump_prescan(2)

            if debug_stop == "A":
                pk = (cDVE, cDVE.n)
                sk = (cSCA, cSCA.n)
                for k in range(NKH):
                    dma("gpsimd", rodbg[k * 128:(k + 1) * 128,
                                        own0:own0 + width],
                        bAv(k, 0, width), sem("stRO"), waits=[sk, pk])
                    dma("gpsimd", pdbg[k * 128:(k + 1) * 128,
                                       own0:own0 + width],
                        bCv(k, 0, width), sem("stRO"), waits=[])

            def xrv(m, s):
                base = A0 if m < 8 else A2
                mi = m % 8
                return base[:, mi * 1024 + s * SUB:mi * 1024 + s * SUB + SUB]

            def wsl_ro(k):
                return A4_16[:, k * 512:(k + 1) * 512]

            def wsl_dn(k):
                return A4_16[:, (16 + k) * 512:(16 + k + 1) * 512]

            free_tok = {0: None, 4: None}
            gi = 0
            xr_ev = {}
            bufB_guard_pending = True
            op_first = "wro" not in proj_pass
            proj_pass["wro"] = True
            oc_tok = [None]

            for q in range(4):
                extra_t = [t_ for t_ in oc_tok if t_] + \
                    [t_ for t_ in (prev_ck[0], prev_ck[1]) if t_]
                prev_ck[0] = prev_ck[1] = None
                if q == 0:
                    guard1 = list(wq_guard_g[0] or []) + extra_t
                    guard2 = list(wq_guard_g[1] or []) + extra_t
                else:
                    guard1 = guard2 = [(cPE, cPE.n)] + extra_t
                guard = guard1
                if op_first:
                    cd1, ck1 = stream_quarter(
                        wap["wro"][q], wsl_ro, extra_guard=guard,
                        cache_dst=wf16s[4, q],
                        dest_full=A4_16[:, 0:16 * 512])
                    cd2, ck2 = stream_quarter(
                        wap["wdn"][q], wsl_dn, extra_guard=None,
                        cache_dst=wf16s[7, q],
                        dest_full=A4_16[:, 16 * 512:32 * 512])
                    oc_tok[:] = [ck1, ck2]
                    prev_ck[0] = ck1
                    prev_ck[1] = ck2
                    cd1a = cd1b = cd1
                    cd2a = cd2b = cd2
                else:
                    cd1a, cd1b = load_quarter_f16_split(
                        wf16s[4, q], A4_16[:, 0:16 * 512],
                        extra_guard=guard1, qi=0)
                    cd2a, cd2b = load_quarter_f16_split(
                        wf16s[7, q], A4_16[:, 16 * 512:32 * 512],
                        extra_guard=guard2, qi=1)
                    oc_tok[:] = [None]
                pump_prescan(6)
                for s in range(ns):
                    bank0 = (gi % 2) * 4
                    gi += 1
                    wts = [cd1a, cd2a] if s == 0 else []
                    if free_tok[bank0] is not None:
                        wts.append(free_tok[bank0])

                    def mmg_lo(E, s_=s, b0=bank0):
                        mm = None
                        for k in range(NKH // 2):
                            for m in range(4):
                                mm = E.matmul(psb[b0 + m][:],
                                              wsl_ro(k)[:, m * 128:(m + 1) * 128],
                                              bAv(k, s_ * SUB, SUB),
                                              start=(k == 0), stop=False)
                        for k in range(NKH // 2):
                            for m in range(4):
                                mm = E.matmul(psb[b0 + m][:],
                                              wsl_dn(k)[:, m * 128:(m + 1) * 128],
                                              bCv(k, s_ * SUB, SUB),
                                              start=False, stop=False)
                        return mm

                    def mmg_hi(E, s_=s, b0=bank0):
                        mm = None
                        for k in range(NKH // 2, NKH):
                            for m in range(4):
                                mm = E.matmul(psb[b0 + m][:],
                                              wsl_ro(k)[:, m * 128:(m + 1) * 128],
                                              bAv(k, s_ * SUB, SUB),
                                              start=False, stop=False)
                        for k in range(NKH // 2, NKH):
                            for m in range(4):
                                mm = E.matmul(psb[b0 + m][:],
                                              wsl_dn(k)[:, m * 128:(m + 1) * 128],
                                              bCv(k, s_ * SUB, SUB),
                                              start=False, stop=(k == NKH - 1))
                        return mm
                    op("tensor", mmg_lo, waits=wts, inc=cPE)
                    gt = op("tensor", mmg_hi,
                            waits=([cd1b, cd2b] if s == 0 else []), inc=cPE)
                    last = None
                    for m in range(4):
                        mi = q * 4 + m
                        slot, tok = zring.acquire()
                        wl = [tok] if tok else []
                        t = dma("sync", xres[slot],
                                xin[mi * 128:(mi + 1) * 128,
                                    tile_t0 + s * SUB:tile_t0 + (s + 1) * SUB],
                                rZ[slot], waits=wl)
                        wts2 = [t, gt] if m == 0 else [t]
                        if mi >= 8 and bufB_guard_pending:
                            bufB_guard_pending = False
                        last = op("vector", lambda E, b=bank0 + m, mi_=mi, s_=s,
                                  xs_=xres[slot]: E.tensor_tensor(
                                      out=xrv(mi_, s_), in0=psb[b][:], in1=xs_,
                                      op=ALU.add), waits=wts2, inc=cDVE)
                        zring.release(slot, last)
                        xr_ev[(mi, s)] = last
                    free_tok[bank0] = last
            wq_guard_g[0] = [gt]
            wq_guard_g[1] = [gt]

            xr_evac_done = (cDVE, cDVE.n)
            st_guard = tile_guard["st16"]
            tile_guard["st16"] = None
            if debug_stop in ("A", "X"):
                xr_all = (cDVE, cDVE.n)
                for m in range(NKH):
                    for s in range(ns):
                        dma("gpsimd",
                            xrdbg[m * 128:(m + 1) * 128,
                                  own0 + s * SUB:own0 + (s + 1) * SUB],
                            xrv(m, s), sem("stXR"), waits=[xr_all])
            for s in range(ns):
                pe0 = cPE.n
                for m in range(NKH):
                    t16 = st16a[m % 2]
                    q16 = st16b[m % 2]
                    wts = [(cPE, pe0 + m - 1)] if m >= 2 else []
                    if m < 2 and st_guard is not None:
                        wts = wts + st_guard
                    ev = xr_ev.get((m, s))
                    if ev is not None:
                        wts = wts + [ev]
                    c1 = cast_slice("gpsimd", t16[:, 0:SUB], xrv(m, s),
                                    waits=wts, inc=cGP)
                    c2 = op("scalar", lambda E, qq=q16, src=xrv(m, s):
                            E.activation(out=qq[:, 0:SUB], in_=src,
                                         func=ACT.Square), waits=wts, inc=cSCA)
                    mw = [c1, c2]
                    if m == 0:
                        mw.append(xr_evac_done)
                        if bank_tok.get("stats") is not None:
                            mw.append(bank_tok["stats"])

                    def smm(E, m_=m, t=t16, qq=q16):
                        E.matmul(psb[0][0:1, :], ones16[:, 0:1], t[:, 0:SUB],
                                 start=(m_ == 0), stop=(m_ == NKH - 1))
                        return E.matmul(psb[1][0:1, :], ones16[:, 0:1],
                                        qq[:, 0:SUB],
                                        start=(m_ == 0), stop=(m_ == NKH - 1))
                    op("tensor", smm, waits=mw, inc=cPE)
                st_done = (cPE, cPE.n)
                sq, bb, bank_rel = emit_ln_rows(0, 1, st_done)
                bank_tok["stats"] = bank_rel
                psA, gA = hilo_bcast(rowC[0:1, :], SUB, 2, waits=[sq],
                                     bank_guard=bank_tok["psA"])
                psB, gB = hilo_bcast(rowA[0:1, :], SUB, 3, waits=[bb],
                                     bank_guard=bank_tok["psB"])
                ap_done = None
                mx_done = None
                for m in range(NKH):
                    wts = [gA, gB] if m == 0 else []
                    if debug_stop == "A" and m == 0:
                        wts = wts + [(sems["stXR"], sems["stXR"].n)]
                    if debug_stop == "X" and m == 0:
                        wts = wts + [(sems["stXR"], sems["stXR"].n)]
                    op("vector", lambda E, m_=m, s_=s: E.tensor_tensor(
                        out=xrv(m_, s_), in0=xrv(m_, s_), in1=psA, op=ALU.mult),
                        waits=wts)
                    ap_done = op("vector", lambda E, m_=m, s_=s: E.tensor_tensor(
                        out=xrv(m_, s_), in0=xrv(m_, s_), in1=psB,
                        op=ALU.subtract), inc=cDVE)
                    dma("gpsimd", x1s[m * 128:(m + 1) * 128,
                                      own0 + s * SUB:own0 + (s + 1) * SUB],
                        xrv(m, s), stX1, waits=[ap_done])
                    mwts = [ap_done]
                    if m == 0:
                        mwts.append((cPE, cPE.n))
                        ab = op("scalar", lambda E, s_=s: E.activation(
                            out=Mx, in_=xrv(0, s_), func=ACT.Abs),
                            waits=mwts, inc=cSCA)
                        mx_done = ab
                    else:
                        ab = op("scalar", lambda E, m_=m, s_=s: E.activation(
                            out=clb[1], in_=xrv(m_, s_), func=ACT.Abs),
                            waits=mwts + [mx_done], inc=cSCA)
                        mx_done = op("vector", lambda E: E.tensor_max(
                            out=Mx, in0=Mx, in1=clb[1]), waits=[ab], inc=cDVE)
                bank_tok["psA"] = ap_done
                bank_tok["psB"] = ap_done
                am = emit_amax_rows(Mx, mx_done)
                srow = rowB[0:1, :]
                rrow = rowC[0:1, :]
                ts_ = op("vector", lambda E: E.tensor_scalar(
                    out=srow, in0=srow, scalar1=1.0 / 127.0, scalar2=EPS,
                    op0=ALU.mult, op1=ALU.add), waits=[am], inc=cDVE)
                rr = op("vector", lambda E: E.reciprocal(out=rrow, in_=srow),
                        waits=[ts_], inc=cDVE)
                psS, gS = hilo_bcast(srow, SUB, 6, waits=[rr],
                                     bank_guard=bank_tok["psS"])
                evS = op("vector", lambda E, s_=s: E.tensor_copy(
                    out=A6[:, own0 + s_ * SUB:own0 + (s_ + 1) * SUB], in_=psS),
                    waits=[gS], inc=cDVE)
                bank_tok["psS"] = evS
                psR, gR = hilo_bcast(rrow, SUB, 7, waits=[rr],
                                     bank_guard=bank_tok["psR"])
                evR = op("vector", lambda E: E.tensor_copy(out=R2sbA, in_=psR),
                         waits=[gR, (cGP, cGP.n)], inc=cDVE)
                bank_tok["psR"] = evR
                for m in range(NKH):
                    slot, tok = xring.acquire()
                    ts = tmp[slot]
                    wl = [evR] if m == 0 else []
                    if tok is not None:
                        wl.append(tok)
                    mq = op("gpsimd", lambda E, m_=m, s_=s, t=ts: E.tensor_tensor(
                        out=t, in0=xrv(m_, s_), in1=R2sbA, op=ALU.mult),
                        waits=wl, inc=cGP)
                    cv = op("vector", lambda E, t=ts: E.tensor_copy(
                        out=t.bitcast(I32), in_=t), waits=[mq], inc=cDVE)
                    xq16 = (st16a + st16b)[m % 4]
                    wl2 = [cv]
                    if m >= 4 or s > 0 or tile_t0 > WARM:
                        wl2.append((stXQ4[m % 4], stXQ4[m % 4].n))
                    x2c = op("vector", lambda E, t=ts, d=xq16: E.tensor_copy(
                        out=d[:, 0:SUB], in_=t.bitcast(I32)), waits=wl2,
                        inc=cDVE)
                    dma("gpsimd",
                        xqs[m * 128:(m + 1) * 128,
                            own0 + s * SUB:own0 + (s + 1) * SUB],
                        xq16[:, 0:SUB], stXQ4[m % 4], waits=[x2c])
                    xring.release(slot, x2c)
                st_guard = [(s_, s_.n) for s_ in stXQ4]
            tile_guard["x16"] = [(cDVE, cDVE.n), (stX1, stX1.n),
                                 (cGP, cGP.n)] + [(s_, s_.n) for s_ in stXQ4]
            tile_guard["bufA"] = None
            tile_guard["bufB"] = [(cDVE, cDVE.n), (stX1, stX1.n),
                                  (cGP, cGP.n)] + [(s_, s_.n) for s_ in stXQ4]
            tile_guard["bufC"] = [(cPE, cPE.n)]
            tile_guard["st16"] = [(s_, s_.n) for s_ in stXQ4]

        emit_attn_tile(0, WARM, warm_only=True)
        t0 = WARM
        while t0 < WARM + T_OWN:
            w = min(1024, WARM + T_OWN - t0)
            emit_attn_tile(t0, w, warm_only=False)
            t0 += w

        if debug_stop == "A":
            barrier("gpsimd", [(stX1, stX1.n)] + [(s_, s_.n) for s_ in stXQ4])
            barrier("sync", [(stX1, stX1.n)] + [(s_, s_.n) for s_ in stXQ4])
            with nc.Block() as block:
                @block.sync
                def _(E):
                    for fn in Q["sync"]:
                        fn(E)

                @block.tensor
                def _(E):
                    for fn in Q["tensor"]:
                        fn(E)

                @block.vector
                def _(E):
                    for fn in Q["vector"]:
                        fn(E)

                @block.scalar
                def _(E):
                    for fn in Q["scalar"]:
                        fn(E)

                @block.gpsimd
                def _(E):
                    for fn in Q["gpsimd"]:
                        fn(E)
            return nc

        def xqv(k, c0, n):
            base = A0_16 if k < 8 else A1_16
            kk = k % 8
            return base[:, kk * T_OWN + c0:kk * T_OWN + c0 + n]

        xq_guard = tile_guard["x16"] + [(cSCA, cSCA.n), (cPE, cPE.n)]
        xq_sp = [(s_, s_.n) for s_ in stXQ4]
        for k in range(NKH):
            dma("scalar", xqv(k, 0, T_OWN), xqs[k * 128:(k + 1) * 128, :],
                sXQ, waits=(xq_guard if k == 0 else []) + xq_sp)
        xq_ready = (sXQ, sXQ.n)

        pump_prescan(len(prescan_units))

        def finalize_ws(ai, ci):
            acc = wsc[:, ai:ai + 1]
            row = rowA[0:1, 0:128]
            tv = transpose_col_to_row(acc, 4, row, waits=[acc_done_tok[ai]])
            sm_ = op("vector", lambda E: E.reduce_sum(
                out=rowB[0:1, 0:1], in_=row, axis=AX.X), waits=[tv], inc=cDVE)
            sc_ = op("vector", lambda E: E.tensor_scalar(
                out=rowB[0:1, 0:1], in0=rowB[0:1, 0:1], scalar1=1.0 / NW_ELEM,
                scalar2=EPS, op0=ALU.mult, op1=ALU.add), waits=[sm_], inc=cDVE)
            ps, g = hilo_bcast(rowB[0:1, 0:1], 1, 5, waits=[sc_])
            cp_ = op("vector", lambda E: E.tensor_copy(
                out=wsc[:, ci:ci + 1], in_=ps[:, 0:1]), waits=[g], inc=cDVE)
            hf_ = op("vector", lambda E: E.tensor_scalar(
                out=wsc[:, ci + 1:ci + 2], in0=wsc[:, ci:ci + 1], scalar1=0.5,
                scalar2=None, op0=ALU.mult), waits=[cp_], inc=cDVE)
            return op("vector", lambda E: E.tensor_scalar(
                out=wsc[:, ci + 2:ci + 3], in0=wsc[:, ci:ci + 1], scalar1=-0.5,
                scalar2=None, op0=ALU.mult), waits=[hf_], inc=cDVE)

        wsg_done = finalize_ws(0, 8)
        wsu_done = finalize_ws(1, 11)
        wsd_done = finalize_ws(2, 14)
        S32 = A6
        b1_guard = [(cPE, cPE.n), (cDVE, cDVE.n), (cSCA, cSCA.n),
                    (cGP, cGP.n), (stX1, stX1.n), (stWF, stWF.n)] + \
            [(s_, s_.n) for s_ in stXQ4]

        def Mh_sub(s):
            return xres[s] if s < 3 else Mx

        clb_tok = [(cGP, cGP.n), None]
        hq_guard = [list(b1_guard), list(b1_guard)]
        wq_dest_guard = [None]
        wqg_guard = None
        bring.tok[0] = wring.tok[0]
        bring.tok[1] = wring.tok[1]
        for i_ in range(2, 10):
            bring.tok[i_] = (cPE, cPE.n)

        def bslot(i):
            if i < 2:
                return A5[:, i * 1024:(i + 1) * 1024]
            if i < 6:
                return A3[:, 4096 + (i - 2) * 1024:4096 + (i - 1) * 1024]
            return A4[:, 4096 + (i - 6) * 1024:4096 + (i - 5) * 1024]

        def emit_quant_unit(dram_slice, dest_ap, ci, extra, sgn_extra=None,
                            cache=None):
            sA, tokA = bring.acquire()
            sB, tokB = bring.acquire()
            wl = [tokA] if tokA else []
            dst = bslot(sA)
            if len(dram_slice.shape) == 3:
                dst = dst.rearrange("p (k c) -> p k c", k=dram_slice.shape[1])
            t = dma("sync" if sA % 2 == 0 else "scalar", dst, dram_slice,
                    rB[sA], waits=wl)
            wts = [t] + (extra or [])
            if tokB:
                wts.append(tokB)
            cl = op("vector", lambda E, a=sA, b=sB, ci_=ci: E.tensor_scalar(
                out=bslot(b), in0=bslot(a),
                scalar1=wsc[:, ci_ + 1:ci_ + 2],
                scalar2=wsc[:, ci_ + 2:ci_ + 3],
                op0=ALU.min, op1=ALU.max), waits=wts, inc=cDVE)
            sb = op("gpsimd", lambda E, a=sA, b=sB: E.tensor_tensor(
                out=bslot(a), in0=bslot(a), in1=bslot(b), op=ALU.subtract),
                waits=[cl], inc=cGP)
            bring.release(sB, sb)
            swl = [sb] + (sgn_extra or [])
            sg_ = op("scalar", lambda E, a=sA, d=dest_ap: E.activation(
                out=d, in_=bslot(a), func=ACT.Sign), waits=swl, inc=cSCA)
            if cache is not None:
                ct = dma("gpsimd", cache, dest_ap, stWDQ, waits=[sg_])
                bring.release(sA, sg_)
                return sg_, ct
            bring.release(sA, sg_)
            return sg_, None

        wdq_units = [(k, hh) for k in range(NKI) for hh in range(2)]
        wdq_state = {"sgtok": None, "first": True}

        def pump_wdquant(n):
            for _ in range(n):
                if not wdq_units:
                    return
                k, hh = wdq_units.pop(0)
                extra = []
                if wdq_state["first"]:
                    wdq_state["first"] = False
                    extra.append((wsd_done[0], wsd_done[1]))
                sgn_extra = []
                if wdq_state["sgtok"] is not None:
                    sgn_extra.append(wdq_state["sgtok"])
                sg_, ct = emit_quant_unit(
                    fd[k * 128:(k + 1) * 128, hh * 1024:(hh + 1) * 1024],
                    stwideB16, 14, extra, sgn_extra=sgn_extra,
                    cache=wdqs[k * 128:(k + 1) * 128,
                               hh * 1024:(hh + 1) * 1024])
                wdq_state["sgtok"] = ct
        mh_started = [False] * NS_T
        gate_free = None
        up_free = None
        h_tmp_tok = {}
        habs_tok = {}
        pend_maxes = []
        for qq in range(16):
            buf = qq % 2
            hqt = A2_16[:, buf * (4 * T_OWN):(buf + 1) * (4 * T_OWN)]

            def hv(jm, c0, n, b=hqt):
                return b[:, jm * T_OWN + c0:jm * T_OWN + c0 + n]

            def quant_quarter(dram_q, dest16, ci, first_extra, guard):
                done = None
                for kg in range(8):
                    extra = list(first_extra) if (kg == 0 and first_extra) \
                        else []
                    sgn_extra = [guard] if (kg == 0 and guard) else []
                    done, _ = emit_quant_unit(
                        dram_q[kg * 2:(kg + 1) * 2, :, :].rearrange(
                            "k p c -> p k c"),
                        dest16(kg * 2, wide=True), ci, extra,
                        sgn_extra=sgn_extra)
                return done

            def wg16(k, wide=False):
                n = 1024 if wide else 512
                return A3_16[:, k * 512:k * 512 + n]

            def wu16(k, wide=False):
                n = 1024 if wide else 512
                return A4_16[:, k * 512:k * 512 + n]

            fe = []
            if qq == 0:
                fe = [wsg_done] + b1_guard
            cdg = quant_quarter(fg[qq], wg16, 8, fe, wq_dest_guard[0])
            cdu = quant_quarter(fu[qq], wu16, 11,
                                ([wsu_done] + b1_guard) if qq == 0 else None,
                                wq_dest_guard[0])

            for s in range(NS_T):
                wts = [cdg, cdu] if s == 0 else []
                if qq == 0 and s == 0:
                    wts.append(xq_ready)
                if gate_free is not None:
                    wts.append(gate_free)

                def mmg_gate(E, s_=s, w_=wg16):
                    mm = None
                    for k in range(NKH):
                        for m in range(4):
                            mm = E.matmul(psb[m][:],
                                          w_(k)[:, m * 128:(m + 1) * 128],
                                          xqv(k, s_ * SUB, SUB),
                                          start=(k == 0), stop=(k == NKH - 1))
                    return mm
                gg = op("tensor", mmg_gate, waits=wts, inc=cPE)
                wts = [up_free] if up_free is not None else []

                def mmg_up(E, s_=s, w_=wu16):
                    mm = None
                    for k in range(NKH):
                        for m in range(4):
                            mm = E.matmul(psb[4 + m][:],
                                          w_(k)[:, m * 128:(m + 1) * 128],
                                          xqv(k, s_ * SUB, SUB),
                                          start=(k == 0), stop=(k == NKH - 1))
                    return mm
                gu = op("tensor", mmg_up, waits=wts, inc=cPE)

                hg = hq_guard[buf] if s == 0 else None
                for jm in range(4):
                    tg = tmp[jm % 2]
                    tu = tmp[2]
                    wts = [gg] if jm == 0 else []
                    pt = h_tmp_tok.get(jm % 2)
                    if pt is not None:
                        wts.append(pt)
                    d1 = op("vector", lambda E, b=jm, t=tg, s_=s:
                            E.scalar_tensor_tensor(
                                out=t, in0=psb[b][:], scalar=wsc[:, 8:9],
                                in1=S32[:, s_ * SUB:(s_ + 1) * SUB],
                                op0=ALU.mult, op1=ALU.mult),
                            waits=wts, inc=cDVE)
                    sl = op("scalar", lambda E, t=tg: E.activation(
                        out=t, in_=t, func=ACT.Silu), waits=[d1], inc=cSCA)
                    wts = [gu] if jm == 0 else []
                    pt = h_tmp_tok.get(2)
                    if pt is not None:
                        wts.append(pt)
                    d2 = op("vector", lambda E, b=4 + jm, t=tu, s_=s:
                            E.scalar_tensor_tensor(
                                out=t, in0=psb[b][:], scalar=wsc[:, 11:12],
                                in1=S32[:, s_ * SUB:(s_ + 1) * SUB],
                                op0=ALU.mult, op1=ALU.mult),
                            waits=wts, inc=cDVE)
                    wts = [sl, d2]
                    if hg is not None and jm == 0:
                        wts = wts + hg
                        hg = None
                    hw = op("vector", lambda E, a=tg, b=tu, jm_=jm, s_=s,
                            hv_=hv: E.tensor_tensor(
                                out=hv_(jm_, s_ * SUB, SUB), in0=a,
                                in1=b, op=ALU.mult),
                            waits=wts, inc=cDVE)
                    h_tmp_tok[jm % 2] = hw
                    h_tmp_tok[2] = hw
                    if not mh_started[s]:
                        mh_started[s] = True
                        op("scalar", lambda E, jm_=jm, s_=s, hv_=hv:
                            E.activation(out=Mh_sub(s_),
                                         in_=hv_(jm_, s_ * SUB, SUB),
                                         func=ACT.Abs),
                            waits=[hw], inc=cSCA)
                    else:
                        habs = (st16a + st16b)[jm]
                        hbt = habs_tok.get(jm)
                        ab = op("scalar", lambda E, jm_=jm, s_=s, hv_=hv,
                                hb=habs: E.activation(
                                    out=hb[:, 0:SUB],
                                    in_=hv_(jm_, s_ * SUB, SUB),
                                    func=ACT.Abs),
                            waits=[hw] + ([hbt] if hbt else []), inc=cSCA)
                        pend_maxes.append((jm, s, ab, habs))
                gate_free = d1
                up_free = d2
                for jm_, s_, ab_, hb_ in pend_maxes:
                    habs_tok[jm_] = op(
                        "vector", lambda E, s2=s_, hb=hb_: E.tensor_max(
                            out=Mh_sub(s2), in0=Mh_sub(s2),
                            in1=hb[:, 0:SUB]),
                        waits=[ab_], inc=cDVE)
                pend_maxes = []
            pump_wdquant(8)
            wqg_guard = (cPE, cPE.n)
            wq_dest_guard[0] = (cPE, cPE.n)
            hq_guard[buf] = [dma(
                "gpsimd",
                hs[qq * 512:(qq + 1) * 512, :].rearrange(
                    "(j p) t -> p j t", p=128),
                hqt.rearrange("p (j t) -> p j t", j=4),
                stH, waits=[(cGP, cGP.n), (cDVE, cDVE.n)])]

        pump_wdquant(len(wdq_units))

        barrier("tensor", [(cDVE, cDVE.n), (cSCA, cSCA.n)])
        R2s = A6
        last_s32_use = (cDVE, cDVE.n)
        for s in range(NS_T):
            am = emit_amax_rows(Mh_sub(s), None)
            srow = rowB[0:1, :]
            rrow = rowC[0:1, :]
            ts_ = op("vector", lambda E: E.tensor_scalar(
                out=srow, in0=srow, scalar1=1.0 / 127.0, scalar2=EPS,
                op0=ALU.mult, op1=ALU.add), waits=[am], inc=cDVE)
            rr = op("vector", lambda E: E.reciprocal(out=rrow, in_=srow),
                    waits=[ts_], inc=cDVE)
            psR, gR = hilo_bcast(rrow, SUB, 7, waits=[rr],
                                 bank_guard=bank_tok["psR"])
            evR = op("vector", lambda E, s_=s: E.tensor_copy(
                out=R2s[:, s_ * SUB:(s_ + 1) * SUB], in_=psR),
                waits=[gR] + ([last_s32_use] if s == 0 else []), inc=cDVE)
            bank_tok["psR"] = evR
        r2_done = (cDVE, cDVE.n)

        if debug_stop == "B1":
            barrier("gpsimd", [(stH, stH.n)])
            barrier("vector", [r2_done, (wsd_done[0], wsd_done[1])])
            dbg = nc.dram_tensor("dbg", [128, T_OWN + 32], F32,
                                 kind="ExternalOutput").ap()
            dmy = dma("gpsimd", dbg[:, 0:T_OWN], R2s[:, 0:T_OWN], sem("stDBG"),
                      waits=[r2_done])
            dmy2 = dma("gpsimd", dbg[:, T_OWN:T_OWN + 17], wsc[:, 0:17],
                       sems["stDBG"], waits=[(wsd_done[0], wsd_done[1])])
            barrier("gpsimd", [(sems["stDBG"], sems["stDBG"].n)])
            barrier("sync", [(stH, stH.n)])
            with nc.Block() as block:
                @block.sync
                def _(E):
                    for fn in Q["sync"]:
                        fn(E)

                @block.tensor
                def _(E):
                    for fn in Q["tensor"]:
                        fn(E)

                @block.vector
                def _(E):
                    for fn in Q["vector"]:
                        fn(E)

                @block.scalar
                def _(E):
                    for fn in Q["scalar"]:
                        fn(E)

                @block.gpsimd
                def _(E):
                    for fn in Q["gpsimd"]:
                        fn(E)
            return nc

        def yv(m, c0, n):
            mm_ = m % 8
            base = A0 if mm_ < 4 else A1
            return base[:, (mm_ % 4) * T_OWN + c0:(mm_ % 4) * T_OWN + c0 + n]

        def wdq16(j):
            return A2_16[:, j * 2048:(j + 1) * 2048]

        def hqslot(i):
            base = A3_16 if i < 8 else A4_16
            ii = i % 8
            return base[:, ii * T_OWN:(ii + 1) * T_OWN]

        def wdhalf(i):
            return A5[:, i * 1024:(i + 1) * 1024]

        b2_guard = [wqg_guard, r2_done, (cSCA, cSCA.n), (cGP, cGP.n),
                    (stH, stH.n), (wsd_done[0], wsd_done[1])]
        hq_free = {}
        wdq_free = {}
        for half in range(2):
            mlist = list(range(half * 8, half * 8 + 8))
            bank_free = {0: None, 1: None, 2: None, 3: None}
            for kg in range(8):
                hq_toks = []
                wd_toks = []
                for j in range(8):
                    k = kg * 8 + j
                    ring_i = k % 16
                    hguard = hq_free.get(ring_i)
                    wguard = wdq_free.get(j)
                    wl2 = [wguard] if wguard else []
                    if kg == 0 and j == 0:
                        wl2.append((stWDQ, stWDQ.n))
                    t2 = dma("sync" if j % 2 == 0 else "scalar", wdq16(j),
                             wdqs[k * 128:(k + 1) * 128, :],
                             rE[j], waits=wl2)
                    wd_toks.append(t2)
                    if half == 0:
                        wl = [hguard] if hguard else []
                        if kg == 0 and j == 0:
                            wl = wl + b2_guard
                        t = dma("sync" if j % 2 == 1 else "scalar",
                                hqslot(ring_i),
                                hs[k * 128:(k + 1) * 128, :], rH[j],
                                waits=wl + [(stH, stH.n)])
                        last = None
                        for hh in range(max(1, T_OWN // 1024)):
                            c0 = hh * 1024
                            cn = min(1024, T_OWN - c0)
                            wslot = [tmpwide, stwide][(j * 2 + hh) % 2]
                            wlq = [t]
                            pt = h_tmp_tok.get((j * 2 + hh) % 2)
                            if pt is not None:
                                wlq.append(pt)
                            mq = op("vector", lambda E, ri=ring_i, c0_=c0,
                                    cn_=cn, t_=wslot: E.tensor_tensor(
                                        out=t_[:, 0:cn_],
                                        in0=hqslot(ri)[:, c0_:c0_ + cn_],
                                        in1=R2s[:, c0_:c0_ + cn_],
                                        op=ALU.mult), waits=wlq, inc=cDVE)
                            cv = op("vector", lambda E, t_=wslot, cn_=cn:
                                    E.tensor_copy(
                                        out=t_.bitcast(I32)[:, 0:cn_],
                                        in_=t_[:, 0:cn_]),
                                    waits=[mq], inc=cDVE)
                            last = op("gpsimd", lambda E, t_=wslot, ri=ring_i,
                                      c0_=c0, cn_=cn: E.tensor_copy(
                                          out=hqslot(ri)[:, c0_:c0_ + cn_],
                                          in_=t_.bitcast(I32)[:, 0:cn_]),
                                      waits=[cv], inc=cGP)
                            h_tmp_tok[(j * 2 + hh) % 2] = last
                        dma("gpsimd", hqs[k * 128:(k + 1) * 128, :],
                            hqslot(ring_i), stHQ, waits=[last])
                        hq_toks.append(last)
                    else:
                        wl = [hguard] if hguard else []
                        if kg == 0 and j == 0:
                            wl.append((stHQ, stHQ.n))
                            wl.append((cGP, cGP.n))
                            wl.append((stY1, stY1.n))
                        t = dma("sync" if j % 2 == 1 else "scalar",
                                hqslot(ring_i),
                                hqs[k * 128:(k + 1) * 128, :], rH[j],
                                waits=wl)
                        hq_toks.append(t)

                gidx = 0
                first_wts = hq_toks + wd_toks
                for s in range(NS_T):
                    for m in mlist:
                        bank = gidx % 4
                        gidx += 1
                        wts = []
                        if first_wts is not None:
                            wts = list(first_wts)
                            first_wts = None
                        if bank_free[bank] is not None:
                            wts.append(bank_free[bank])

                        def mmg(E, s_=s, m_=m, b_=bank, kg_=kg):
                            mm = None
                            for j in range(8):
                                k = kg_ * 8 + j
                                mm = E.matmul(
                                    psb[b_][:],
                                    wdq16(j)[:, m_ * 128:(m_ + 1) * 128],
                                    hqslot(k % 16)[:, s_ * SUB:(s_ + 1) * SUB],
                                    start=(j == 0), stop=(j == 7))
                            return mm
                        gt = op("tensor", mmg, waits=wts, inc=cPE)
                        if kg == 0:
                            ev = op("vector", lambda E, m_=m, s_=s, b_=bank:
                                    E.tensor_copy(out=yv(m_, s_ * SUB, SUB),
                                                  in_=psb[b_][:]),
                                    waits=[gt], inc=cDVE)
                        else:
                            ev = op("vector", lambda E, m_=m, s_=s, b_=bank:
                                    E.tensor_tensor(out=yv(m_, s_ * SUB, SUB),
                                                    in0=yv(m_, s_ * SUB, SUB),
                                                    in1=psb[b_][:], op=ALU.add),
                                    waits=[gt], inc=cDVE)
                        bank_free[bank] = ev
                for j in range(8):
                    k = kg * 8 + j
                    hq_free[k % 16] = (cPE, cPE.n)
                for j in range(8):
                    wdq_free[j] = (cPE, cPE.n)
            if half == 0:
                ydone = (cDVE, cDVE.n)
                for m in range(8):
                    dma("scalar", y1s[m * 128:(m + 1) * 128, :],
                        yv(m, 0, T_OWN), stY1, waits=[ydone])

        pe_all = (cPE, cPE.n)
        y_all = (cDVE, cDVE.n)

        op("vector", lambda E: E.reciprocal(out=R2s[:, 0:T_OWN],
                                            in_=R2s[:, 0:T_OWN]),
           waits=[(cGP, cGP.n), (wsd_done[0], wsd_done[1])])
        s2w_done = op("vector", lambda E: E.tensor_scalar(
            out=R2s[:, 0:T_OWN], in0=R2s[:, 0:T_OWN],
            scalar1=wsc[:, 14:15], scalar2=None, op0=ALU.mult), inc=cDVE)
        S2W = R2s

        def xr2v(m, s):
            if m >= 8:
                return yv(m, s * SUB, SUB)
            base = A3 if m < 4 else A4
            return base[:, (m % 4) * T_OWN + s * SUB:
                        (m % 4) * T_OWN + (s + 1) * SUB]

        for m in range(NKH):
            slot, tok = fring.acquire()
            stage = A2[:, (slot % 4) * 2048:((slot % 4) + 1) * 2048]
            wl = [tok] if tok else []
            if m == 0:
                wl += [pe_all, y_all, (cGP, cGP.n), (cSCA, cSCA.n)]
            t = dma("sync", stage[:, 0:T_OWN],
                    x1s[m * 128:(m + 1) * 128, :], rF[slot],
                    waits=wl + [(stX1, stX1.n)])
            if m < 8:
                slot2, tok2 = fring.acquire()
                stage2 = A2[:, (slot2 % 4) * 2048:((slot2 % 4) + 1) * 2048]
                wl2 = [tok2] if tok2 else []
                t2 = dma("sync", stage2[:, 0:T_OWN],
                         y1s[m * 128:(m + 1) * 128, :], rF[slot2],
                         waits=wl2 + [(stY1, stY1.n)])
                last = None
                for s in range(NS_T):
                    wl3 = [t, t2, s2w_done] if s == 0 else []
                    op("vector", lambda E, m_=m, s_=s, b=stage2:
                       E.tensor_tensor(
                           out=xr2v(m_, s_),
                           in0=b[:, s_ * SUB:(s_ + 1) * SUB],
                           in1=S2W[:, s_ * SUB:(s_ + 1) * SUB],
                           op=ALU.mult), waits=wl3)
                    last = op("vector", lambda E, m_=m, s_=s, a=stage:
                              E.tensor_tensor(
                                  out=xr2v(m_, s_), in0=xr2v(m_, s_),
                                  in1=a[:, s_ * SUB:(s_ + 1) * SUB],
                                  op=ALU.add), inc=cDVE)
                fring.release(slot, last)
                fring.release(slot2, last)
            else:
                last = None
                for s in range(NS_T):
                    wl3 = [t, s2w_done] if s == 0 else []
                    op("vector", lambda E, m_=m, s_=s:
                       E.tensor_tensor(out=xr2v(m_, s_), in0=xr2v(m_, s_),
                                       in1=S2W[:, s_ * SUB:(s_ + 1) * SUB],
                                       op=ALU.mult), waits=wl3)
                    last = op("vector", lambda E, m_=m, s_=s, a=stage:
                              E.tensor_tensor(out=xr2v(m_, s_),
                                              in0=xr2v(m_, s_),
                                              in1=a[:, s_ * SUB:(s_ + 1) * SUB],
                                              op=ALU.add),
                              waits=[] if s > 0 else [], inc=cDVE)
                fring.release(slot, last)
        xr2_done = (cDVE, cDVE.n)

        out_tok = [None] * NKH
        for s in range(NS_T):
            pe0 = cPE.n
            for m in range(NKH):
                t16 = st16a[m % 2]
                q16 = st16b[m % 2]
                wts = [(cPE, pe0 + m - 1)] if m >= 2 else [xr2_done]
                c1 = cast_slice("gpsimd", t16[:, 0:SUB], xr2v(m, s), waits=wts,
                                inc=cGP)
                c2 = op("scalar", lambda E, qq=q16, src=xr2v(m, s): E.activation(
                    out=qq[:, 0:SUB], in_=src, func=ACT.Square), waits=wts,
                    inc=cSCA)
                mw = [c1, c2]
                if m == 0 and bank_tok.get("stats") is not None:
                    mw.append(bank_tok["stats"])

                def smm(E, m_=m, t=t16, qq=q16):
                    E.matmul(psb[0][0:1, :], ones16[:, 0:1], t[:, 0:SUB],
                             start=(m_ == 0), stop=(m_ == NKH - 1))
                    return E.matmul(psb[1][0:1, :], ones16[:, 0:1], qq[:, 0:SUB],
                                    start=(m_ == 0), stop=(m_ == NKH - 1))
                op("tensor", smm, waits=mw, inc=cPE)
            st_done = (cPE, cPE.n)
            sq, bb, bank_rel = emit_ln_rows(0, 1, st_done)
            bank_tok["stats"] = bank_rel
            psA, gA = hilo_bcast(rowC[0:1, :], SUB, 2, waits=[sq],
                                 bank_guard=bank_tok["psA"])
            psB, gB = hilo_bcast(rowA[0:1, :], SUB, 3, waits=[bb],
                                 bank_guard=bank_tok["psB"])
            ap_done = None
            for m in range(NKH):
                wts = [gA, gB] if m == 0 else []
                op("vector", lambda E, m_=m, s_=s: E.tensor_tensor(
                    out=xr2v(m_, s_), in0=xr2v(m_, s_), in1=psA, op=ALU.mult),
                    waits=wts)
                ap_done = op("vector", lambda E, m_=m, s_=s: E.tensor_tensor(
                    out=xr2v(m_, s_), in0=xr2v(m_, s_), in1=psB,
                    op=ALU.subtract), inc=cDVE)
                if s == NS_T - 1:
                    out_tok[m] = ap_done
            bank_tok["psA"] = ap_done
            bank_tok["psB"] = ap_done
        for m in range(NKH):
            if m >= 8:
                mm_ = m % 8
                base = A0 if mm_ < 4 else A1
                srcv = base[:, (mm_ % 4) * T_OWN:(mm_ % 4 + 1) * T_OWN]
            else:
                base = A3 if m < 4 else A4
                srcv = base[:, (m % 4) * T_OWN:(m % 4 + 1) * T_OWN]
            dma("scalar", out_d[m * 128:(m + 1) * 128, :], srcv,
                stOUT, waits=[out_tok[m]])
        barrier("gpsimd", [(stOUT, stOUT.n)])
        barrier("scalar", [(stOUT, stOUT.n)])
        barrier("sync", [(stOUT, stOUT.n)])

        with nc.Block() as block:
            @block.sync
            def _(E):
                for fn in Q["sync"]:
                    fn(E)

            @block.tensor
            def _(E):
                for fn in Q["tensor"]:
                    fn(E)

            @block.vector
            def _(E):
                for fn in Q["vector"]:
                    fn(E)

            @block.scalar
            def _(E):
                for fn in Q["scalar"]:
                    fn(E)

            @block.gpsimd
            def _(E):
                for fn in Q["gpsimd"]:
                    fn(E)

    return nc


def _qmajor(w, nq):
    K, M = w.shape
    assert M == nq * 512
    a = w.reshape(K // 128, 128, nq, 512)
    return np.ascontiguousarray(a.transpose(2, 0, 1, 3))


def kernel(**inp):
    B_, S_, H_ = 2, 8192, 2048
    T = 2048
    x = np.asarray(inp["L0_attn_x"], np.float32)

    att = {k: np.ascontiguousarray(np.asarray(inp["L0_attn_" + k], np.float32))
           for k in ["w_f", "w_i", "w_v", "w_q", "w_ro", "w_up", "w_gate",
                     "w_down"]}
    ffn_g = np.ascontiguousarray(np.asarray(inp["L0_ffn_w_gate"], np.float32).T)
    ffn_u = np.ascontiguousarray(np.asarray(inp["L0_ffn_w_up"], np.float32).T)
    ffn_d = np.ascontiguousarray(np.asarray(inp["L0_ffn_w_down"], np.float32).T)

    wm = {
        "wf": _qmajor(att["w_f"], 4), "wi": _qmajor(att["w_i"], 4),
        "wv": _qmajor(att["w_v"], 4), "wq_": _qmajor(att["w_q"], 4),
        "wro": _qmajor(att["w_ro"], 4), "wup": _qmajor(att["w_up"], 4),
        "wgt": _qmajor(att["w_gate"], 4), "wdn": _qmajor(att["w_down"], 4),
        "fg": _qmajor(ffn_g, 16), "fu": _qmajor(ffn_u, 16),
        "fd": ffn_d,
        "ident": np.eye(128, dtype=np.float32),
    }

    in_maps = []
    for c in range(8):
        b, ch = divmod(c, 4)
        lo = ch * T - WARM
        if lo < 0:
            seg = np.concatenate(
                [np.zeros((WARM, H_), np.float32), x[b, 0:(ch + 1) * T]], 0)
        else:
            seg = x[b, lo:(ch + 1) * T]
        xc = np.ascontiguousarray(seg.T)
        m = {"xin": xc}
        m.update(wm)
        in_maps.append(m)

    nc = build_nc(T_OWN=T)
    res = run_bass_kernel_spmd(nc, in_maps, core_ids=list(range(8)))
    out = np.empty((B_, S_, H_), np.float32)
    for c in range(8):
        b, ch = divmod(c, 4)
        out[b, ch * T:(ch + 1) * T, :] = res.results[c]["out"].T
    return out
